# version 12
# baseline (speedup 1.0000x reference)
import numpy as np
from contextlib import ExitStack

import concourse.bass as bass
import concourse.mybir as mybir
from concourse.bass_utils import run_bass_kernel_spmd

F32 = mybir.dt.float32
BF16 = mybir.dt.bfloat16
AF = mybir.ActivationFunctionType
ALU = mybir.AluOpType
AX = mybir.AxisListType

B, S, D = 16, 2048, 1024
NCORE = 8
BPC = B // NCORE
NT = S // 128
G = 8
H, E, DH = 8, 5, 64
NB = S // 256
EPS = 1e-5


def hpos(h):
    return (h % 2) * 1024 + (h // 2) * 256


class Carve:
    def __init__(self, t):
        self.t = t
        self.off = 0

    def take(self, shape, dt=BF16):
        n = 1
        for d_ in shape:
            n *= d_
        w = n * (2 if dt is F32 else 1)
        self.off = (self.off + 15) // 16 * 16
        ap = self.t[:, self.off:self.off + w]
        self.off += w
        if dt is F32:
            ap = ap.bitcast(F32)
        if len(shape) == 1:
            return ap
        spec = "p (" + " ".join(f"a{i}" for i in range(len(shape))) + ") -> p " + \
               " ".join(f"a{i}" for i in range(len(shape)))
        kw = {f"a{i}": shape[i] for i in range(1, len(shape))}
        return ap.rearrange(spec, **kw)


class Builder:
    def __init__(self, nc, dbg=()):
        self.nc = nc
        self.dbgnames = set(dbg)
        self.dbg_t = {}
        self.es = ExitStack()
        self.stop_after = None
        self.nseq = BPC

    def declare_io(self):
        nc = self.nc
        self.xc = nc.dram_tensor("xc", [BPC, S, D], F32, kind="ExternalInput").ap()
        self.wq_a = nc.dram_tensor("L0_attn_Wq", [D, D], F32, kind="ExternalInput").ap()
        self.bq_a = nc.dram_tensor("L0_attn_bq", [D], F32, kind="ExternalInput").ap()
        self.wk_a = nc.dram_tensor("L0_attn_Wk", [D, D], F32, kind="ExternalInput").ap()
        self.bk_a = nc.dram_tensor("L0_attn_bk", [D], F32, kind="ExternalInput").ap()
        self.wv_a = nc.dram_tensor("L0_attn_Wv", [D, D], F32, kind="ExternalInput").ap()
        self.bv_a = nc.dram_tensor("L0_attn_bv", [D], F32, kind="ExternalInput").ap()
        self.wq_f = nc.dram_tensor("L0_ffn_Wq", [D, H * E * DH], F32, kind="ExternalInput").ap()
        self.wkv_f = nc.dram_tensor("L0_ffn_Wkv", [D, 2 * H * E * DH], F32, kind="ExternalInput").ap()
        self.ws_f = nc.dram_tensor("L0_ffn_Ws", [D, H * E], F32, kind="ExternalInput").ap()
        self.wd_f = nc.dram_tensor("L0_ffn_Wd", [D, H * E], F32, kind="ExternalInput").ap()
        self.wo_f = nc.dram_tensor("L0_ffn_Wo", [DH, E * D], F32, kind="ExternalInput").ap()
        self.yout = nc.dram_tensor("yout", [BPC, S, D], F32, kind="ExternalOutput").ap()
        if getattr(self, "dbg_ln_reload", False):
            self.dbg_ln_in = nc.dram_tensor("dbg_ln_in", [128, 8, S], F32, kind="ExternalInput").ap()
        if getattr(self, "dbg_qkv_reload", False):
            self.dbg_kfm_in = nc.dram_tensor("dbg_kfm_in", [128, G, S], F32, kind="ExternalInput").ap()
            self.dbg_qfm_in = nc.dram_tensor("dbg_qfm_in", [128, G, S], F32, kind="ExternalInput").ap()
            self.dbg_vtm_in = nc.dram_tensor("dbg_vtm_in", [128, NT, D], F32, kind="ExternalInput").ap()

    def declare_dbg(self, name, free_shape):
        if name in self.dbgnames:
            self.dbg_t[name] = self.nc.dram_tensor(
                "dbg_" + name, [128] + list(free_shape), F32, kind="ExternalOutput").ap()

    def alloc(self):
        nc, es = self.nc, self.es
        sb = lambda name, shape, dt=BF16: es.enter_context(nc.sbuf_tensor(name, shape, dt))

        self.ident = sb("ident", [128, 128])
        if "statd" in self.dbgnames:
            self.statd = sb("statd", [128, NT, 8], F32)
        self.ones1 = sb("ones1", [1, 128])
        self.bqk = sb("bqk", [128, 16], F32)
        self.bvv = sb("bvv", [1, D])
        self.wsd = sb("wsd", [128, 8, 2 * H * E])

        self.stat = sb("stat", [128, 8], F32)
        self.X = sb("X", [128, NT, D])
        self.xstage = sb("xstage", [128, 4, D], F32)
        self.wslc = sb("wslc", [128, 2, 8, 320])
        self.msk = sb("msk", [128, 2, NT, H * E])
        self.ssg = sb("ssg", [128, NT, H * E])
        self.sdg = sb("sdg", [128, NT, H * E])

        tpe = sb("tpe", [128, 12 * 512])
        c = Carve(tpe)
        self.ln_tm = c.take((4, D))
        self.sq = c.take((D,), F32)
        c = Carve(tpe)
        self.wk_ring = c.take((4, 8, 128))
        self.wv_ring = c.take((2, D))
        c = Carve(tpe)
        self.pe_buf = c.take((2, 2048))
        self.pe_scr = c.take((1024,))
        self.d32 = c.take((256,), F32)
        self.r_buf = c.take((256,))
        c = Carve(tpe)
        self.iota_scr = c.take((256,), F32)

        lnr = sb("lnr", [128, 32 * 512])
        c = Carve(lnr)
        self.lnfm = c.take((8, S))
        self.lnd = self.lnfm[:, 0, 0:512].bitcast(F32)
        c = Carve(lnr)
        self.out_stg = c.take((2, D), F32)
        self.o_sb = c.take((2, 512))
        self.z_acc = c.take((8, 320), F32)
        self.z_b = c.take((2, 384))
        self.zt = c.take((2, 384))
        self.r8 = c.take((2, 8), F32)
        self.mdr = c.take((2, H * E), F32)
        w0 = c.take((D,))
        w1 = c.take((D,))
        w2 = c.take((D,))
        self.woc = [w0, w1, w2.t if False else w2]
        self.woc2_64 = None

        shr = sb("shr", [128, 97 * 512])
        c = Carve(shr)
        self.kfm = c.take((G, S))
        self.vtm = c.take((NT, D))
        self.qfm = c.take((G, S))
        c = Carve(shr)
        self.qh = c.take((NT, 512))
        self.kh = c.take((NT, 512))
        self.vtmf = c.take((NT, 528))
        self.qfmb = c.take((4, S))
        self.kfmb = c.take((4, S))
        self.ssd = c.take((NT, 2 * H * E), F32)
        if "ped" in self.dbgnames:
            self.ped = self.lnfm[:, 4, :]
            self.scd = self.lnfm[:, 0:2, :].rearrange("p a b -> p (a b)").bitcast(F32)
            self.scd1 = self.lnfm[:, 2:4, :].rearrange("p a b -> p (a b)").bitcast(F32)
        self.gated = c.take((2, 4, 5, 64))
        self.cmp = c.take((4, NT, H), F32)
        self.cnt = c.take((NT, H), F32)

        self.PS = es.enter_context(nc.psum_tensor("PS", [128, 4096], F32))
        self.PSB = self.PS[:].bitcast(BF16)

        names = ("ldw", "ldc", "st", "tsc", "tcx", "tgp", "sexp", "seva", "sgen",
                 "vsm", "vres", "vgen", "ggen", "bar",
                 "lx0", "lx1", "lx2", "lx3", "so0", "so1", "zt0", "zt1",
                 "vdr", "vyh", "vden", "srec",
                 "lwk0", "lwk1", "lwk2", "lwk3", "lwv0", "lwv1", "lws0", "lws1")
        self.sm = {n_: [es.enter_context(nc.semaphore(n_)), 0] for n_ in names}

    def inc(self, name, bi, k=1):
        s = self.sm[name]
        bi.then_inc(s[0], k)
        s[1] += k
        return bi

    def w(self, eng, name, val=None):
        s = self.sm[name]
        v = s[1] if val is None else val
        if v > 0:
            eng.wait_ge(s[0], v)

    def n(self, name):
        return self.sm[name][1]

    def barrier(self, with_sync=True):
        blk = self.block
        with_sync = True
        target = self.n("bar") + 5

        @blk.tensor
        def _(t):
            t.drain()
            self.inc("bar", t.nop())
            t.wait_ge(self.sm["bar"][0], target)

        @blk.vector
        def _(v):
            v.drain()
            self.inc("bar", v.engine_nop())
            v.wait_ge(self.sm["bar"][0], target)

        @blk.scalar
        def _(s_):
            s_.drain()
            self.inc("bar", s_.nop())
            s_.wait_ge(self.sm["bar"][0], target)

        @blk.gpsimd
        def _(g_):
            g_.drain()
            self.inc("bar", g_.nop())
            g_.wait_ge(self.sm["bar"][0], target)

        if with_sync:
            @blk.sync
            def _(sy):
                self.w(sy, "st")
                self.w(sy, "so0")
                self.w(sy, "so1")
                self.inc("bar", sy.nop())
                sy.wait_ge(self.sm["bar"][0], target)
        assert self.n("bar") == target

    def tap(self, name, src_ap, free_shape):
        if name not in self.dbg_t:
            return
        dst = self.dbg_t[name]

        src_ap = src_ap[:]

        @self.block.gpsimd
        def _(g_):
            self.inc("st", g_.dma_start(out=dst, in_=src_ap), 16)
            g_.wait_ge(self.sm["st"][0], self.n("st"))

    def build(self):
        self.declare_io()
        for nm, shp in (("lntm", [4, D]), ("lnfm2", [8, S]), ("kfm3", [G, S]), ("kfm2", [G, S]), ("vtm2", [NT, D]), ("qfm2", [G, S]), ("ped", [2048]), ("scd", [2048]), ("scd1", [2048]), ("statd", [NT, 8]), ("lnfm", [8, S]), ("kfm", [G, S]), ("vtm", [NT, D]),
                        ("qfm", [G, S]), ("x1", [NT, D]), ("ssd", [NT, 2 * H * E]),
                        ("msk", [2, NT, H * E]), ("qh", [NT, 512]), ("kh", [NT, 512]),
                        ("vtmf", [NT, 528]), ("pe", [2, 2048])):
            self.declare_dbg(nm, shp)
        self.alloc()
        with self.es:
            self.block = self.es.enter_context(self.nc.Block())
            self.p0_consts()
            stop = False
            for s in range(self.nseq):
                self.seq = s
                phases = [
                    ("ln1", lambda: (self.ln_phase(src_hbm=True), self.barrier())),
                    ("tap1", lambda: ((self.tap("lnfm", self.lnfm, None), self.tap("statd", getattr(self, "statd", None), None) if "statd" in self.dbg_t else None) if s == 0 else None)),
                    ("a2", lambda: (self.a2_proj(), self.barrier())),
                    ("tap2", lambda: ([self.tap(n_, getattr(self, n_), None) for n_ in ("kfm", "vtm", "qfm")] + [self.tap("lnfm2", self.lnfm, None)] if s == 0 else None)),
                    ("a3", lambda: (self.a3_attn(), self.barrier(with_sync=True))),
                    ("tap3", lambda: ((self.tap("kfm2", self.kfm, None), self.tap("vtm2", self.vtm, None), self.tap("qfm2", self.qfm, None), self.tap("x1", self.X, None), self.tap("ped", getattr(self, "ped", None), None) if "ped" in self.dbg_t else None, self.tap("scd", getattr(self, "scd", None), None) if "scd" in self.dbg_t else None, self.tap("scd1", getattr(self, "scd1", None), None) if "scd1" in self.dbg_t else None) if s == 0 else None)),
                    ("ln2", lambda: (self.ln_phase(src_hbm=False), self.barrier())),
                    ("b1", lambda: (self.b1_router(), self.barrier())),
                    ("tap4", lambda: ([self.tap(n_, getattr(self, n_), None) for n_ in ("ssd", "msk")] if s == 0 else None)),
                    ("b2", lambda: (self.b2_projgate(), self.barrier())),
                    ("tap5", lambda: ([self.tap(n_, getattr(self, n_), None) for n_ in ("qh", "kh", "vtmf")] if s == 0 else None)),
                    ("b3", lambda: (self.b3_transpose(), self.barrier())),
                    ("b4", lambda: (self.b4_attn(), self.barrier(with_sync=True))),
                ]
                for pname, pf in phases:
                    pf()
                    if self.stop_after is not None and pname == self.stop_after:
                        stop = True
                        break
                if stop:
                    break

            @self.block.sync
            def _(sy):
                self.w(sy, "st")
                self.w(sy, "so0")
                self.w(sy, "so1")

    def p0_consts(self):
        blk = self.block

        @blk.gpsimd
        def _(g_):
            g_.iota(self.iota_scr[:, 0:128], pattern=[[1, 128]], base=0,
                    channel_multiplier=0, allow_small_or_imprecise_dtypes=True)
            g_.iota(self.iota_scr[:, 128:256], pattern=[[0, 128]], base=0,
                    channel_multiplier=1, allow_small_or_imprecise_dtypes=True)
            g_.memset(self.ones1[:], 1.0)
            self.inc("ldc", g_.dma_start(out=self.bqk[:, 0:8],
                     in_=self.bq_a.rearrange("(g p) -> p g", p=128),
                     allow_slow_non_contiguous=True), 16)
            self.inc("ldc", g_.dma_start(out=self.bqk[:, 8:16],
                     in_=self.bk_a.rearrange("(g p) -> p g", p=128),
                     allow_slow_non_contiguous=True), 16)
            self.inc("ldc", g_.dma_start(out=self.bvv[:], in_=self.bv_a.unsqueeze(0)), 16)
            for dc in range(8):
                self.inc("ldc", g_.dma_start(out=self.wsd[:, dc, 0:40],
                         in_=self.ws_f[dc * 128:(dc + 1) * 128, :]), 16)
                self.inc("ldc", g_.dma_start(out=self.wsd[:, dc, 40:80],
                         in_=self.wd_f[dc * 128:(dc + 1) * 128, :]), 16)
            self.inc("ggen", g_.drain())

        @blk.vector
        def _(v):
            self.w(v, "ggen", 1)
            tt = v.tensor_tensor(out=self.ident[:], in0=self.iota_scr[:, 0:128],
                                 in1=self.iota_scr[:, 128:256], op=ALU.is_equal)
            self.inc("vgen", tt)

        @blk.tensor
        def _(t):
            self.w(t, "vgen", self.n("vgen"))

    def ln_phase(self, src_hbm):
        blk, s = self.block, self.seq
        lxb = [self.n(f"lx{k}") for k in range(4)]
        sg0 = self.n("sgen")
        vg0 = self.n("vgen")
        tg0 = self.n("tgp")
        vr0 = self.n("vres")

        if src_hbm and "lntm" in self.dbg_t and self.seq == 0:
            @blk.gpsimd
            def _(g_):
                self.w(g_, "sgen", sg0 + 48)
                self.inc("st", g_.dma_start(out=self.dbg_t["lntm"], in_=self.ln_tm), 16)
                g_.wait_ge(self.sm["st"][0], self.n("st"))

        if src_hbm:
            @blk.sync
            def _(sy):
                for nt in range(NT):
                    if nt >= 4:
                        self.w(sy, "sgen", sg0 + 3 * (nt - 4) + 3)
                    self.inc(f"lx{nt % 4}", sy.dma_start(out=self.xstage[:, nt % 4, :],
                             in_=self.xc[s, nt * 128:(nt + 1) * 128, :]), 16)

        @blk.scalar
        def _(s_):
            for nt in range(NT):
                xin = self.xstage[:, nt % 4, :] if src_hbm else self.X[:, nt, :]
                if src_hbm:
                    self.w(s_, f"lx{nt % 4}", lxb[nt % 4] + 16 * (nt // 4 + 1))
                a = s_.activation(out=self.sq[:], in_=xin, func=AF.Square,
                                  accum_out=self.stat[:, 1:2])
                self.inc("sgen", a)
                self.w(s_, "vgen", vg0 + 2 * nt + 1)
                a = s_.activation(out=self.stat[:, 5:6], in_=self.stat[:, 3:4],
                                  func=AF.Sqrt)
                self.inc("sgen", a)
                self.w(s_, "vgen", vg0 + 2 * nt + 2)
                if nt >= 4:
                    self.w(s_, "tgp", tg0 + ((nt - 4) // 4 + 1) * 8)
                s_.activation(out=self.ln_tm[:, nt % 4, :], in_=xin, func=AF.Identity,
                              bias=self.stat[:, 7:8], scale=self.stat[:, 6:7])
                dr = s_.drain()
                self.inc("sgen", dr)

        @blk.vector
        def _(v):
            for nt in range(NT):
                xin = self.xstage[:, nt % 4, :] if src_hbm else self.X[:, nt, :]
                if src_hbm:
                    self.w(v, f"lx{nt % 4}", lxb[nt % 4] + 16 * (nt // 4 + 1))
                v.reduce_sum(self.stat[:, 0:1], xin, axis=AX.X)
                self.w(v, "sgen", sg0 + 3 * nt + 1)
                v.drain()
                v.tensor_scalar(out=self.stat[:, 2:3], in0=self.stat[:, 0:1],
                                scalar1=1.0 / D, scalar2=None, op0=ALU.mult)
                v.tensor_scalar(out=self.stat[:, 3:4], in0=self.stat[:, 1:2],
                                scalar1=1.0 / D, scalar2=None, op0=ALU.mult)
                v.drain()
                v.tensor_tensor(out=self.stat[:, 4:5], in0=self.stat[:, 2:3],
                                in1=self.stat[:, 2:3], op=ALU.mult)
                v.drain()
                v.tensor_tensor(out=self.stat[:, 3:4], in0=self.stat[:, 3:4],
                                 in1=self.stat[:, 4:5], op=ALU.subtract)
                v.drain()
                tt = v.tensor_scalar(out=self.stat[:, 3:4], in0=self.stat[:, 3:4],
                                     scalar1=float(EPS), scalar2=None, op0=ALU.add)
                v.drain()
                self.inc("vgen", tt)
                self.w(v, "sgen", sg0 + 3 * nt + 2)
                v.reciprocal(out=self.stat[:, 6:7], in_=self.stat[:, 5:6])
                v.drain()
                tt = v.scalar_tensor_tensor(out=self.stat[:, 7:8], in0=self.stat[:, 2:3],
                                            scalar=-1.0, in1=self.stat[:, 6:7],
                                            op0=ALU.mult, op1=ALU.mult)
                v.drain()
                self.inc("vgen", tt)
                if "statd" in self.dbg_t and self.seq == 0 and src_hbm:
                    v.drain()
                    v.tensor_copy(out=self.statd[:, nt, :], in_=self.stat[:])

                if nt % 4 == 3:
                    tg = nt // 4
                    for dc in range(8):
                        self.w(v, "tgp", tg0 + tg * 8 + dc + 1)
                        base = (dc % 4) * 2048 + (dc // 4) * 512
                        cp = v.tensor_copy(
                            out=self.lnfm[:, dc, (4 * tg) * 128:(4 * tg + 4) * 128],
                            in_=self.PSB[:, base:base + 512])
                        self.inc("vres", cp)

        @blk.tensor
        def _(t):
            for tg in range(NT // 4):
                self.w(t, "sgen", sg0 + 3 * (4 * tg + 4))
                for dc in range(8):
                    gidx = tg * 8 + dc
                    if gidx >= 4:
                        self.w(t, "vres", vr0 + gidx - 3)
                    base = (dc % 4) * 2048 + (dc // 4) * 512
                    for j in range(4):
                        t.transpose(self.PSB[:, base + j * 128: base + (j + 1) * 128],
                                    self.ln_tm[:, j, dc * 128:(dc + 1) * 128],
                                    self.ident[:])
                    self.inc("tgp", t.drain())

    def a2_proj(self):
        blk = self.block
        if getattr(self, "dbg_ln_reload", False) and self.seq == 0:
            @blk.gpsimd
            def _(g_):
                for dc in range(8):
                    self.inc("ldw", g_.dma_start(out=self.lnfm[:, dc, :],
                             in_=self.dbg_ln_in[:, dc, :]), 16)
                self.inc("ggen", g_.nop())

            @blk.tensor
            def _(t):
                self.w(t, "ldw")
                self.w(t, "ggen")
        lwkb = [self.n(f"lwk{k}") for k in range(4)]
        lwvb = [self.n(f"lwv{k}") for k in range(2)]
        tg0 = self.n("tgp")
        se0 = self.n("seva")

        @blk.gpsimd
        def _(g_):
            for it, wsrc in enumerate((self.wk_a, self.wq_a)):
                for g in range(G):
                    gi = it * G + g
                    if gi >= 4:
                        self.w(g_, "tgp", tg0 + (gi - 3) * 2)
                    for dc in range(8):
                        self.inc(f"lwk{gi % 4}", g_.dma_start(
                            out=self.wk_ring[:, gi % 4, dc, :],
                            in_=wsrc[dc * 128:(dc + 1) * 128, g * 128:(g + 1) * 128]), 16)
            for ntp in range(8):
                for dc in range(8):
                    idx = ntp * 8 + dc
                    if idx >= 2:
                        i2 = idx - 2
                        self.w(g_, "tgp", tg0 + 32 + (i2 // 8) * 9 + (i2 % 8) + 1)
                    self.inc(f"lwv{idx % 2}", g_.dma_start(
                        out=self.wv_ring[:, idx % 2, :],
                        in_=self.wv_a[dc * 128:(dc + 1) * 128, :]), 16)

        @blk.tensor
        def _(t):
            self.w(t, "ldc")
            for it, dst_bias in enumerate(((self.kfm, 8), (self.qfm, 0))):
                for g in range(G):
                    gi = it * G + g
                    self.w(t, f"lwk{gi % 4}", lwkb[gi % 4] + 16 * 8 * (gi // 4 + 1))
                    for np_ in range(2):
                        m = gi * 2 + np_
                        if m >= 2:
                            self.w(t, "seva", se0 + (m - 1) * 2)
                        for dc in range(8):
                            for nbl in range(2):
                                nb = np_ * 2 + nbl
                                bank = (np_ * 2 + nbl) * 512
                                mm = t.matmul(self.PS[:, bank:bank + 512],
                                              self.wk_ring[:, gi % 4, dc, :],
                                              self.lnfm[:, dc, nb * 512:(nb + 1) * 512],
                                              start=(dc == 0), stop=(dc == 7))
                        self.inc("tgp", mm)
            for ntp in range(8):
                for dc in range(8):
                    idx = ntp * 8 + dc
                    self.w(t, f"lwv{idx % 2}", lwvb[idx % 2] + 16 * (idx // 2 + 1))
                    if dc == 0:
                        if ntp >= 2:
                            self.w(t, "seva", se0 + 64 + (ntp - 1) * 2)
                        else:
                            self.w(t, "seva", se0 + 64)
                    for t01 in range(2):
                        nt = ntp * 2 + t01
                        rbase = (ntp % 2) * 2048 + t01 * 1024
                        for half in range(2):
                            mm = t.matmul(self.PS[:, rbase + half * 512: rbase + (half + 1) * 512],
                                          self.lnfm[:, dc, nt * 128:(nt + 1) * 128],
                                          self.wv_ring[:, idx % 2, half * 512:(half + 1) * 512],
                                          start=(dc == 0), stop=False)
                    self.inc("tgp", mm)
                for t01 in range(2):
                    rbase = (ntp % 2) * 2048 + t01 * 1024
                    for half in range(2):
                        mm = t.matmul(self.PS[:, rbase + half * 512: rbase + (half + 1) * 512],
                                      self.ones1[:],
                                      self.bvv[0:1, half * 512:(half + 1) * 512],
                                      start=False, stop=True)
                self.inc("tgp", mm)

        @blk.vector
        def _(v):
            self.w(v, "ldc")
            for it, (dst, boff) in enumerate(((self.kfm, 8), (self.qfm, 0))):
                for g in range(G):
                    gi = it * G + g
                    for np_ in range(2):
                        m = gi * 2 + np_
                        self.w(v, "tgp", tg0 + m + 1)
                        for nbl in range(2):
                            nb = np_ * 2 + nbl
                            bank = (np_ * 2 + nbl) * 512
                            a = v.tensor_scalar(out=dst[:, g, nb * 512:(nb + 1) * 512],
                                                in0=self.PS[:, bank:bank + 512],
                                                scalar1=self.bqk[:, boff + g:boff + g + 1],
                                                scalar2=None, op0=ALU.add)
                            self.inc("seva", a)
            for ntp in range(8):
                self.w(v, "tgp", tg0 + 32 + ntp * 9 + 9)
                for t01 in range(2):
                    nt = ntp * 2 + t01
                    rbase = (ntp % 2) * 2048 + t01 * 1024
                    a = v.tensor_copy(out=self.vtm[:, nt, :],
                                      in_=self.PS[:, rbase:rbase + 1024])
                    self.inc("seva", a)

    def a3_attn(self):
        blk, s = self.block, self.seq
        if getattr(self, "dbg_qkv_reload", False) and s == 0:
            import os as _os
            _which = _os.environ.get("RELOAD", "kqv")

            @blk.gpsimd
            def _(g_):
                for dc in range(8):
                    if "k" in _which:
                        self.inc("ldw", g_.dma_start(out=self.kfm[:, dc, :],
                                 in_=self.dbg_kfm_in[:, dc, :]), 16)
                    if "q" in _which:
                        self.inc("ldw", g_.dma_start(out=self.qfm[:, dc, :],
                                 in_=self.dbg_qfm_in[:, dc, :]), 16)
                if "v" in _which:
                    for ti in range(NT):
                        self.inc("ldw", g_.dma_start(out=self.vtm[:, ti, :],
                                 in_=self.dbg_vtm_in[:, ti, :]), 16)
                self.inc("ggen", g_.nop())

            @blk.tensor
            def _(t):
                self.w(t, "ldw")
                self.w(t, "ggen")
        lxb = [self.n(f"lx{k}") for k in range(4)]
        tsc0 = self.n("tsc")
        tcx0 = self.n("tcx")
        sx0 = self.n("sexp")
        vs0 = self.n("vsm")
        vr0 = self.n("vres")

        @blk.sync
        def _(sy):
            for nt in range(NT):
                if nt >= 4:
                    self.w(sy, "vres", vr0 + nt - 3)
                self.inc(f"lx{nt % 4}", sy.dma_start(out=self.xstage[:, nt % 4, :],
                         in_=self.xc[s, nt * 128:(nt + 1) * 128, :]), 16)

        if "kfm3" in self.dbg_t and self.seq == 0:
            @blk.gpsimd
            def _(g_):
                self.w(g_, "vres", vr0 + 2)
                self.inc("st", g_.dma_start(out=self.dbg_t["kfm3"], in_=self.kfm), 16)

        morder = list(range(16))
        if getattr(self, "rev_m", False):
            morder = morder[::-1]

        gg0 = self.n("ggen")

        def ctx_mm(t, i, mi_, m, g, j):
            return t.matmul(self.PS[:, 2048 + j * 1024 + g * 128:
                                    2048 + j * 1024 + (g + 1) * 128],
                            self.pe_buf[:, i % 2, g * 256 + j * 128:
                                        g * 256 + (j + 1) * 128],
                            self.vtm[:, m, g * 128:(g + 1) * 128],
                            start=(mi_ == 0 and g % 4 == 0),
                            stop=(mi_ == 15 and g % 4 == 3))

        def emit_ctx(t, ip):
            nbp, mip = ip // 16, ip % 16
            mp = morder[mip]
            if mip == 0 and nbp > 0:
                self.w(t, "vres", vr0 + nbp * 2)
            self.w(t, "vsm", vs0 + ip + 1)
            for j in range(2):
                for g in range(6):
                    mm = ctx_mm(t, ip, mip, mp, g, j)
            self.w(t, "ggen", gg0 + ip + 1)
            for j in range(2):
                for g in range(6, G):
                    mm = ctx_mm(t, ip, mip, mp, g, j)
            self.inc("tcx", mm)

        @blk.tensor
        def _(t):
            for nb in range(NB):
                for mi_, m in enumerate(morder):
                    i = nb * 16 + mi_
                    if i > 0:
                        self.w(t, "sexp", sx0 + 2 * i - 1)
                    for g in range(4):
                        mm = t.matmul(self.PS[:, g * 256:(g + 1) * 256],
                                      self.kfm[:, g, m * 128:(m + 1) * 128],
                                      self.qfm[:, g, nb * 256:(nb + 1) * 256],
                                      start=True, stop=True)
                    self.inc("tsc", mm)
                    if i > 0:
                        self.w(t, "sexp", sx0 + 2 * i)
                    for g in range(4, G):
                        mm = t.matmul(self.PS[:, g * 256:(g + 1) * 256],
                                      self.kfm[:, g, m * 128:(m + 1) * 128],
                                      self.qfm[:, g, nb * 256:(nb + 1) * 256],
                                      start=True, stop=True)
                    self.inc("tsc", mm)
                    if i > 0:
                        emit_ctx(t, i - 1)
            emit_ctx(t, NB * 16 - 1)

        vden0 = self.n("vden")
        sr0 = self.n("srec")

        @blk.scalar
        def _(s_):
            def recip_pair(j):
                self.w(s_, "vden", vden0 + j + 1)
                if j >= 1:
                    self.w(s_, "ggen", gg0 + j)
                s_.activation(out=self.lnd[:], in_=self.d32[:], func=AF.Ln)
                s_.drain()
                a = s_.activation(out=self.r_buf[:], in_=self.lnd[:],
                                  func=AF.Exp, scale=-1.0)
                self.inc("srec", a)

            for nb in range(NB):
                for m in range(16):
                    i = nb * 16 + m
                    if i >= 2:
                        self.w(s_, "tcx", tcx0 + i - 1)
                    self.w(s_, "tsc", tsc0 + 2 * i + 1)
                    a = s_.activation(out=self.pe_buf[:, i % 2, 0:1024],
                                      in_=self.PS[:, 0:1024], func=AF.Exp,
                                      scale=1.0 / 32.0)
                    self.inc("sexp", a)
                    self.w(s_, "tsc", tsc0 + 2 * i + 2)
                    a = s_.activation(out=self.pe_buf[:, i % 2, 1024:2048],
                                      in_=self.PS[:, 1024:2048], func=AF.Exp,
                                      scale=1.0 / 32.0)
                    self.inc("sexp", a)
                    if i > 0:
                        recip_pair(i - 1)
            recip_pair(NB * 16 - 1)

        @blk.vector
        def _(v):
            def muls(j):
                self.w(v, "srec", sr0 + j + 1)
                pe_ = self.pe_buf[:, j % 2, :]
                for g in range(6):
                    tt = v.tensor_tensor(out=pe_[:, g * 256:(g + 1) * 256],
                                         in0=pe_[:, g * 256:(g + 1) * 256],
                                         in1=self.r_buf[:], op=ALU.mult)
                self.inc("vsm", tt)

            for nb in range(NB):
                for m in range(16):
                    i = nb * 16 + m
                    pe = self.pe_buf[:, i % 2, :]
                    t0, t1, t2, t3 = (self.pe_scr[:, k * 256:(k + 1) * 256] for k in range(4))
                    self.w(v, "sexp", sx0 + 2 * i + 1)
                    v.tensor_tensor(out=t0, in0=pe[:, 0:256], in1=pe[:, 256:512], op=ALU.add)
                    v.tensor_tensor(out=t1, in0=pe[:, 512:768], in1=pe[:, 768:1024], op=ALU.add)
                    self.w(v, "sexp", sx0 + 2 * i + 2)
                    v.tensor_tensor(out=t2, in0=pe[:, 1024:1280], in1=pe[:, 1280:1536], op=ALU.add)
                    v.tensor_tensor(out=t3, in0=pe[:, 1536:1792], in1=pe[:, 1792:2048], op=ALU.add)
                    v.drain()
                    v.tensor_tensor(out=t0, in0=t0, in1=t1, op=ALU.add)
                    v.tensor_tensor(out=t2, in0=t2, in1=t3, op=ALU.add)
                    v.drain()
                    if i > 0:
                        self.w(v, "srec", sr0 + i)
                    tt = v.tensor_tensor(out=self.d32[:], in0=t0, in1=t2, op=ALU.add)
                    self.inc("vden", tt)
                    if m > 0:
                        muls(i - 1)
                muls(nb * 16 + 15)
                self.w(v, "tcx", tcx0 + nb * 16 + 16)
                for j in range(2):
                    nt = nb * 2 + j
                    self.w(v, f"lx{nt % 4}", lxb[nt % 4] + 16 * (nt // 4 + 1))
                    tt = v.tensor_tensor(out=self.X[:, nt, :],
                                         in0=self.PS[:, 2048 + j * 1024:2048 + (j + 1) * 1024],
                                         in1=self.xstage[:, nt % 4, :], op=ALU.add)
                    self.inc("vres", tt)

        @blk.gpsimd
        def _(g_):
            for nb in range(NB):
                for m in range(16):
                    i = nb * 16 + m
                    pe = self.pe_buf[:, i % 2, :]
                    self.w(g_, "vsm", vs0 + i + 1)
                    for g in range(6, G):
                        tt = g_.tensor_tensor(out=pe[:, g * 256:(g + 1) * 256],
                                              in0=pe[:, g * 256:(g + 1) * 256],
                                              in1=self.r_buf[:], op=ALU.mult)
                    self.inc("ggen", tt)

    def b1_router(self):
        blk = self.block
        tg0 = self.n("tgp")
        sg0 = self.n("sgen")

        @blk.tensor
        def _(t):
            self.w(t, "ldc")
            for nt in range(NT):
                if nt >= 2:
                    self.w(t, "sgen", sg0 + nt - 1)
                r = (nt % 2) * 512
                for dc in range(8):
                    mm = t.matmul(self.PS[:, r:r + 80],
                                  self.lnfm[:, dc, nt * 128:(nt + 1) * 128],
                                  self.wsd[:, dc, :],
                                  start=(dc == 0), stop=(dc == 7))
                self.inc("tgp", mm)

        @blk.scalar
        def _(s_):
            for nt in range(NT):
                self.w(s_, "tgp", tg0 + nt + 1)
                r = (nt % 2) * 512
                a = s_.activation(out=self.ssd[:, nt, :], in_=self.PS[:, r:r + 80],
                                  func=AF.Sigmoid)
                self.inc("sgen", a)

        @blk.vector
        def _(v):
            self.w(v, "sgen", sg0 + NT)
            for gi in range(2):
                gsl = self.ssd[:, :, gi * 40:(gi + 1) * 40].rearrange(
                    "p t (h e) -> p t h e", e=E)
                for ei in range(E):
                    jj = 0
                    for ej in range(E):
                        if ej == ei:
                            continue
                        op = ALU.is_ge if ej < ei else ALU.is_gt
                        v.tensor_tensor(out=self.cmp[:, jj, :, :].unsqueeze(3),
                                        in0=gsl[:, :, :, ej:ej + 1],
                                        in1=gsl[:, :, :, ei:ei + 1], op=op)
                        jj += 1
                    v.drain()
                    v.tensor_tensor(out=self.cnt[:], in0=self.cmp[:, 0, :, :],
                                    in1=self.cmp[:, 1, :, :], op=ALU.add)
                    v.drain()
                    v.tensor_tensor(out=self.cnt[:], in0=self.cnt[:],
                                    in1=self.cmp[:, 2, :, :], op=ALU.add)
                    v.drain()
                    v.tensor_tensor(out=self.cnt[:], in0=self.cnt[:],
                                    in1=self.cmp[:, 3, :, :], op=ALU.add)
                    v.drain()
                    mview = self.msk[:, gi, :, :].rearrange(
                        "p t (h e) -> p t h e", e=E)[:, :, :, ei:ei + 1]
                    v.tensor_scalar(out=mview, in0=self.cnt[:].unsqueeze(3),
                                    scalar1=2.5, scalar2=None, op0=ALU.is_le)
                    v.drain()
            v.tensor_tensor(out=self.ssg[:], in0=self.ssd[:, :, 0:40],
                            in1=self.msk[:, 0, :, :], op=ALU.mult)
            v.tensor_tensor(out=self.sdg[:], in0=self.ssd[:, :, 40:80],
                            in1=self.msk[:, 1, :, :], op=ALU.mult)
            vv = self.vtmf.rearrange("p t (h c) -> p t h c", c=66)
            tt = v.memset(vv[:, :, :, 64:65], 1.0)
            self.inc("vgen", tt)

    def b2_projgate(self):
        blk = self.block
        lwsb = [self.n(f"lws{k}") for k in range(2)]
        tg0 = self.n("tgp")
        vg0 = self.n("vgen")

        blocks = []
        for h in range(H):
            blocks.append((self.wq_f, h * 320, self.sdg, self.qh, h * 64, 64))
        for h in range(H):
            blocks.append((self.wkv_f, h * 320, self.ssg, self.kh, h * 64, 64))
        for h in range(H):
            blocks.append((self.wkv_f, 2560 + h * 320, self.ssg, self.vtmf, h * 66, 66))

        @blk.gpsimd
        def _(g_):
            for bi_, (wsrc, coff, _, _, _, _) in enumerate(blocks):
                if bi_ >= 2:
                    self.w(g_, "vgen", vg0 + (bi_ - 2) * 4 + 4)
                for dc in range(8):
                    self.inc(f"lws{bi_ % 2}", g_.dma_start(
                        out=self.wslc[:, bi_ % 2, dc, :],
                        in_=wsrc[dc * 128:(dc + 1) * 128, coff:coff + 320]), 16)

        @blk.tensor
        def _(t):
            for bi_ in range(24):
                self.w(t, f"lws{bi_ % 2}", lwsb[bi_ % 2] + 16 * 8 * (bi_ // 2 + 1))
                for tg in range(4):
                    idx = bi_ * 4 + tg
                    if idx >= 2:
                        self.w(t, "vgen", vg0 + idx - 1)
                    r = (tg % 2) * 2048
                    for t_ in range(4):
                        nt = tg * 4 + t_
                        for dc in range(8):
                            mm = t.matmul(self.PS[:, r + t_ * 512: r + t_ * 512 + 320],
                                          self.lnfm[:, dc, nt * 128:(nt + 1) * 128],
                                          self.wslc[:, bi_ % 2, dc, :],
                                          start=(dc == 0), stop=(dc == 7))
                    self.inc("tgp", mm)

        @blk.vector
        def _(v):
            for bi_, (wsrc, coff, gates, dst, dcol, dstride) in enumerate(blocks):
                for tg in range(4):
                    idx = bi_ * 4 + tg
                    self.w(v, "tgp", tg0 + idx + 1)
                    r = (tg % 2) * 2048
                    psin = self.PS[:, r:r + 2048].rearrange(
                        "p (t c) -> p t c", c=512)[:, :, 0:320].rearrange(
                        "p t (e d) -> p t e d", d=64)
                    gin = gates[:, tg * 4:(tg + 1) * 4,
                                (coff % 2560) // 320 * 5:(coff % 2560) // 320 * 5 + 5]
                    gin = gin.unsqueeze(3).to_broadcast([128, 4, 5, 64])
                    gb = self.gated[:, idx % 2, :, :, :]
                    tt = v.tensor_tensor(out=gb, in0=psin, in1=gin, op=ALU.mult)
                    self.inc("vgen", tt)
                    v.drain()
                    dv = dst[:, tg * 4:(tg + 1) * 4, dcol:dcol + 64]
                    with self.nc.allow_low_precision(reason="bf16 gated sum"):
                        v.reduce_sum(dv.unsqueeze(3),
                                     gb.rearrange("p t e c -> p t c e"), axis=AX.X)

    def b3_transpose(self):
        blk = self.block
        tg0 = self.n("tgp")
        vg0 = self.n("vgen")

        @blk.tensor
        def _(t):
            for ti, (src, dst) in enumerate(((self.qh, self.qfmb), (self.kh, self.kfmb))):
                for ntg in range(4):
                    for p in range(4):
                        idx = (ti * 4 + ntg) * 4 + p
                        if idx >= 2:
                            self.w(t, "vgen", vg0 + idx - 1)
                        base = (p % 2) * 2048
                        for tt_ in range(4):
                            nt = ntg * 4 + tt_
                            tp = t.transpose(self.PSB[:, base + tt_ * 128: base + (tt_ + 1) * 128],
                                             src[:, nt, p * 128:(p + 1) * 128],
                                             self.ident[:])
                        self.inc("tgp", tp)

        @blk.vector
        def _(v):
            for ti, (src, dst) in enumerate(((self.qh, self.qfmb), (self.kh, self.kfmb))):
                for ntg in range(4):
                    for p in range(4):
                        idx = (ti * 4 + ntg) * 4 + p
                        self.w(v, "tgp", tg0 + idx + 1)
                        base = (p % 2) * 2048
                        cp = v.tensor_copy(out=dst[:, p, ntg * 512:(ntg + 1) * 512],
                                           in_=self.PSB[:, base:base + 512])
                        self.inc("vgen", cp)

    def b4_attn(self):
        blk, s = self.block, self.seq
        ldw0 = self.n("ldw")

        @blk.gpsimd
        def _(g_):
            for e_ in range(2):
                self.inc("ldw", g_.dma_start(out=self.woc[0][e_ * 64:(e_ + 1) * 64, :],
                         in_=self.wo_f[:, e_ * D:(e_ + 1) * D]), 16)
                self.inc("ldw", g_.dma_start(out=self.woc[1][e_ * 64:(e_ + 1) * 64, :],
                         in_=self.wo_f[:, (2 + e_) * D:(3 + e_) * D]), 16)
            self.inc("ldw", g_.dma_start(out=self.woc[2][0:64, :], in_=self.wo_f[:, 4 * D:5 * D]), 16)

        tsc0 = self.n("tsc")
        tcx0 = self.n("tcx")
        tg0 = self.n("tgp")
        sx0 = self.n("sexp")
        se0 = self.n("seva")
        vs0 = self.n("vsm")
        vd0 = self.n("vdr")
        vy0 = self.n("vyh")
        sob = [self.n(f"so{k}") for k in range(2)]
        ztb = [self.n("zt0"), self.n("zt1")]

        OB, YB = 2048, 3584
        ocol = lambda q: OB + (q // 7) * 512 + (q % 7) * 65

        def o_group(t, i):
            nb_, jc_ = i // 16, i % 16
            if jc_ == 0 and nb_ > 0:
                self.w(t, "seva", se0 + nb_ * 2)
                self.w(t, "vdr", vd0 + nb_ * 2)
            for j in range(2):
                for h in range(H):
                    q = j * 8 + h
                    mm = t.matmul(self.PS[:, ocol(q):ocol(q) + 65],
                                  self.pe_buf[:, i % 2, hpos(h) + j * 128:
                                              hpos(h) + (j + 1) * 128],
                                  self.vtmf[:, jc_, h * 66:h * 66 + 65],
                                  start=(jc_ == 0 and q % 7 == 0),
                                  stop=(jc_ == 15 and (q % 7 == 6 or q == 15)))
            self.inc("tcx", mm)

        def y_piece(t, nb_, pidx):
            j, half = pidx // 2, pidx % 2
            nt = nb_ * 2 + j
            self.w(t, f"zt{nt % 2}", ztb[nt % 2] + 48 * (nt // 2 + 1))
            self.w(t, "vyh", vy0 + 2 * nt + half)
            if nb_ == 0 and pidx == 0:
                self.w(t, "ldw", ldw0 + 16 * 5)
            for cch in range(3):
                if cch < 2:
                    lhs = self.zt[:, nt % 2, cch * 128:(cch + 1) * 128]
                    rhsw = self.woc[cch][:, half * 512:(half + 1) * 512]
                else:
                    lhs = self.zt[0:64, nt % 2, 256:384]
                    rhsw = self.woc[2][0:64, half * 512:(half + 1) * 512]
                mm = t.matmul(self.PS[:, YB:YB + 512], lhs, rhsw,
                              start=(cch == 0), stop=(cch == 2))
            self.inc("tgp", mm)

        @blk.tensor
        def _(t):
            for nb in range(NB):
                for jc in range(16):
                    i = nb * 16 + jc
                    self.w(t, "sexp", sx0 + i)
                    for p in range(4):
                        h0, h1 = 2 * p, 2 * p + 1
                        t.matmul(self.PS[:, hpos(h0):hpos(h0) + 256],
                                 self.kfmb[0:64, p, jc * 128:(jc + 1) * 128],
                                 self.qfmb[0:64, p, nb * 256:(nb + 1) * 256],
                                 start=True, stop=True)
                        mm = t.matmul(self.PS[:, hpos(h1):hpos(h1) + 256],
                                      self.kfmb[64:128, p, jc * 128:(jc + 1) * 128],
                                      self.qfmb[64:128, p, nb * 256:(nb + 1) * 256],
                                      start=True, stop=True, tile_position=(64, 0))
                    self.inc("tsc", mm)
                    if nb > 0 and jc in (5, 7, 9, 11):
                        y_piece(t, nb - 1, (jc - 5) // 2)
                    if jc > 0:
                        o_group(t, i - 1)
                self.w(t, "sexp", sx0 + nb * 16 + 16)
                o_group(t, nb * 16 + 15)
            for pidx in range(4):
                y_piece(t, NB - 1, pidx)

        @blk.scalar
        def _(s_):
            for nb in range(NB):
                for jc in range(16):
                    i = nb * 16 + jc
                    self.w(s_, "tsc", tsc0 + i + 1)
                    if i >= 2:
                        self.w(s_, "tcx", tcx0 + i - 1)
                    a = s_.activation(out=self.pe_buf[:, i % 2, :],
                                      in_=self.PS[:, 0:2048], func=AF.Exp,
                                      scale=0.125)
                    self.inc("sexp", a)
                self.w(s_, "tcx", tcx0 + nb * 16 + 16)
                for j in range(2):
                    nt = nb * 2 + j
                    if nt >= 2:
                        self.w(s_, "vsm", vs0 + nt - 1)
                    if j == 0:
                        segs = (((2048, 7), 0), ((2560, 1), 448))
                    else:
                        segs = (((2625, 6), 0), ((3072, 2), 384))
                    for (pc, nq), oc in segs:
                        pin = self.PS[:, pc:pc + nq * 65].rearrange(
                            "p (q c) -> p q c", c=65)[:, :, 0:64]
                        a = s_.activation(
                            out=self.o_sb[:, nt % 2, oc:oc + nq * 64].rearrange(
                                "p (q c) -> p q c", c=64),
                            in_=pin, func=AF.Copy)
                    self.inc("seva", a)

        @blk.vector
        def _(v):
            v.memset(self.z_b[:, :, 320:384], 0.0)
            for nb in range(NB + 1):
                if nb > 0:
                    for j in range(2):
                        nt = (nb - 1) * 2 + j
                        for half in range(2):
                            self.w(v, "tgp", tg0 + 2 * nt + half + 1)
                            if half == 0 and nt >= 2:
                                self.w(v, f"so{nt % 2}",
                                       sob[nt % 2] + 16 * (nt // 2))
                            tt = v.tensor_tensor(
                                out=self.out_stg[:, nt % 2, half * 512:(half + 1) * 512],
                                in0=self.PS[:, YB:YB + 512],
                                in1=self.X[:, nt, half * 512:(half + 1) * 512],
                                op=ALU.add)
                            self.inc("vyh", tt)
                if nb == NB:
                    break
                self.w(v, "tcx", tcx0 + nb * 16 + 16)
                for j in range(2):
                    nt = nb * 2 + j
                    if j == 0:
                        segs = (((2048 + 64, 7), 0), ((2624, 1), 7))
                    else:
                        segs = (((2625 + 64, 6), 0), ((3136, 2), 6))
                    for (pc, nq), rc in segs:
                        if nq > 1:
                            pv = self.PS[:, pc:pc + nq * 65].rearrange(
                                "p (q c) -> p q c", c=65)[:, :, 0:1]
                            rr = v.reciprocal(
                                out=self.r8[:, nt % 2, rc:rc + nq].unsqueeze(2),
                                in_=pv)
                        else:
                            rr = v.reciprocal(out=self.r8[:, nt % 2, rc:rc + 1],
                                              in_=self.PS[:, pc:pc + 1])
                    self.inc("vdr", rr)
                v.drain()
                for j in range(2):
                    nt = nb * 2 + j
                    for h in range(H):
                        v.tensor_scalar(out=self.mdr[:, nt % 2, h * 5:(h + 1) * 5],
                                        in0=self.msk[:, 1, nt, h * 5:(h + 1) * 5],
                                        scalar1=self.r8[:, nt % 2, h:h + 1],
                                        scalar2=None, op0=ALU.mult)
                    v.drain()
                    self.w(v, "seva", se0 + nt + 1)
                    ob = self.o_sb[:, nt % 2, :].rearrange(
                        "p (h c) -> p h c", c=64).unsqueeze(2).to_broadcast(
                        [128, H, E, 64])
                    mb = self.mdr[:, nt % 2, :].rearrange(
                        "p (h e) -> p h e", e=E).unsqueeze(3).to_broadcast(
                        [128, H, E, 64])
                    v.tensor_tensor(out=self.z_acc.rearrange(
                        "p h (e c) -> p h e c", c=64), in0=ob, in1=mb, op=ALU.mult)
                    v.drain()
                    if nt >= 2:
                        self.w(v, f"zt{nt % 2}", ztb[nt % 2] + 48 * (nt // 2))
                    with self.nc.allow_low_precision(reason="fp32 accum, bf16 out"):
                        cp = v.reduce_sum(self.z_b[:, nt % 2, 0:320].unsqueeze(2),
                                          self.z_acc.rearrange("p h c -> p c h"),
                                          axis=AX.X)
                    self.inc("vsm", cp)

        @blk.sync
        def _(sy):
            for nb in range(NB):
                for j in range(2):
                    nt = nb * 2 + j
                    self.w(sy, "vsm", vs0 + nt + 1)
                    if nt >= 2:
                        self.w(sy, "tgp", tg0 + 2 * nt - 2)
                    for cch in range(3):
                        self.inc(f"zt{nt % 2}", sy.dma_start(
                            out=self.zt[:, nt % 2, cch * 128:(cch + 1) * 128],
                            in_=self.z_b[:, nt % 2, cch * 128:(cch + 1) * 128],
                            transpose=True), 16)
                for j in range(2):
                    nt = nb * 2 + j
                    self.w(sy, "vyh", vy0 + 2 * nt + 2)
                    self.inc(f"so{nt % 2}", sy.dma_start(
                             out=self.yout[s, nt * 128:(nt + 1) * 128, :],
                             in_=self.out_stg[:, nt % 2, :]), 16)


def build_nc(dbg=(), stop_after=None, nseq=BPC):
    nc = bass.Bass("TRN2", target_bir_lowering=False)
    b = Builder(nc, dbg)
    b.stop_after = stop_after
    b.nseq = nseq
    b.build()
    return nc, b.dbg_t


_NC_CACHE = {}


def kernel(**inputs):
    key = "main"
    if key not in _NC_CACHE:
        _NC_CACHE[key] = build_nc()
    nc, _ = _NC_CACHE[key]
    x = np.ascontiguousarray(inputs["L0_attn_hidden_states"])
    wkeys = ["L0_attn_Wq", "L0_attn_bq", "L0_attn_Wk", "L0_attn_bk", "L0_attn_Wv",
             "L0_attn_bv", "L0_ffn_Wq", "L0_ffn_Wkv", "L0_ffn_Ws", "L0_ffn_Wd",
             "L0_ffn_Wo"]
    in_maps = []
    for c in range(NCORE):
        m = {k: np.ascontiguousarray(inputs[k]) for k in wkeys}
        m["xc"] = x[c * BPC:(c + 1) * BPC]
        in_maps.append(m)
    res = run_bass_kernel_spmd(nc, in_maps, core_ids=list(range(NCORE)))
    return np.concatenate([r["yout"] for r in res.results], axis=0)


# version 13
# speedup vs baseline: 1.0021x; 1.0021x over previous
import numpy as np
from contextlib import ExitStack

import concourse.bass as bass
import concourse.mybir as mybir
from concourse.bass_utils import run_bass_kernel_spmd

F32 = mybir.dt.float32
BF16 = mybir.dt.bfloat16
AF = mybir.ActivationFunctionType
ALU = mybir.AluOpType
AX = mybir.AxisListType

B, S, D = 16, 2048, 1024
NCORE = 8
BPC = B // NCORE
NT = S // 128
G = 8
H, E, DH = 8, 5, 64
NB = S // 256
EPS = 1e-5


def hpos(h):
    return (h % 2) * 1024 + (h // 2) * 256


class Carve:
    def __init__(self, t):
        self.t = t
        self.off = 0

    def take(self, shape, dt=BF16):
        n = 1
        for d_ in shape:
            n *= d_
        w = n * (2 if dt is F32 else 1)
        self.off = (self.off + 15) // 16 * 16
        ap = self.t[:, self.off:self.off + w]
        self.off += w
        if dt is F32:
            ap = ap.bitcast(F32)
        if len(shape) == 1:
            return ap
        spec = "p (" + " ".join(f"a{i}" for i in range(len(shape))) + ") -> p " + \
               " ".join(f"a{i}" for i in range(len(shape)))
        kw = {f"a{i}": shape[i] for i in range(1, len(shape))}
        return ap.rearrange(spec, **kw)


class Builder:
    def __init__(self, nc, dbg=()):
        self.nc = nc
        self.dbgnames = set(dbg)
        self.dbg_t = {}
        self.es = ExitStack()
        self.stop_after = None
        self.nseq = BPC

    def declare_io(self):
        nc = self.nc
        self.xc = nc.dram_tensor("xc", [BPC, S, D], F32, kind="ExternalInput").ap()
        self.wq_a = nc.dram_tensor("L0_attn_Wq", [D, D], F32, kind="ExternalInput").ap()
        self.bq_a = nc.dram_tensor("L0_attn_bq", [D], F32, kind="ExternalInput").ap()
        self.wk_a = nc.dram_tensor("L0_attn_Wk", [D, D], F32, kind="ExternalInput").ap()
        self.bk_a = nc.dram_tensor("L0_attn_bk", [D], F32, kind="ExternalInput").ap()
        self.wv_a = nc.dram_tensor("L0_attn_Wv", [D, D], F32, kind="ExternalInput").ap()
        self.bv_a = nc.dram_tensor("L0_attn_bv", [D], F32, kind="ExternalInput").ap()
        self.wq_f = nc.dram_tensor("L0_ffn_Wq", [D, H * E * DH], F32, kind="ExternalInput").ap()
        self.wkv_f = nc.dram_tensor("L0_ffn_Wkv", [D, 2 * H * E * DH], F32, kind="ExternalInput").ap()
        self.ws_f = nc.dram_tensor("L0_ffn_Ws", [D, H * E], F32, kind="ExternalInput").ap()
        self.wd_f = nc.dram_tensor("L0_ffn_Wd", [D, H * E], F32, kind="ExternalInput").ap()
        self.wo_f = nc.dram_tensor("L0_ffn_Wo", [DH, E * D], F32, kind="ExternalInput").ap()
        self.yout = nc.dram_tensor("yout", [BPC, S, D], F32, kind="ExternalOutput").ap()
        if getattr(self, "dbg_ln_reload", False):
            self.dbg_ln_in = nc.dram_tensor("dbg_ln_in", [128, 8, S], F32, kind="ExternalInput").ap()
        if getattr(self, "dbg_qkv_reload", False):
            self.dbg_kfm_in = nc.dram_tensor("dbg_kfm_in", [128, G, S], F32, kind="ExternalInput").ap()
            self.dbg_qfm_in = nc.dram_tensor("dbg_qfm_in", [128, G, S], F32, kind="ExternalInput").ap()
            self.dbg_vtm_in = nc.dram_tensor("dbg_vtm_in", [128, NT, D], F32, kind="ExternalInput").ap()

    def declare_dbg(self, name, free_shape):
        if name in self.dbgnames:
            self.dbg_t[name] = self.nc.dram_tensor(
                "dbg_" + name, [128] + list(free_shape), F32, kind="ExternalOutput").ap()

    def alloc(self):
        nc, es = self.nc, self.es
        sb = lambda name, shape, dt=BF16: es.enter_context(nc.sbuf_tensor(name, shape, dt))

        self.ident = sb("ident", [128, 128])
        if "statd" in self.dbgnames:
            self.statd = sb("statd", [128, NT, 8], F32)
        self.ones1 = sb("ones1", [1, 128])
        self.bqk = sb("bqk", [128, 16], F32)
        self.bvv = sb("bvv", [1, D])
        self.wsd = sb("wsd", [128, 8, 2 * H * E])

        self.stat = sb("stat", [128, 8], F32)
        self.X = sb("X", [128, NT, D])
        self.xstage = sb("xstage", [128, 4, D], F32)
        self.wslc = sb("wslc", [128, 2, 8, 320])
        self.msk = sb("msk", [128, 2, NT, H * E])
        self.ssg = sb("ssg", [128, NT, H * E])
        self.sdg = sb("sdg", [128, NT, H * E])

        tpe = sb("tpe", [128, 12 * 512])
        c = Carve(tpe)
        self.ln_tm = c.take((4, D))
        self.sq = c.take((D,), F32)
        c = Carve(tpe)
        self.wk_ring = c.take((4, 8, 128))
        self.wv_ring = c.take((2, D))
        c = Carve(tpe)
        self.pe_buf = c.take((2, 2048))
        self.pe_scr = c.take((1024,))
        self.d32 = c.take((256,), F32)
        self.r_buf = c.take((2, 256))
        c = Carve(tpe)
        self.iota_scr = c.take((256,), F32)

        lnr = sb("lnr", [128, 32 * 512])
        c = Carve(lnr)
        self.lnfm = c.take((8, S))
        self.lnd = self.lnfm[:, 0, 0:512].bitcast(F32)
        self.pe_ext = [self.lnfm[:, 1, :], self.lnfm[:, 2, :]]
        self.d32b = self.lnfm[:, 3, 0:512].bitcast(F32)
        c = Carve(lnr)
        self.out_stg = c.take((2, D), F32)
        self.o_sb = c.take((2, 512))
        self.z_acc = c.take((8, 320), F32)
        self.z_b = c.take((2, 384))
        self.zt = c.take((2, 384))
        self.r8 = c.take((2, 8), F32)
        self.mdr = c.take((2, H * E), F32)
        w0 = c.take((D,))
        w1 = c.take((D,))
        w2 = c.take((D,))
        self.woc = [w0, w1, w2.t if False else w2]
        self.woc2_64 = None

        shr = sb("shr", [128, 97 * 512])
        c = Carve(shr)
        self.kfm = c.take((G, S))
        self.vtm = c.take((NT, D))
        self.qfm = c.take((G, S))
        c = Carve(shr)
        self.qh = c.take((NT, 512))
        self.kh = c.take((NT, 512))
        self.vtmf = c.take((NT, 528))
        self.qfmb = c.take((4, S))
        self.kfmb = c.take((4, S))
        self.ssd = c.take((NT, 2 * H * E), F32)
        if "ped" in self.dbgnames:
            self.ped = self.lnfm[:, 4, :]
            self.scd = self.lnfm[:, 0:2, :].rearrange("p a b -> p (a b)").bitcast(F32)
            self.scd1 = self.lnfm[:, 2:4, :].rearrange("p a b -> p (a b)").bitcast(F32)
        self.gated = c.take((2, 4, 5, 64))
        self.cmp = c.take((4, NT, H), F32)
        self.cnt = c.take((NT, H), F32)

        self.PS = es.enter_context(nc.psum_tensor("PS", [128, 4096], F32))
        self.PSB = self.PS[:].bitcast(BF16)

        names = ("ldw", "ldc", "st", "tsc", "tcx", "tgp", "sexp", "seva", "sgen",
                 "vsm", "vres", "vgen", "ggen", "bar",
                 "lx0", "lx1", "lx2", "lx3", "so0", "so1", "zt0", "zt1",
                 "vdr", "vyh", "vden", "srec",
                 "lwk0", "lwk1", "lwk2", "lwk3", "lwv0", "lwv1", "lws0", "lws1")
        self.sm = {n_: [es.enter_context(nc.semaphore(n_)), 0] for n_ in names}

    def inc(self, name, bi, k=1):
        s = self.sm[name]
        bi.then_inc(s[0], k)
        s[1] += k
        return bi

    def w(self, eng, name, val=None):
        s = self.sm[name]
        v = s[1] if val is None else val
        if v > 0:
            eng.wait_ge(s[0], v)

    def n(self, name):
        return self.sm[name][1]

    def pe_a3(self, i):
        k = i % 4
        return self.pe_buf[:, k, :] if k < 2 else self.pe_ext[k - 2]

    def d32s(self, i):
        return self.d32[:] if i % 2 == 0 else self.d32b

    def barrier(self, with_sync=True):
        blk = self.block
        with_sync = True
        target = self.n("bar") + 5

        @blk.tensor
        def _(t):
            t.drain()
            self.inc("bar", t.nop())
            t.wait_ge(self.sm["bar"][0], target)

        @blk.vector
        def _(v):
            v.drain()
            self.inc("bar", v.engine_nop())
            v.wait_ge(self.sm["bar"][0], target)

        @blk.scalar
        def _(s_):
            s_.drain()
            self.inc("bar", s_.nop())
            s_.wait_ge(self.sm["bar"][0], target)

        @blk.gpsimd
        def _(g_):
            g_.drain()
            self.inc("bar", g_.nop())
            g_.wait_ge(self.sm["bar"][0], target)

        if with_sync:
            @blk.sync
            def _(sy):
                self.w(sy, "st")
                self.w(sy, "so0")
                self.w(sy, "so1")
                self.inc("bar", sy.nop())
                sy.wait_ge(self.sm["bar"][0], target)
        assert self.n("bar") == target

    def tap(self, name, src_ap, free_shape):
        if name not in self.dbg_t:
            return
        dst = self.dbg_t[name]

        src_ap = src_ap[:]

        @self.block.gpsimd
        def _(g_):
            self.inc("st", g_.dma_start(out=dst, in_=src_ap), 16)
            g_.wait_ge(self.sm["st"][0], self.n("st"))

    def build(self):
        self.declare_io()
        for nm, shp in (("lntm", [4, D]), ("lnfm2", [8, S]), ("kfm3", [G, S]), ("kfm2", [G, S]), ("vtm2", [NT, D]), ("qfm2", [G, S]), ("ped", [2048]), ("scd", [2048]), ("scd1", [2048]), ("statd", [NT, 8]), ("lnfm", [8, S]), ("kfm", [G, S]), ("vtm", [NT, D]),
                        ("qfm", [G, S]), ("x1", [NT, D]), ("ssd", [NT, 2 * H * E]),
                        ("msk", [2, NT, H * E]), ("qh", [NT, 512]), ("kh", [NT, 512]),
                        ("vtmf", [NT, 528]), ("pe", [2, 2048])):
            self.declare_dbg(nm, shp)
        self.alloc()
        with self.es:
            self.block = self.es.enter_context(self.nc.Block())
            self.p0_consts()
            stop = False
            for s in range(self.nseq):
                self.seq = s
                phases = [
                    ("ln1", lambda: (self.ln_phase(src_hbm=True), self.barrier())),
                    ("tap1", lambda: ((self.tap("lnfm", self.lnfm, None), self.tap("statd", getattr(self, "statd", None), None) if "statd" in self.dbg_t else None) if s == 0 else None)),
                    ("a2", lambda: (self.a2_proj(), self.barrier())),
                    ("tap2", lambda: ([self.tap(n_, getattr(self, n_), None) for n_ in ("kfm", "vtm", "qfm")] + [self.tap("lnfm2", self.lnfm, None)] if s == 0 else None)),
                    ("a3", lambda: (self.a3_attn(), self.barrier(with_sync=True))),
                    ("tap3", lambda: ((self.tap("kfm2", self.kfm, None), self.tap("vtm2", self.vtm, None), self.tap("qfm2", self.qfm, None), self.tap("x1", self.X, None), self.tap("ped", getattr(self, "ped", None), None) if "ped" in self.dbg_t else None, self.tap("scd", getattr(self, "scd", None), None) if "scd" in self.dbg_t else None, self.tap("scd1", getattr(self, "scd1", None), None) if "scd1" in self.dbg_t else None) if s == 0 else None)),
                    ("ln2", lambda: (self.ln_phase(src_hbm=False), self.barrier())),
                    ("b1", lambda: (self.b1_router(), self.barrier())),
                    ("tap4", lambda: ([self.tap(n_, getattr(self, n_), None) for n_ in ("ssd", "msk")] if s == 0 else None)),
                    ("b2", lambda: (self.b2_projgate(), self.barrier())),
                    ("tap5", lambda: ([self.tap(n_, getattr(self, n_), None) for n_ in ("qh", "kh", "vtmf")] if s == 0 else None)),
                    ("b3", lambda: (self.b3_transpose(), self.barrier())),
                    ("b4", lambda: (self.b4_attn(), self.barrier(with_sync=True))),
                ]
                for pname, pf in phases:
                    pf()
                    if self.stop_after is not None and pname == self.stop_after:
                        stop = True
                        break
                if stop:
                    break

            @self.block.sync
            def _(sy):
                self.w(sy, "st")
                self.w(sy, "so0")
                self.w(sy, "so1")

    def p0_consts(self):
        blk = self.block

        @blk.gpsimd
        def _(g_):
            g_.iota(self.iota_scr[:, 0:128], pattern=[[1, 128]], base=0,
                    channel_multiplier=0, allow_small_or_imprecise_dtypes=True)
            g_.iota(self.iota_scr[:, 128:256], pattern=[[0, 128]], base=0,
                    channel_multiplier=1, allow_small_or_imprecise_dtypes=True)
            g_.memset(self.ones1[:], 1.0)
            self.inc("ldc", g_.dma_start(out=self.bqk[:, 0:8],
                     in_=self.bq_a.rearrange("(g p) -> p g", p=128),
                     allow_slow_non_contiguous=True), 16)
            self.inc("ldc", g_.dma_start(out=self.bqk[:, 8:16],
                     in_=self.bk_a.rearrange("(g p) -> p g", p=128),
                     allow_slow_non_contiguous=True), 16)
            self.inc("ldc", g_.dma_start(out=self.bvv[:], in_=self.bv_a.unsqueeze(0)), 16)
            for dc in range(8):
                self.inc("ldc", g_.dma_start(out=self.wsd[:, dc, 0:40],
                         in_=self.ws_f[dc * 128:(dc + 1) * 128, :]), 16)
                self.inc("ldc", g_.dma_start(out=self.wsd[:, dc, 40:80],
                         in_=self.wd_f[dc * 128:(dc + 1) * 128, :]), 16)
            self.inc("ggen", g_.drain())

        @blk.vector
        def _(v):
            self.w(v, "ggen", 1)
            tt = v.tensor_tensor(out=self.ident[:], in0=self.iota_scr[:, 0:128],
                                 in1=self.iota_scr[:, 128:256], op=ALU.is_equal)
            self.inc("vgen", tt)

        @blk.tensor
        def _(t):
            self.w(t, "vgen", self.n("vgen"))

    def ln_phase(self, src_hbm):
        blk, s = self.block, self.seq
        lxb = [self.n(f"lx{k}") for k in range(4)]
        sg0 = self.n("sgen")
        vg0 = self.n("vgen")
        tg0 = self.n("tgp")
        vr0 = self.n("vres")

        if src_hbm and "lntm" in self.dbg_t and self.seq == 0:
            @blk.gpsimd
            def _(g_):
                self.w(g_, "sgen", sg0 + 48)
                self.inc("st", g_.dma_start(out=self.dbg_t["lntm"], in_=self.ln_tm), 16)
                g_.wait_ge(self.sm["st"][0], self.n("st"))

        if src_hbm:
            @blk.sync
            def _(sy):
                for nt in range(NT):
                    if nt >= 4:
                        self.w(sy, "sgen", sg0 + 3 * (nt - 4) + 3)
                    self.inc(f"lx{nt % 4}", sy.dma_start(out=self.xstage[:, nt % 4, :],
                             in_=self.xc[s, nt * 128:(nt + 1) * 128, :]), 16)

        @blk.scalar
        def _(s_):
            for nt in range(NT):
                xin = self.xstage[:, nt % 4, :] if src_hbm else self.X[:, nt, :]
                if src_hbm:
                    self.w(s_, f"lx{nt % 4}", lxb[nt % 4] + 16 * (nt // 4 + 1))
                a = s_.activation(out=self.sq[:], in_=xin, func=AF.Square,
                                  accum_out=self.stat[:, 1:2])
                self.inc("sgen", a)
                self.w(s_, "vgen", vg0 + 2 * nt + 1)
                a = s_.activation(out=self.stat[:, 5:6], in_=self.stat[:, 3:4],
                                  func=AF.Sqrt)
                self.inc("sgen", a)
                self.w(s_, "vgen", vg0 + 2 * nt + 2)
                if nt >= 4:
                    self.w(s_, "tgp", tg0 + ((nt - 4) // 4 + 1) * 8)
                s_.activation(out=self.ln_tm[:, nt % 4, :], in_=xin, func=AF.Identity,
                              bias=self.stat[:, 7:8], scale=self.stat[:, 6:7])
                dr = s_.drain()
                self.inc("sgen", dr)

        @blk.vector
        def _(v):
            for nt in range(NT):
                xin = self.xstage[:, nt % 4, :] if src_hbm else self.X[:, nt, :]
                if src_hbm:
                    self.w(v, f"lx{nt % 4}", lxb[nt % 4] + 16 * (nt // 4 + 1))
                v.reduce_sum(self.stat[:, 0:1], xin, axis=AX.X)
                self.w(v, "sgen", sg0 + 3 * nt + 1)
                v.drain()
                v.tensor_scalar(out=self.stat[:, 2:3], in0=self.stat[:, 0:1],
                                scalar1=1.0 / D, scalar2=None, op0=ALU.mult)
                v.tensor_scalar(out=self.stat[:, 3:4], in0=self.stat[:, 1:2],
                                scalar1=1.0 / D, scalar2=None, op0=ALU.mult)
                v.drain()
                v.tensor_tensor(out=self.stat[:, 4:5], in0=self.stat[:, 2:3],
                                in1=self.stat[:, 2:3], op=ALU.mult)
                v.drain()
                v.tensor_tensor(out=self.stat[:, 3:4], in0=self.stat[:, 3:4],
                                 in1=self.stat[:, 4:5], op=ALU.subtract)
                v.drain()
                tt = v.tensor_scalar(out=self.stat[:, 3:4], in0=self.stat[:, 3:4],
                                     scalar1=float(EPS), scalar2=None, op0=ALU.add)
                v.drain()
                self.inc("vgen", tt)
                self.w(v, "sgen", sg0 + 3 * nt + 2)
                v.reciprocal(out=self.stat[:, 6:7], in_=self.stat[:, 5:6])
                v.drain()
                tt = v.scalar_tensor_tensor(out=self.stat[:, 7:8], in0=self.stat[:, 2:3],
                                            scalar=-1.0, in1=self.stat[:, 6:7],
                                            op0=ALU.mult, op1=ALU.mult)
                v.drain()
                self.inc("vgen", tt)
                if "statd" in self.dbg_t and self.seq == 0 and src_hbm:
                    v.drain()
                    v.tensor_copy(out=self.statd[:, nt, :], in_=self.stat[:])

                if nt % 4 == 3:
                    tg = nt // 4
                    for dc in range(8):
                        self.w(v, "tgp", tg0 + tg * 8 + dc + 1)
                        base = (dc % 4) * 2048 + (dc // 4) * 512
                        cp = v.tensor_copy(
                            out=self.lnfm[:, dc, (4 * tg) * 128:(4 * tg + 4) * 128],
                            in_=self.PSB[:, base:base + 512])
                        self.inc("vres", cp)

        @blk.tensor
        def _(t):
            for tg in range(NT // 4):
                self.w(t, "sgen", sg0 + 3 * (4 * tg + 4))
                for dc in range(8):
                    gidx = tg * 8 + dc
                    if gidx >= 4:
                        self.w(t, "vres", vr0 + gidx - 3)
                    base = (dc % 4) * 2048 + (dc // 4) * 512
                    for j in range(4):
                        t.transpose(self.PSB[:, base + j * 128: base + (j + 1) * 128],
                                    self.ln_tm[:, j, dc * 128:(dc + 1) * 128],
                                    self.ident[:])
                    self.inc("tgp", t.drain())

    def a2_proj(self):
        blk = self.block
        if getattr(self, "dbg_ln_reload", False) and self.seq == 0:
            @blk.gpsimd
            def _(g_):
                for dc in range(8):
                    self.inc("ldw", g_.dma_start(out=self.lnfm[:, dc, :],
                             in_=self.dbg_ln_in[:, dc, :]), 16)
                self.inc("ggen", g_.nop())

            @blk.tensor
            def _(t):
                self.w(t, "ldw")
                self.w(t, "ggen")
        lwkb = [self.n(f"lwk{k}") for k in range(4)]
        lwvb = [self.n(f"lwv{k}") for k in range(2)]
        tg0 = self.n("tgp")
        se0 = self.n("seva")

        @blk.gpsimd
        def _(g_):
            for it, wsrc in enumerate((self.wk_a, self.wq_a)):
                for g in range(G):
                    gi = it * G + g
                    if gi >= 4:
                        self.w(g_, "tgp", tg0 + (gi - 3) * 2)
                    for dc in range(8):
                        self.inc(f"lwk{gi % 4}", g_.dma_start(
                            out=self.wk_ring[:, gi % 4, dc, :],
                            in_=wsrc[dc * 128:(dc + 1) * 128, g * 128:(g + 1) * 128]), 16)
            for ntp in range(8):
                for dc in range(8):
                    idx = ntp * 8 + dc
                    if idx >= 2:
                        i2 = idx - 2
                        self.w(g_, "tgp", tg0 + 32 + (i2 // 8) * 9 + (i2 % 8) + 1)
                    self.inc(f"lwv{idx % 2}", g_.dma_start(
                        out=self.wv_ring[:, idx % 2, :],
                        in_=self.wv_a[dc * 128:(dc + 1) * 128, :]), 16)

        @blk.tensor
        def _(t):
            self.w(t, "ldc")
            for it, dst_bias in enumerate(((self.kfm, 8), (self.qfm, 0))):
                for g in range(G):
                    gi = it * G + g
                    self.w(t, f"lwk{gi % 4}", lwkb[gi % 4] + 16 * 8 * (gi // 4 + 1))
                    for np_ in range(2):
                        m = gi * 2 + np_
                        if m >= 2:
                            self.w(t, "seva", se0 + (m - 1) * 2)
                        for dc in range(8):
                            for nbl in range(2):
                                nb = np_ * 2 + nbl
                                bank = (np_ * 2 + nbl) * 512
                                mm = t.matmul(self.PS[:, bank:bank + 512],
                                              self.wk_ring[:, gi % 4, dc, :],
                                              self.lnfm[:, dc, nb * 512:(nb + 1) * 512],
                                              start=(dc == 0), stop=(dc == 7))
                        self.inc("tgp", mm)
            for ntp in range(8):
                for dc in range(8):
                    idx = ntp * 8 + dc
                    self.w(t, f"lwv{idx % 2}", lwvb[idx % 2] + 16 * (idx // 2 + 1))
                    if dc == 0:
                        if ntp >= 2:
                            self.w(t, "seva", se0 + 64 + (ntp - 1) * 2)
                        else:
                            self.w(t, "seva", se0 + 64)
                    for t01 in range(2):
                        nt = ntp * 2 + t01
                        rbase = (ntp % 2) * 2048 + t01 * 1024
                        for half in range(2):
                            mm = t.matmul(self.PS[:, rbase + half * 512: rbase + (half + 1) * 512],
                                          self.lnfm[:, dc, nt * 128:(nt + 1) * 128],
                                          self.wv_ring[:, idx % 2, half * 512:(half + 1) * 512],
                                          start=(dc == 0), stop=False)
                    self.inc("tgp", mm)
                for t01 in range(2):
                    rbase = (ntp % 2) * 2048 + t01 * 1024
                    for half in range(2):
                        mm = t.matmul(self.PS[:, rbase + half * 512: rbase + (half + 1) * 512],
                                      self.ones1[:],
                                      self.bvv[0:1, half * 512:(half + 1) * 512],
                                      start=False, stop=True)
                self.inc("tgp", mm)

        @blk.vector
        def _(v):
            self.w(v, "ldc")
            for it, (dst, boff) in enumerate(((self.kfm, 8), (self.qfm, 0))):
                for g in range(G):
                    gi = it * G + g
                    for np_ in range(2):
                        m = gi * 2 + np_
                        self.w(v, "tgp", tg0 + m + 1)
                        for nbl in range(2):
                            nb = np_ * 2 + nbl
                            bank = (np_ * 2 + nbl) * 512
                            a = v.tensor_scalar(out=dst[:, g, nb * 512:(nb + 1) * 512],
                                                in0=self.PS[:, bank:bank + 512],
                                                scalar1=self.bqk[:, boff + g:boff + g + 1],
                                                scalar2=None, op0=ALU.add)
                            self.inc("seva", a)
            for ntp in range(8):
                self.w(v, "tgp", tg0 + 32 + ntp * 9 + 9)
                for t01 in range(2):
                    nt = ntp * 2 + t01
                    rbase = (ntp % 2) * 2048 + t01 * 1024
                    a = v.tensor_copy(out=self.vtm[:, nt, :],
                                      in_=self.PS[:, rbase:rbase + 1024])
                    self.inc("seva", a)

    def a3_attn(self):
        blk, s = self.block, self.seq
        if getattr(self, "dbg_qkv_reload", False) and s == 0:
            import os as _os
            _which = _os.environ.get("RELOAD", "kqv")

            @blk.gpsimd
            def _(g_):
                for dc in range(8):
                    if "k" in _which:
                        self.inc("ldw", g_.dma_start(out=self.kfm[:, dc, :],
                                 in_=self.dbg_kfm_in[:, dc, :]), 16)
                    if "q" in _which:
                        self.inc("ldw", g_.dma_start(out=self.qfm[:, dc, :],
                                 in_=self.dbg_qfm_in[:, dc, :]), 16)
                if "v" in _which:
                    for ti in range(NT):
                        self.inc("ldw", g_.dma_start(out=self.vtm[:, ti, :],
                                 in_=self.dbg_vtm_in[:, ti, :]), 16)
                self.inc("ggen", g_.nop())

            @blk.tensor
            def _(t):
                self.w(t, "ldw")
                self.w(t, "ggen")
        lxb = [self.n(f"lx{k}") for k in range(4)]
        tsc0 = self.n("tsc")
        tcx0 = self.n("tcx")
        sx0 = self.n("sexp")
        vs0 = self.n("vsm")
        vr0 = self.n("vres")

        @blk.sync
        def _(sy):
            for nt in range(NT):
                if nt >= 4:
                    self.w(sy, "vres", vr0 + nt - 3)
                self.inc(f"lx{nt % 4}", sy.dma_start(out=self.xstage[:, nt % 4, :],
                         in_=self.xc[s, nt * 128:(nt + 1) * 128, :]), 16)

        if "kfm3" in self.dbg_t and self.seq == 0:
            @blk.gpsimd
            def _(g_):
                self.w(g_, "vres", vr0 + 2)
                self.inc("st", g_.dma_start(out=self.dbg_t["kfm3"], in_=self.kfm), 16)

        morder = list(range(16))
        if getattr(self, "rev_m", False):
            morder = morder[::-1]

        gg0 = self.n("ggen")

        def ctx_mm(t, i, mi_, m, g, j):
            return t.matmul(self.PS[:, 2048 + j * 1024 + g * 128:
                                    2048 + j * 1024 + (g + 1) * 128],
                            self.pe_a3(i)[:, g * 256 + j * 128:
                                          g * 256 + (j + 1) * 128],
                            self.vtm[:, m, g * 128:(g + 1) * 128],
                            start=(mi_ == 0 and g % 4 == 0),
                            stop=(mi_ == 15 and g % 4 == 3))

        def emit_ctx(t, ip):
            nbp, mip = ip // 16, ip % 16
            mp = morder[mip]
            if mip == 0 and nbp > 0:
                self.w(t, "vres", vr0 + nbp * 2)
            self.w(t, "vsm", vs0 + ip + 1)
            for j in range(2):
                for g in range(6):
                    mm = ctx_mm(t, ip, mip, mp, g, j)
            self.w(t, "ggen", gg0 + ip + 1)
            for j in range(2):
                for g in range(6, G):
                    mm = ctx_mm(t, ip, mip, mp, g, j)
            self.inc("tcx", mm)

        @blk.tensor
        def _(t):
            for nb in range(NB):
                for mi_, m in enumerate(morder):
                    i = nb * 16 + mi_
                    if i > 0:
                        self.w(t, "sexp", sx0 + 2 * i - 1)
                    for g in range(4):
                        mm = t.matmul(self.PS[:, g * 256:(g + 1) * 256],
                                      self.kfm[:, g, m * 128:(m + 1) * 128],
                                      self.qfm[:, g, nb * 256:(nb + 1) * 256],
                                      start=True, stop=True)
                    self.inc("tsc", mm)
                    if i > 0:
                        self.w(t, "sexp", sx0 + 2 * i)
                    for g in range(4, G):
                        mm = t.matmul(self.PS[:, g * 256:(g + 1) * 256],
                                      self.kfm[:, g, m * 128:(m + 1) * 128],
                                      self.qfm[:, g, nb * 256:(nb + 1) * 256],
                                      start=True, stop=True)
                    self.inc("tsc", mm)
                    if i > 0:
                        emit_ctx(t, i - 1)
            emit_ctx(t, NB * 16 - 1)

        vden0 = self.n("vden")
        sr0 = self.n("srec")

        @blk.scalar
        def _(s_):
            def recip_pair(j):
                self.w(s_, "vden", vden0 + j + 1)
                if j >= 2:
                    self.w(s_, "ggen", gg0 + j - 1)
                s_.activation(out=self.lnd, in_=self.d32s(j), func=AF.Ln)
                s_.drain()
                a = s_.activation(out=self.r_buf[:, j % 2, :], in_=self.lnd,
                                  func=AF.Exp, scale=-1.0)
                self.inc("srec", a)

            for nb in range(NB):
                for m in range(16):
                    i = nb * 16 + m
                    if i >= 4:
                        self.w(s_, "tcx", tcx0 + i - 3)
                    self.w(s_, "tsc", tsc0 + 2 * i + 1)
                    a = s_.activation(out=self.pe_a3(i)[:, 0:1024],
                                      in_=self.PS[:, 0:1024], func=AF.Exp,
                                      scale=1.0 / 32.0)
                    self.inc("sexp", a)
                    self.w(s_, "tsc", tsc0 + 2 * i + 2)
                    a = s_.activation(out=self.pe_a3(i)[:, 1024:2048],
                                      in_=self.PS[:, 1024:2048], func=AF.Exp,
                                      scale=1.0 / 32.0)
                    self.inc("sexp", a)
                    if i > 0:
                        recip_pair(i - 1)
            recip_pair(NB * 16 - 1)

        @blk.vector
        def _(v):
            def muls(j):
                self.w(v, "srec", sr0 + j + 1)
                pe_ = self.pe_a3(j)
                for g in range(6):
                    tt = v.tensor_tensor(out=pe_[:, g * 256:(g + 1) * 256],
                                         in0=pe_[:, g * 256:(g + 1) * 256],
                                         in1=self.r_buf[:, j % 2, :], op=ALU.mult)
                self.inc("vsm", tt)

            for nb in range(NB):
                for m in range(16):
                    i = nb * 16 + m
                    pe = self.pe_a3(i)
                    t0, t1, t2, t3 = (self.pe_scr[:, k * 256:(k + 1) * 256] for k in range(4))
                    self.w(v, "sexp", sx0 + 2 * i + 1)
                    v.tensor_tensor(out=t0, in0=pe[:, 0:256], in1=pe[:, 256:512], op=ALU.add)
                    v.tensor_tensor(out=t1, in0=pe[:, 512:768], in1=pe[:, 768:1024], op=ALU.add)
                    self.w(v, "sexp", sx0 + 2 * i + 2)
                    v.tensor_tensor(out=t2, in0=pe[:, 1024:1280], in1=pe[:, 1280:1536], op=ALU.add)
                    v.tensor_tensor(out=t3, in0=pe[:, 1536:1792], in1=pe[:, 1792:2048], op=ALU.add)
                    v.drain()
                    v.tensor_tensor(out=t0, in0=t0, in1=t1, op=ALU.add)
                    v.tensor_tensor(out=t2, in0=t2, in1=t3, op=ALU.add)
                    v.drain()
                    if i > 1:
                        self.w(v, "srec", sr0 + i - 1)
                    tt = v.tensor_tensor(out=self.d32s(i), in0=t0, in1=t2, op=ALU.add)
                    self.inc("vden", tt)
                    if m > 0:
                        muls(i - 1)
                muls(nb * 16 + 15)
                self.w(v, "tcx", tcx0 + nb * 16 + 16)
                for j in range(2):
                    nt = nb * 2 + j
                    self.w(v, f"lx{nt % 4}", lxb[nt % 4] + 16 * (nt // 4 + 1))
                    tt = v.tensor_tensor(out=self.X[:, nt, :],
                                         in0=self.PS[:, 2048 + j * 1024:2048 + (j + 1) * 1024],
                                         in1=self.xstage[:, nt % 4, :], op=ALU.add)
                    self.inc("vres", tt)

        @blk.gpsimd
        def _(g_):
            for nb in range(NB):
                for m in range(16):
                    i = nb * 16 + m
                    pe = self.pe_a3(i)
                    self.w(g_, "vsm", vs0 + i + 1)
                    for g in range(6, G):
                        tt = g_.tensor_tensor(out=pe[:, g * 256:(g + 1) * 256],
                                              in0=pe[:, g * 256:(g + 1) * 256],
                                              in1=self.r_buf[:, i % 2, :], op=ALU.mult)
                    self.inc("ggen", tt)

    def b1_router(self):
        blk = self.block
        tg0 = self.n("tgp")
        sg0 = self.n("sgen")

        @blk.tensor
        def _(t):
            self.w(t, "ldc")
            for nt in range(NT):
                if nt >= 2:
                    self.w(t, "sgen", sg0 + nt - 1)
                r = (nt % 2) * 512
                for dc in range(8):
                    mm = t.matmul(self.PS[:, r:r + 80],
                                  self.lnfm[:, dc, nt * 128:(nt + 1) * 128],
                                  self.wsd[:, dc, :],
                                  start=(dc == 0), stop=(dc == 7))
                self.inc("tgp", mm)

        @blk.scalar
        def _(s_):
            for nt in range(NT):
                self.w(s_, "tgp", tg0 + nt + 1)
                r = (nt % 2) * 512
                a = s_.activation(out=self.ssd[:, nt, :], in_=self.PS[:, r:r + 80],
                                  func=AF.Sigmoid)
                self.inc("sgen", a)

        @blk.vector
        def _(v):
            self.w(v, "sgen", sg0 + NT)
            for gi in range(2):
                gsl = self.ssd[:, :, gi * 40:(gi + 1) * 40].rearrange(
                    "p t (h e) -> p t h e", e=E)
                for ei in range(E):
                    jj = 0
                    for ej in range(E):
                        if ej == ei:
                            continue
                        op = ALU.is_ge if ej < ei else ALU.is_gt
                        v.tensor_tensor(out=self.cmp[:, jj, :, :].unsqueeze(3),
                                        in0=gsl[:, :, :, ej:ej + 1],
                                        in1=gsl[:, :, :, ei:ei + 1], op=op)
                        jj += 1
                    v.drain()
                    v.tensor_tensor(out=self.cnt[:], in0=self.cmp[:, 0, :, :],
                                    in1=self.cmp[:, 1, :, :], op=ALU.add)
                    v.drain()
                    v.tensor_tensor(out=self.cnt[:], in0=self.cnt[:],
                                    in1=self.cmp[:, 2, :, :], op=ALU.add)
                    v.drain()
                    v.tensor_tensor(out=self.cnt[:], in0=self.cnt[:],
                                    in1=self.cmp[:, 3, :, :], op=ALU.add)
                    v.drain()
                    mview = self.msk[:, gi, :, :].rearrange(
                        "p t (h e) -> p t h e", e=E)[:, :, :, ei:ei + 1]
                    v.tensor_scalar(out=mview, in0=self.cnt[:].unsqueeze(3),
                                    scalar1=2.5, scalar2=None, op0=ALU.is_le)
                    v.drain()
            v.tensor_tensor(out=self.ssg[:], in0=self.ssd[:, :, 0:40],
                            in1=self.msk[:, 0, :, :], op=ALU.mult)
            v.tensor_tensor(out=self.sdg[:], in0=self.ssd[:, :, 40:80],
                            in1=self.msk[:, 1, :, :], op=ALU.mult)
            vv = self.vtmf.rearrange("p t (h c) -> p t h c", c=66)
            tt = v.memset(vv[:, :, :, 64:65], 1.0)
            self.inc("vgen", tt)

    def b2_projgate(self):
        blk = self.block
        lwsb = [self.n(f"lws{k}") for k in range(2)]
        tg0 = self.n("tgp")
        vg0 = self.n("vgen")

        blocks = []
        for h in range(H):
            blocks.append((self.wq_f, h * 320, self.sdg, self.qh, h * 64, 64))
        for h in range(H):
            blocks.append((self.wkv_f, h * 320, self.ssg, self.kh, h * 64, 64))
        for h in range(H):
            blocks.append((self.wkv_f, 2560 + h * 320, self.ssg, self.vtmf, h * 66, 66))

        @blk.gpsimd
        def _(g_):
            for bi_, (wsrc, coff, _, _, _, _) in enumerate(blocks):
                if bi_ >= 2:
                    self.w(g_, "vgen", vg0 + (bi_ - 2) * 4 + 4)
                for dc in range(8):
                    self.inc(f"lws{bi_ % 2}", g_.dma_start(
                        out=self.wslc[:, bi_ % 2, dc, :],
                        in_=wsrc[dc * 128:(dc + 1) * 128, coff:coff + 320]), 16)

        @blk.tensor
        def _(t):
            for bi_ in range(24):
                self.w(t, f"lws{bi_ % 2}", lwsb[bi_ % 2] + 16 * 8 * (bi_ // 2 + 1))
                for tg in range(4):
                    idx = bi_ * 4 + tg
                    if idx >= 2:
                        self.w(t, "vgen", vg0 + idx - 1)
                    r = (tg % 2) * 2048
                    for t_ in range(4):
                        nt = tg * 4 + t_
                        for dc in range(8):
                            mm = t.matmul(self.PS[:, r + t_ * 512: r + t_ * 512 + 320],
                                          self.lnfm[:, dc, nt * 128:(nt + 1) * 128],
                                          self.wslc[:, bi_ % 2, dc, :],
                                          start=(dc == 0), stop=(dc == 7))
                    self.inc("tgp", mm)

        @blk.vector
        def _(v):
            for bi_, (wsrc, coff, gates, dst, dcol, dstride) in enumerate(blocks):
                for tg in range(4):
                    idx = bi_ * 4 + tg
                    self.w(v, "tgp", tg0 + idx + 1)
                    r = (tg % 2) * 2048
                    psin = self.PS[:, r:r + 2048].rearrange(
                        "p (t c) -> p t c", c=512)[:, :, 0:320].rearrange(
                        "p t (e d) -> p t e d", d=64)
                    gin = gates[:, tg * 4:(tg + 1) * 4,
                                (coff % 2560) // 320 * 5:(coff % 2560) // 320 * 5 + 5]
                    gin = gin.unsqueeze(3).to_broadcast([128, 4, 5, 64])
                    gb = self.gated[:, idx % 2, :, :, :]
                    tt = v.tensor_tensor(out=gb, in0=psin, in1=gin, op=ALU.mult)
                    self.inc("vgen", tt)
                    v.drain()
                    dv = dst[:, tg * 4:(tg + 1) * 4, dcol:dcol + 64]
                    with self.nc.allow_low_precision(reason="bf16 gated sum"):
                        v.reduce_sum(dv.unsqueeze(3),
                                     gb.rearrange("p t e c -> p t c e"), axis=AX.X)

    def b3_transpose(self):
        blk = self.block
        tg0 = self.n("tgp")
        vg0 = self.n("vgen")

        @blk.tensor
        def _(t):
            for ti, (src, dst) in enumerate(((self.qh, self.qfmb), (self.kh, self.kfmb))):
                for ntg in range(4):
                    for p in range(4):
                        idx = (ti * 4 + ntg) * 4 + p
                        if idx >= 2:
                            self.w(t, "vgen", vg0 + idx - 1)
                        base = (p % 2) * 2048
                        for tt_ in range(4):
                            nt = ntg * 4 + tt_
                            tp = t.transpose(self.PSB[:, base + tt_ * 128: base + (tt_ + 1) * 128],
                                             src[:, nt, p * 128:(p + 1) * 128],
                                             self.ident[:])
                        self.inc("tgp", tp)

        @blk.vector
        def _(v):
            for ti, (src, dst) in enumerate(((self.qh, self.qfmb), (self.kh, self.kfmb))):
                for ntg in range(4):
                    for p in range(4):
                        idx = (ti * 4 + ntg) * 4 + p
                        self.w(v, "tgp", tg0 + idx + 1)
                        base = (p % 2) * 2048
                        cp = v.tensor_copy(out=dst[:, p, ntg * 512:(ntg + 1) * 512],
                                           in_=self.PSB[:, base:base + 512])
                        self.inc("vgen", cp)

    def b4_attn(self):
        blk, s = self.block, self.seq
        ldw0 = self.n("ldw")

        @blk.gpsimd
        def _(g_):
            for e_ in range(2):
                self.inc("ldw", g_.dma_start(out=self.woc[0][e_ * 64:(e_ + 1) * 64, :],
                         in_=self.wo_f[:, e_ * D:(e_ + 1) * D]), 16)
                self.inc("ldw", g_.dma_start(out=self.woc[1][e_ * 64:(e_ + 1) * 64, :],
                         in_=self.wo_f[:, (2 + e_) * D:(3 + e_) * D]), 16)
            self.inc("ldw", g_.dma_start(out=self.woc[2][0:64, :], in_=self.wo_f[:, 4 * D:5 * D]), 16)

        tsc0 = self.n("tsc")
        tcx0 = self.n("tcx")
        tg0 = self.n("tgp")
        sx0 = self.n("sexp")
        se0 = self.n("seva")
        vs0 = self.n("vsm")
        vd0 = self.n("vdr")
        vy0 = self.n("vyh")
        sob = [self.n(f"so{k}") for k in range(2)]
        ztb = [self.n("zt0"), self.n("zt1")]

        OB, YB = 2048, 3584
        ocol = lambda q: OB + (q // 7) * 512 + (q % 7) * 65

        def o_group(t, i):
            nb_, jc_ = i // 16, i % 16
            if jc_ == 0 and nb_ > 0:
                self.w(t, "seva", se0 + nb_ * 2)
                self.w(t, "vdr", vd0 + nb_ * 2)
            for j in range(2):
                for h in range(H):
                    q = j * 8 + h
                    mm = t.matmul(self.PS[:, ocol(q):ocol(q) + 65],
                                  self.pe_buf[:, i % 2, hpos(h) + j * 128:
                                              hpos(h) + (j + 1) * 128],
                                  self.vtmf[:, jc_, h * 66:h * 66 + 65],
                                  start=(jc_ == 0 and q % 7 == 0),
                                  stop=(jc_ == 15 and (q % 7 == 6 or q == 15)))
            self.inc("tcx", mm)

        def y_piece(t, nb_, pidx):
            j, half = pidx // 2, pidx % 2
            nt = nb_ * 2 + j
            self.w(t, f"zt{nt % 2}", ztb[nt % 2] + 48 * (nt // 2 + 1))
            self.w(t, "vyh", vy0 + 2 * nt + half)
            if nb_ == 0 and pidx == 0:
                self.w(t, "ldw", ldw0 + 16 * 5)
            for cch in range(3):
                if cch < 2:
                    lhs = self.zt[:, nt % 2, cch * 128:(cch + 1) * 128]
                    rhsw = self.woc[cch][:, half * 512:(half + 1) * 512]
                else:
                    lhs = self.zt[0:64, nt % 2, 256:384]
                    rhsw = self.woc[2][0:64, half * 512:(half + 1) * 512]
                mm = t.matmul(self.PS[:, YB:YB + 512], lhs, rhsw,
                              start=(cch == 0), stop=(cch == 2))
            self.inc("tgp", mm)

        @blk.tensor
        def _(t):
            for nb in range(NB):
                for jc in range(16):
                    i = nb * 16 + jc
                    self.w(t, "sexp", sx0 + i)
                    for p in range(4):
                        h0, h1 = 2 * p, 2 * p + 1
                        t.matmul(self.PS[:, hpos(h0):hpos(h0) + 256],
                                 self.kfmb[0:64, p, jc * 128:(jc + 1) * 128],
                                 self.qfmb[0:64, p, nb * 256:(nb + 1) * 256],
                                 start=True, stop=True)
                        mm = t.matmul(self.PS[:, hpos(h1):hpos(h1) + 256],
                                      self.kfmb[64:128, p, jc * 128:(jc + 1) * 128],
                                      self.qfmb[64:128, p, nb * 256:(nb + 1) * 256],
                                      start=True, stop=True, tile_position=(64, 0))
                    self.inc("tsc", mm)
                    if nb > 0 and jc in (5, 7, 9, 11):
                        y_piece(t, nb - 1, (jc - 5) // 2)
                    if jc > 0:
                        o_group(t, i - 1)
                self.w(t, "sexp", sx0 + nb * 16 + 16)
                o_group(t, nb * 16 + 15)
            for pidx in range(4):
                y_piece(t, NB - 1, pidx)

        @blk.scalar
        def _(s_):
            for nb in range(NB):
                for jc in range(16):
                    i = nb * 16 + jc
                    self.w(s_, "tsc", tsc0 + i + 1)
                    if i >= 2:
                        self.w(s_, "tcx", tcx0 + i - 1)
                    a = s_.activation(out=self.pe_buf[:, i % 2, :],
                                      in_=self.PS[:, 0:2048], func=AF.Exp,
                                      scale=0.125)
                    self.inc("sexp", a)
                self.w(s_, "tcx", tcx0 + nb * 16 + 16)
                for j in range(2):
                    nt = nb * 2 + j
                    if nt >= 2:
                        self.w(s_, "vsm", vs0 + nt - 1)
                    if j == 0:
                        segs = (((2048, 7), 0), ((2560, 1), 448))
                    else:
                        segs = (((2625, 6), 0), ((3072, 2), 384))
                    for (pc, nq), oc in segs:
                        pin = self.PS[:, pc:pc + nq * 65].rearrange(
                            "p (q c) -> p q c", c=65)[:, :, 0:64]
                        a = s_.activation(
                            out=self.o_sb[:, nt % 2, oc:oc + nq * 64].rearrange(
                                "p (q c) -> p q c", c=64),
                            in_=pin, func=AF.Copy)
                    self.inc("seva", a)

        @blk.vector
        def _(v):
            v.memset(self.z_b[:, :, 320:384], 0.0)
            for nb in range(NB + 1):
                if nb > 0:
                    for j in range(2):
                        nt = (nb - 1) * 2 + j
                        for half in range(2):
                            self.w(v, "tgp", tg0 + 2 * nt + half + 1)
                            if half == 0 and nt >= 2:
                                self.w(v, f"so{nt % 2}",
                                       sob[nt % 2] + 16 * (nt // 2))
                            tt = v.tensor_tensor(
                                out=self.out_stg[:, nt % 2, half * 512:(half + 1) * 512],
                                in0=self.PS[:, YB:YB + 512],
                                in1=self.X[:, nt, half * 512:(half + 1) * 512],
                                op=ALU.add)
                            self.inc("vyh", tt)
                if nb == NB:
                    break
                self.w(v, "tcx", tcx0 + nb * 16 + 16)
                for j in range(2):
                    nt = nb * 2 + j
                    if j == 0:
                        segs = (((2048 + 64, 7), 0), ((2624, 1), 7))
                    else:
                        segs = (((2625 + 64, 6), 0), ((3136, 2), 6))
                    for (pc, nq), rc in segs:
                        if nq > 1:
                            pv = self.PS[:, pc:pc + nq * 65].rearrange(
                                "p (q c) -> p q c", c=65)[:, :, 0:1]
                            rr = v.reciprocal(
                                out=self.r8[:, nt % 2, rc:rc + nq].unsqueeze(2),
                                in_=pv)
                        else:
                            rr = v.reciprocal(out=self.r8[:, nt % 2, rc:rc + 1],
                                              in_=self.PS[:, pc:pc + 1])
                    self.inc("vdr", rr)
                v.drain()
                for j in range(2):
                    nt = nb * 2 + j
                    for h in range(H):
                        v.tensor_scalar(out=self.mdr[:, nt % 2, h * 5:(h + 1) * 5],
                                        in0=self.msk[:, 1, nt, h * 5:(h + 1) * 5],
                                        scalar1=self.r8[:, nt % 2, h:h + 1],
                                        scalar2=None, op0=ALU.mult)
                    v.drain()
                    self.w(v, "seva", se0 + nt + 1)
                    ob = self.o_sb[:, nt % 2, :].rearrange(
                        "p (h c) -> p h c", c=64).unsqueeze(2).to_broadcast(
                        [128, H, E, 64])
                    mb = self.mdr[:, nt % 2, :].rearrange(
                        "p (h e) -> p h e", e=E).unsqueeze(3).to_broadcast(
                        [128, H, E, 64])
                    v.tensor_tensor(out=self.z_acc.rearrange(
                        "p h (e c) -> p h e c", c=64), in0=ob, in1=mb, op=ALU.mult)
                    v.drain()
                    if nt >= 2:
                        self.w(v, f"zt{nt % 2}", ztb[nt % 2] + 48 * (nt // 2))
                    with self.nc.allow_low_precision(reason="fp32 accum, bf16 out"):
                        cp = v.reduce_sum(self.z_b[:, nt % 2, 0:320].unsqueeze(2),
                                          self.z_acc.rearrange("p h c -> p c h"),
                                          axis=AX.X)
                    self.inc("vsm", cp)

        @blk.sync
        def _(sy):
            for nb in range(NB):
                for j in range(2):
                    nt = nb * 2 + j
                    self.w(sy, "vsm", vs0 + nt + 1)
                    if nt >= 2:
                        self.w(sy, "tgp", tg0 + 2 * nt - 2)
                    for cch in range(3):
                        self.inc(f"zt{nt % 2}", sy.dma_start(
                            out=self.zt[:, nt % 2, cch * 128:(cch + 1) * 128],
                            in_=self.z_b[:, nt % 2, cch * 128:(cch + 1) * 128],
                            transpose=True), 16)
                for j in range(2):
                    nt = nb * 2 + j
                    self.w(sy, "vyh", vy0 + 2 * nt + 2)
                    self.inc(f"so{nt % 2}", sy.dma_start(
                             out=self.yout[s, nt * 128:(nt + 1) * 128, :],
                             in_=self.out_stg[:, nt % 2, :]), 16)


def build_nc(dbg=(), stop_after=None, nseq=BPC):
    nc = bass.Bass("TRN2", target_bir_lowering=False)
    b = Builder(nc, dbg)
    b.stop_after = stop_after
    b.nseq = nseq
    b.build()
    return nc, b.dbg_t


_NC_CACHE = {}


def kernel(**inputs):
    key = "main"
    if key not in _NC_CACHE:
        _NC_CACHE[key] = build_nc()
    nc, _ = _NC_CACHE[key]
    x = np.ascontiguousarray(inputs["L0_attn_hidden_states"])
    wkeys = ["L0_attn_Wq", "L0_attn_bq", "L0_attn_Wk", "L0_attn_bk", "L0_attn_Wv",
             "L0_attn_bv", "L0_ffn_Wq", "L0_ffn_Wkv", "L0_ffn_Ws", "L0_ffn_Wd",
             "L0_ffn_Wo"]
    in_maps = []
    for c in range(NCORE):
        m = {k: np.ascontiguousarray(inputs[k]) for k in wkeys}
        m["xc"] = x[c * BPC:(c + 1) * BPC]
        in_maps.append(m)
    res = run_bass_kernel_spmd(nc, in_maps, core_ids=list(range(NCORE)))
    return np.concatenate([r["yout"] for r in res.results], axis=0)


# version 14
# speedup vs baseline: 1.1054x; 1.1031x over previous
import numpy as np
from contextlib import ExitStack

import concourse.bass as bass
import concourse.mybir as mybir
from concourse.bass_utils import run_bass_kernel_spmd

F32 = mybir.dt.float32
BF16 = mybir.dt.bfloat16
AF = mybir.ActivationFunctionType
ALU = mybir.AluOpType
AX = mybir.AxisListType

B, S, D = 16, 2048, 1024
NCORE = 8
BPC = B // NCORE
NT = S // 128
G = 8
H, E, DH = 8, 5, 64
NB = S // 256
EPS = 1e-5


def hpos(h):
    return (h % 2) * 1024 + (h // 2) * 256


class Carve:
    def __init__(self, t):
        self.t = t
        self.off = 0

    def take(self, shape, dt=BF16):
        n = 1
        for d_ in shape:
            n *= d_
        w = n * (2 if dt is F32 else 1)
        self.off = (self.off + 15) // 16 * 16
        ap = self.t[:, self.off:self.off + w]
        self.off += w
        if dt is F32:
            ap = ap.bitcast(F32)
        if len(shape) == 1:
            return ap
        spec = "p (" + " ".join(f"a{i}" for i in range(len(shape))) + ") -> p " + \
               " ".join(f"a{i}" for i in range(len(shape)))
        kw = {f"a{i}": shape[i] for i in range(1, len(shape))}
        return ap.rearrange(spec, **kw)


class Builder:
    def __init__(self, nc, dbg=()):
        self.nc = nc
        self.dbgnames = set(dbg)
        self.dbg_t = {}
        self.es = ExitStack()
        self.stop_after = None
        self.nseq = BPC

    def declare_io(self):
        nc = self.nc
        self.xc = nc.dram_tensor("xc", [BPC, S, D], F32, kind="ExternalInput").ap()
        self.wq_a = nc.dram_tensor("L0_attn_Wq", [D, D], F32, kind="ExternalInput").ap()
        self.bq_a = nc.dram_tensor("L0_attn_bq", [D], F32, kind="ExternalInput").ap()
        self.wk_a = nc.dram_tensor("L0_attn_Wk", [D, D], F32, kind="ExternalInput").ap()
        self.bk_a = nc.dram_tensor("L0_attn_bk", [D], F32, kind="ExternalInput").ap()
        self.wv_a = nc.dram_tensor("L0_attn_Wv", [D, D], F32, kind="ExternalInput").ap()
        self.bv_a = nc.dram_tensor("L0_attn_bv", [D], F32, kind="ExternalInput").ap()
        self.wq_f = nc.dram_tensor("L0_ffn_Wq", [D, H * E * DH], F32, kind="ExternalInput").ap()
        self.wkv_f = nc.dram_tensor("L0_ffn_Wkv", [D, 2 * H * E * DH], F32, kind="ExternalInput").ap()
        self.ws_f = nc.dram_tensor("L0_ffn_Ws", [D, H * E], F32, kind="ExternalInput").ap()
        self.wd_f = nc.dram_tensor("L0_ffn_Wd", [D, H * E], F32, kind="ExternalInput").ap()
        self.wo_f = nc.dram_tensor("L0_ffn_Wo", [DH, E * D], F32, kind="ExternalInput").ap()
        self.yout = nc.dram_tensor("yout", [BPC, S, D], F32, kind="ExternalOutput").ap()
        if getattr(self, "dbg_ln_reload", False):
            self.dbg_ln_in = nc.dram_tensor("dbg_ln_in", [128, 8, S], F32, kind="ExternalInput").ap()
        if getattr(self, "dbg_qkv_reload", False):
            self.dbg_kfm_in = nc.dram_tensor("dbg_kfm_in", [128, G, S], F32, kind="ExternalInput").ap()
            self.dbg_qfm_in = nc.dram_tensor("dbg_qfm_in", [128, G, S], F32, kind="ExternalInput").ap()
            self.dbg_vtm_in = nc.dram_tensor("dbg_vtm_in", [128, NT, D], F32, kind="ExternalInput").ap()

    def declare_dbg(self, name, free_shape):
        if name in self.dbgnames:
            self.dbg_t[name] = self.nc.dram_tensor(
                "dbg_" + name, [128] + list(free_shape), F32, kind="ExternalOutput").ap()

    def alloc(self):
        nc, es = self.nc, self.es
        sb = lambda name, shape, dt=BF16: es.enter_context(nc.sbuf_tensor(name, shape, dt))

        self.ident = sb("ident", [128, 128])
        if "statd" in self.dbgnames:
            self.statd = sb("statd", [128, NT, 8], F32)
        self.ones1 = sb("ones1", [1, 128])
        self.bqk = sb("bqk", [128, 16], F32)
        self.bvv = sb("bvv", [1, D])
        self.wsd = sb("wsd", [128, 8, 2 * H * E])

        self.stat = sb("stat", [128, 8], F32)
        self.X = sb("X", [128, NT, D])
        self.xstage = sb("xstage", [128, 4, D], F32)
        self.wslc = sb("wslc", [128, 2, 8, 320])
        self.msk = sb("msk", [128, 2, NT, H * E])
        self.ssg = sb("ssg", [128, NT, H * E])
        self.sdg = sb("sdg", [128, NT, H * E])

        tpe = sb("tpe", [128, 12 * 512])
        c = Carve(tpe)
        self.ln_tm = c.take((4, D))
        self.sq = c.take((D,), F32)
        c = Carve(tpe)
        self.wk_ring = c.take((4, 8, 128))
        self.wv_ring = c.take((2, D))
        c = Carve(tpe)
        self.pe_buf = c.take((2, 2048))
        self.pe_scr = c.take((1024,))
        self.d32 = c.take((256,), F32)
        self.r_buf = c.take((256,))
        c = Carve(tpe)
        self.iota_scr = c.take((256,), F32)

        lnr = sb("lnr", [128, 32 * 512])
        c = Carve(lnr)
        self.lnfm = c.take((8, S))
        c = Carve(lnr)
        self.out_stg = c.take((2, D), F32)
        self.o_sb = c.take((2, 512))
        self.z_acc = c.take((8, 320), F32)
        self.z_b = c.take((2, 384))
        self.zt = c.take((2, 384))
        self.r8 = c.take((2, 8), F32)
        self.mdr = c.take((2, H * E), F32)
        w0 = c.take((D,))
        w1 = c.take((D,))
        w2 = c.take((D,))
        self.woc = [w0, w1, w2.t if False else w2]
        self.woc2_64 = None

        shr = sb("shr", [128, 97 * 512])
        c = Carve(shr)
        self.kfm = c.take((G, S))
        self.vtm = c.take((NT, D))
        self.qfm = c.take((G, S))
        c = Carve(shr)
        self.qh = c.take((NT, 512))
        self.kh = c.take((NT, 512))
        self.vtmf = c.take((NT, 528))
        self.qfmb = c.take((4, S))
        self.kfmb = c.take((4, S))
        self.ssd = c.take((NT, 2 * H * E), F32)
        if "ped" in self.dbgnames:
            self.ped = self.lnfm[:, 4, :]
            self.scd = self.lnfm[:, 0:2, :].rearrange("p a b -> p (a b)").bitcast(F32)
            self.scd1 = self.lnfm[:, 2:4, :].rearrange("p a b -> p (a b)").bitcast(F32)
        self.gated = c.take((2, 4, 5, 64))
        self.cmp = c.take((4, NT, H), F32)
        self.cnt = c.take((NT, H), F32)

        self.PS = es.enter_context(nc.psum_tensor("PS", [128, 4096], F32))
        self.PSB = self.PS[:].bitcast(BF16)

        names = ("ldw", "ldc", "st", "tsc", "tcx", "tgp", "sexp", "seva", "sgen",
                 "vsm", "vres", "vgen", "ggen", "bar",
                 "lx0", "lx1", "lx2", "lx3", "so0", "so1", "zt0", "zt1",
                 "vdr", "vyh",
                 "lwk0", "lwk1", "lwk2", "lwk3", "lwv0", "lwv1", "lws0", "lws1")
        self.sm = {n_: [es.enter_context(nc.semaphore(n_)), 0] for n_ in names}

    def inc(self, name, bi, k=1):
        s = self.sm[name]
        bi.then_inc(s[0], k)
        s[1] += k
        return bi

    def w(self, eng, name, val=None):
        s = self.sm[name]
        v = s[1] if val is None else val
        if v > 0:
            eng.wait_ge(s[0], v)

    def n(self, name):
        return self.sm[name][1]

    def barrier(self, with_sync=True):
        blk = self.block
        with_sync = True
        target = self.n("bar") + 5

        @blk.tensor
        def _(t):
            t.drain()
            self.inc("bar", t.nop())
            t.wait_ge(self.sm["bar"][0], target)

        @blk.vector
        def _(v):
            v.drain()
            self.inc("bar", v.engine_nop())
            v.wait_ge(self.sm["bar"][0], target)

        @blk.scalar
        def _(s_):
            s_.drain()
            self.inc("bar", s_.nop())
            s_.wait_ge(self.sm["bar"][0], target)

        @blk.gpsimd
        def _(g_):
            g_.drain()
            self.inc("bar", g_.nop())
            g_.wait_ge(self.sm["bar"][0], target)

        if with_sync:
            @blk.sync
            def _(sy):
                self.w(sy, "st")
                self.w(sy, "so0")
                self.w(sy, "so1")
                self.inc("bar", sy.nop())
                sy.wait_ge(self.sm["bar"][0], target)
        assert self.n("bar") == target

    def tap(self, name, src_ap, free_shape):
        if name not in self.dbg_t:
            return
        dst = self.dbg_t[name]

        src_ap = src_ap[:]

        @self.block.gpsimd
        def _(g_):
            self.inc("st", g_.dma_start(out=dst, in_=src_ap), 16)
            g_.wait_ge(self.sm["st"][0], self.n("st"))

    def build(self):
        self.declare_io()
        for nm, shp in (("lntm", [4, D]), ("lnfm2", [8, S]), ("kfm3", [G, S]), ("kfm2", [G, S]), ("vtm2", [NT, D]), ("qfm2", [G, S]), ("ped", [2048]), ("scd", [2048]), ("scd1", [2048]), ("statd", [NT, 8]), ("lnfm", [8, S]), ("kfm", [G, S]), ("vtm", [NT, D]),
                        ("qfm", [G, S]), ("x1", [NT, D]), ("ssd", [NT, 2 * H * E]),
                        ("msk", [2, NT, H * E]), ("qh", [NT, 512]), ("kh", [NT, 512]),
                        ("vtmf", [NT, 528]), ("pe", [2, 2048])):
            self.declare_dbg(nm, shp)
        self.alloc()
        with self.es:
            self.block = self.es.enter_context(self.nc.Block())
            self.p0_consts()
            stop = False
            for s in range(self.nseq):
                self.seq = s
                phases = [
                    ("ln1", lambda: (self.ln_phase(src_hbm=True), self.barrier())),
                    ("tap1", lambda: ((self.tap("lnfm", self.lnfm, None), self.tap("statd", getattr(self, "statd", None), None) if "statd" in self.dbg_t else None) if s == 0 else None)),
                    ("a2", lambda: (self.a2_proj(), self.barrier())),
                    ("tap2", lambda: ([self.tap(n_, getattr(self, n_), None) for n_ in ("kfm", "vtm", "qfm")] + [self.tap("lnfm2", self.lnfm, None)] if s == 0 else None)),
                    ("a3", lambda: (self.a3_attn(), self.barrier(with_sync=True))),
                    ("tap3", lambda: ((self.tap("kfm2", self.kfm, None), self.tap("vtm2", self.vtm, None), self.tap("qfm2", self.qfm, None), self.tap("x1", self.X, None), self.tap("ped", getattr(self, "ped", None), None) if "ped" in self.dbg_t else None, self.tap("scd", getattr(self, "scd", None), None) if "scd" in self.dbg_t else None, self.tap("scd1", getattr(self, "scd1", None), None) if "scd1" in self.dbg_t else None) if s == 0 else None)),
                    ("ln2", lambda: (self.ln_phase(src_hbm=False), self.barrier())),
                    ("b1", lambda: (self.b1_router(), self.barrier())),
                    ("tap4", lambda: ([self.tap(n_, getattr(self, n_), None) for n_ in ("ssd", "msk")] if s == 0 else None)),
                    ("b2", lambda: (self.b2_projgate(), self.barrier())),
                    ("tap5", lambda: ([self.tap(n_, getattr(self, n_), None) for n_ in ("qh", "kh", "vtmf")] if s == 0 else None)),
                    ("b3", lambda: (self.b3_transpose(), self.barrier())),
                    ("b4", lambda: (self.b4_attn(), self.barrier(with_sync=True))),
                ]
                for pname, pf in phases:
                    pf()
                    if self.stop_after is not None and pname == self.stop_after:
                        stop = True
                        break
                if stop:
                    break

            @self.block.sync
            def _(sy):
                self.w(sy, "st")
                self.w(sy, "so0")
                self.w(sy, "so1")

    def p0_consts(self):
        blk = self.block

        @blk.gpsimd
        def _(g_):
            g_.iota(self.iota_scr[:, 0:128], pattern=[[1, 128]], base=0,
                    channel_multiplier=0, allow_small_or_imprecise_dtypes=True)
            g_.iota(self.iota_scr[:, 128:256], pattern=[[0, 128]], base=0,
                    channel_multiplier=1, allow_small_or_imprecise_dtypes=True)
            g_.memset(self.ones1[:], 1.0)
            self.inc("ldc", g_.dma_start(out=self.bqk[:, 0:8],
                     in_=self.bq_a.rearrange("(g p) -> p g", p=128),
                     allow_slow_non_contiguous=True), 16)
            self.inc("ldc", g_.dma_start(out=self.bqk[:, 8:16],
                     in_=self.bk_a.rearrange("(g p) -> p g", p=128),
                     allow_slow_non_contiguous=True), 16)
            self.inc("ldc", g_.dma_start(out=self.bvv[:], in_=self.bv_a.unsqueeze(0)), 16)
            for dc in range(8):
                self.inc("ldc", g_.dma_start(out=self.wsd[:, dc, 0:40],
                         in_=self.ws_f[dc * 128:(dc + 1) * 128, :]), 16)
                self.inc("ldc", g_.dma_start(out=self.wsd[:, dc, 40:80],
                         in_=self.wd_f[dc * 128:(dc + 1) * 128, :]), 16)
            self.inc("ggen", g_.drain())

        @blk.vector
        def _(v):
            self.w(v, "ggen", 1)
            tt = v.tensor_tensor(out=self.ident[:], in0=self.iota_scr[:, 0:128],
                                 in1=self.iota_scr[:, 128:256], op=ALU.is_equal)
            self.inc("vgen", tt)

        @blk.tensor
        def _(t):
            self.w(t, "vgen", self.n("vgen"))

    def ln_phase(self, src_hbm):
        blk, s = self.block, self.seq
        lxb = [self.n(f"lx{k}") for k in range(4)]
        sg0 = self.n("sgen")
        vg0 = self.n("vgen")
        tg0 = self.n("tgp")
        vr0 = self.n("vres")

        if src_hbm and "lntm" in self.dbg_t and self.seq == 0:
            @blk.gpsimd
            def _(g_):
                self.w(g_, "sgen", sg0 + 48)
                self.inc("st", g_.dma_start(out=self.dbg_t["lntm"], in_=self.ln_tm), 16)
                g_.wait_ge(self.sm["st"][0], self.n("st"))

        if src_hbm:
            @blk.sync
            def _(sy):
                for nt in range(NT):
                    if nt >= 4:
                        self.w(sy, "sgen", sg0 + 3 * (nt - 4) + 3)
                    self.inc(f"lx{nt % 4}", sy.dma_start(out=self.xstage[:, nt % 4, :],
                             in_=self.xc[s, nt * 128:(nt + 1) * 128, :]), 16)

        @blk.scalar
        def _(s_):
            for nt in range(NT):
                xin = self.xstage[:, nt % 4, :] if src_hbm else self.X[:, nt, :]
                if src_hbm:
                    self.w(s_, f"lx{nt % 4}", lxb[nt % 4] + 16 * (nt // 4 + 1))
                a = s_.activation(out=self.sq[:], in_=xin, func=AF.Square,
                                  accum_out=self.stat[:, 1:2])
                self.inc("sgen", a)
                self.w(s_, "vgen", vg0 + 2 * nt + 1)
                a = s_.activation(out=self.stat[:, 5:6], in_=self.stat[:, 3:4],
                                  func=AF.Sqrt)
                self.inc("sgen", a)
                self.w(s_, "vgen", vg0 + 2 * nt + 2)
                if nt >= 4:
                    self.w(s_, "tgp", tg0 + ((nt - 4) // 4 + 1) * 8)
                s_.activation(out=self.ln_tm[:, nt % 4, :], in_=xin, func=AF.Identity,
                              bias=self.stat[:, 7:8], scale=self.stat[:, 6:7])
                dr = s_.drain()
                self.inc("sgen", dr)

        @blk.vector
        def _(v):
            for nt in range(NT):
                xin = self.xstage[:, nt % 4, :] if src_hbm else self.X[:, nt, :]
                if src_hbm:
                    self.w(v, f"lx{nt % 4}", lxb[nt % 4] + 16 * (nt // 4 + 1))
                v.reduce_sum(self.stat[:, 0:1], xin, axis=AX.X)
                self.w(v, "sgen", sg0 + 3 * nt + 1)
                v.drain()
                v.tensor_scalar(out=self.stat[:, 2:3], in0=self.stat[:, 0:1],
                                scalar1=1.0 / D, scalar2=None, op0=ALU.mult)
                v.tensor_scalar(out=self.stat[:, 3:4], in0=self.stat[:, 1:2],
                                scalar1=1.0 / D, scalar2=None, op0=ALU.mult)
                v.drain()
                v.tensor_tensor(out=self.stat[:, 4:5], in0=self.stat[:, 2:3],
                                in1=self.stat[:, 2:3], op=ALU.mult)
                v.drain()
                v.tensor_tensor(out=self.stat[:, 3:4], in0=self.stat[:, 3:4],
                                 in1=self.stat[:, 4:5], op=ALU.subtract)
                v.drain()
                tt = v.tensor_scalar(out=self.stat[:, 3:4], in0=self.stat[:, 3:4],
                                     scalar1=float(EPS), scalar2=None, op0=ALU.add)
                v.drain()
                self.inc("vgen", tt)
                self.w(v, "sgen", sg0 + 3 * nt + 2)
                v.reciprocal(out=self.stat[:, 6:7], in_=self.stat[:, 5:6])
                v.drain()
                tt = v.scalar_tensor_tensor(out=self.stat[:, 7:8], in0=self.stat[:, 2:3],
                                            scalar=-1.0, in1=self.stat[:, 6:7],
                                            op0=ALU.mult, op1=ALU.mult)
                v.drain()
                self.inc("vgen", tt)
                if "statd" in self.dbg_t and self.seq == 0 and src_hbm:
                    v.drain()
                    v.tensor_copy(out=self.statd[:, nt, :], in_=self.stat[:])

                if nt % 4 == 3:
                    tg = nt // 4
                    for dc in range(8):
                        self.w(v, "tgp", tg0 + tg * 8 + dc + 1)
                        base = (dc % 4) * 2048 + (dc // 4) * 512
                        cp = v.tensor_copy(
                            out=self.lnfm[:, dc, (4 * tg) * 128:(4 * tg + 4) * 128],
                            in_=self.PSB[:, base:base + 512])
                        self.inc("vres", cp)

        @blk.tensor
        def _(t):
            for tg in range(NT // 4):
                self.w(t, "sgen", sg0 + 3 * (4 * tg + 4))
                for dc in range(8):
                    gidx = tg * 8 + dc
                    if gidx >= 4:
                        self.w(t, "vres", vr0 + gidx - 3)
                    base = (dc % 4) * 2048 + (dc // 4) * 512
                    for j in range(4):
                        t.transpose(self.PSB[:, base + j * 128: base + (j + 1) * 128],
                                    self.ln_tm[:, j, dc * 128:(dc + 1) * 128],
                                    self.ident[:])
                    self.inc("tgp", t.drain())

    def a2_proj(self):
        blk = self.block
        if getattr(self, "dbg_ln_reload", False) and self.seq == 0:
            @blk.gpsimd
            def _(g_):
                for dc in range(8):
                    self.inc("ldw", g_.dma_start(out=self.lnfm[:, dc, :],
                             in_=self.dbg_ln_in[:, dc, :]), 16)
                self.inc("ggen", g_.nop())

            @blk.tensor
            def _(t):
                self.w(t, "ldw")
                self.w(t, "ggen")
        lwkb = [self.n(f"lwk{k}") for k in range(4)]
        lwvb = [self.n(f"lwv{k}") for k in range(2)]
        tg0 = self.n("tgp")
        se0 = self.n("seva")

        @blk.gpsimd
        def _(g_):
            for it, wsrc in enumerate((self.wk_a, self.wq_a)):
                for g in range(G):
                    gi = it * G + g
                    if gi >= 4:
                        self.w(g_, "tgp", tg0 + (gi - 3) * 2)
                    for dc in range(8):
                        self.inc(f"lwk{gi % 4}", g_.dma_start(
                            out=self.wk_ring[:, gi % 4, dc, :],
                            in_=wsrc[dc * 128:(dc + 1) * 128, g * 128:(g + 1) * 128]), 16)
            for ntp in range(8):
                for dc in range(8):
                    idx = ntp * 8 + dc
                    if idx >= 2:
                        i2 = idx - 2
                        self.w(g_, "tgp", tg0 + 32 + (i2 // 8) * 9 + (i2 % 8) + 1)
                    self.inc(f"lwv{idx % 2}", g_.dma_start(
                        out=self.wv_ring[:, idx % 2, :],
                        in_=self.wv_a[dc * 128:(dc + 1) * 128, :]), 16)

        @blk.tensor
        def _(t):
            self.w(t, "ldc")
            for it, dst_bias in enumerate(((self.kfm, 8), (self.qfm, 0))):
                for g in range(G):
                    gi = it * G + g
                    self.w(t, f"lwk{gi % 4}", lwkb[gi % 4] + 16 * 8 * (gi // 4 + 1))
                    for np_ in range(2):
                        m = gi * 2 + np_
                        if m >= 2:
                            self.w(t, "seva", se0 + (m - 1) * 2)
                        for dc in range(8):
                            for nbl in range(2):
                                nb = np_ * 2 + nbl
                                bank = (np_ * 2 + nbl) * 512
                                mm = t.matmul(self.PS[:, bank:bank + 512],
                                              self.wk_ring[:, gi % 4, dc, :],
                                              self.lnfm[:, dc, nb * 512:(nb + 1) * 512],
                                              start=(dc == 0), stop=(dc == 7))
                        self.inc("tgp", mm)
            for ntp in range(8):
                for dc in range(8):
                    idx = ntp * 8 + dc
                    self.w(t, f"lwv{idx % 2}", lwvb[idx % 2] + 16 * (idx // 2 + 1))
                    if dc == 0:
                        if ntp >= 2:
                            self.w(t, "seva", se0 + 64 + (ntp - 1) * 2)
                        else:
                            self.w(t, "seva", se0 + 64)
                    for t01 in range(2):
                        nt = ntp * 2 + t01
                        rbase = (ntp % 2) * 2048 + t01 * 1024
                        for half in range(2):
                            mm = t.matmul(self.PS[:, rbase + half * 512: rbase + (half + 1) * 512],
                                          self.lnfm[:, dc, nt * 128:(nt + 1) * 128],
                                          self.wv_ring[:, idx % 2, half * 512:(half + 1) * 512],
                                          start=(dc == 0), stop=False)
                    self.inc("tgp", mm)
                for t01 in range(2):
                    rbase = (ntp % 2) * 2048 + t01 * 1024
                    for half in range(2):
                        mm = t.matmul(self.PS[:, rbase + half * 512: rbase + (half + 1) * 512],
                                      self.ones1[:],
                                      self.bvv[0:1, half * 512:(half + 1) * 512],
                                      start=False, stop=True)
                self.inc("tgp", mm)

        @blk.vector
        def _(v):
            self.w(v, "ldc")
            for it, (dst, boff) in enumerate(((self.kfm, 8), (self.qfm, 0))):
                for g in range(G):
                    gi = it * G + g
                    for np_ in range(2):
                        m = gi * 2 + np_
                        self.w(v, "tgp", tg0 + m + 1)
                        for nbl in range(2):
                            nb = np_ * 2 + nbl
                            bank = (np_ * 2 + nbl) * 512
                            a = v.tensor_scalar(out=dst[:, g, nb * 512:(nb + 1) * 512],
                                                in0=self.PS[:, bank:bank + 512],
                                                scalar1=self.bqk[:, boff + g:boff + g + 1],
                                                scalar2=None, op0=ALU.add)
                            self.inc("seva", a)
            for ntp in range(8):
                self.w(v, "tgp", tg0 + 32 + ntp * 9 + 9)
                for t01 in range(2):
                    nt = ntp * 2 + t01
                    rbase = (ntp % 2) * 2048 + t01 * 1024
                    a = v.tensor_copy(out=self.vtm[:, nt, :],
                                      in_=self.PS[:, rbase:rbase + 1024])
                    self.inc("seva", a)

    def a3_attn(self):
        blk, s = self.block, self.seq
        if getattr(self, "dbg_qkv_reload", False) and s == 0:
            import os as _os
            _which = _os.environ.get("RELOAD", "kqv")

            @blk.gpsimd
            def _(g_):
                for dc in range(8):
                    if "k" in _which:
                        self.inc("ldw", g_.dma_start(out=self.kfm[:, dc, :],
                                 in_=self.dbg_kfm_in[:, dc, :]), 16)
                    if "q" in _which:
                        self.inc("ldw", g_.dma_start(out=self.qfm[:, dc, :],
                                 in_=self.dbg_qfm_in[:, dc, :]), 16)
                if "v" in _which:
                    for ti in range(NT):
                        self.inc("ldw", g_.dma_start(out=self.vtm[:, ti, :],
                                 in_=self.dbg_vtm_in[:, ti, :]), 16)
                self.inc("ggen", g_.nop())

            @blk.tensor
            def _(t):
                self.w(t, "ldw")
                self.w(t, "ggen")
        lxb = [self.n(f"lx{k}") for k in range(4)]
        tsc0 = self.n("tsc")
        tcx0 = self.n("tcx")
        sx0 = self.n("sexp")
        vs0 = self.n("vsm")
        vr0 = self.n("vres")

        @blk.sync
        def _(sy):
            for nt in range(NT):
                if nt >= 4:
                    self.w(sy, "vres", vr0 + nt - 3)
                self.inc(f"lx{nt % 4}", sy.dma_start(out=self.xstage[:, nt % 4, :],
                         in_=self.xc[s, nt * 128:(nt + 1) * 128, :]), 16)

        if "kfm3" in self.dbg_t and self.seq == 0:
            @blk.gpsimd
            def _(g_):
                self.w(g_, "vres", vr0 + 2)
                self.inc("st", g_.dma_start(out=self.dbg_t["kfm3"], in_=self.kfm), 16)

        morder = list(range(16))
        if getattr(self, "rev_m", False):
            morder = morder[::-1]

        gg0 = self.n("ggen")

        def ctx_mm(t, i, mi_, m, g, j):
            return t.matmul(self.PS[:, 2048 + j * 1024 + g * 128:
                                    2048 + j * 1024 + (g + 1) * 128],
                            self.pe_buf[:, i % 2, g * 256 + j * 128:
                                        g * 256 + (j + 1) * 128],
                            self.vtm[:, m, g * 128:(g + 1) * 128],
                            start=(mi_ == 0 and g % 4 == 0),
                            stop=(mi_ == 15 and g % 4 == 3))

        def emit_ctx(t, ip):
            nbp, mip = ip // 16, ip % 16
            mp = morder[mip]
            if mip == 0 and nbp > 0:
                self.w(t, "vres", vr0 + nbp * 2)
            self.w(t, "vsm", vs0 + ip + 1)
            for j in range(2):
                for g in range(6):
                    mm = ctx_mm(t, ip, mip, mp, g, j)
            self.w(t, "ggen", gg0 + ip + 1)
            for j in range(2):
                for g in range(6, G):
                    mm = ctx_mm(t, ip, mip, mp, g, j)
            self.inc("tcx", mm)

        @blk.tensor
        def _(t):
            for nb in range(NB):
                for mi_, m in enumerate(morder):
                    i = nb * 16 + mi_
                    if i > 0:
                        self.w(t, "sexp", sx0 + 2 * i - 1)
                    for g in range(4):
                        mm = t.matmul(self.PS[:, g * 256:(g + 1) * 256],
                                      self.kfm[:, g, m * 128:(m + 1) * 128],
                                      self.qfm[:, g, nb * 256:(nb + 1) * 256],
                                      start=True, stop=True)
                    self.inc("tsc", mm)
                    if i > 0:
                        self.w(t, "sexp", sx0 + 2 * i)
                    for g in range(4, G):
                        mm = t.matmul(self.PS[:, g * 256:(g + 1) * 256],
                                      self.kfm[:, g, m * 128:(m + 1) * 128],
                                      self.qfm[:, g, nb * 256:(nb + 1) * 256],
                                      start=True, stop=True)
                    self.inc("tsc", mm)
                    if i > 0:
                        emit_ctx(t, i - 1)
            emit_ctx(t, NB * 16 - 1)

        @blk.scalar
        def _(s_):
            for nb in range(NB):
                for m in range(16):
                    i = nb * 16 + m
                    if i >= 2:
                        self.w(s_, "tcx", tcx0 + i - 1)
                    self.w(s_, "tsc", tsc0 + 2 * i + 1)
                    a = s_.activation(out=self.pe_buf[:, i % 2, 0:1024],
                                      in_=self.PS[:, 0:1024], func=AF.Exp,
                                      scale=1.0 / 32.0)
                    self.inc("sexp", a)
                    self.w(s_, "tsc", tsc0 + 2 * i + 2)
                    a = s_.activation(out=self.pe_buf[:, i % 2, 1024:2048],
                                      in_=self.PS[:, 1024:2048], func=AF.Exp,
                                      scale=1.0 / 32.0)
                    self.inc("sexp", a)

        @blk.vector
        def _(v):
            for nb in range(NB):
                for m in range(16):
                    i = nb * 16 + m
                    pe = self.pe_buf[:, i % 2, :]
                    t0, t1, t2, t3 = (self.pe_scr[:, k * 256:(k + 1) * 256] for k in range(4))
                    self.w(v, "sexp", sx0 + 2 * i + 1)
                    v.tensor_tensor(out=t0, in0=pe[:, 0:256], in1=pe[:, 256:512], op=ALU.add)
                    v.tensor_tensor(out=t1, in0=pe[:, 512:768], in1=pe[:, 768:1024], op=ALU.add)
                    self.w(v, "sexp", sx0 + 2 * i + 2)
                    v.tensor_tensor(out=t2, in0=pe[:, 1024:1280], in1=pe[:, 1280:1536], op=ALU.add)
                    v.tensor_tensor(out=t3, in0=pe[:, 1536:1792], in1=pe[:, 1792:2048], op=ALU.add)
                    v.drain()
                    v.tensor_tensor(out=t0, in0=t0, in1=t1, op=ALU.add)
                    v.tensor_tensor(out=t2, in0=t2, in1=t3, op=ALU.add)
                    v.drain()
                    v.tensor_tensor(out=self.d32[:], in0=t0, in1=t2, op=ALU.add)
                    v.drain()
                    if i > 0:
                        self.w(v, "ggen", gg0 + i)
                    with self.nc.allow_low_precision(reason="bf16 softmax scale"):
                        v.reciprocal(out=self.r_buf[:], in_=self.d32[:])
                    v.drain()
                    for g in range(6):
                        tt = v.tensor_tensor(out=pe[:, g * 256:(g + 1) * 256],
                                             in0=pe[:, g * 256:(g + 1) * 256],
                                             in1=self.r_buf[:], op=ALU.mult)
                    self.inc("vsm", tt)
                self.w(v, "tcx", tcx0 + nb * 16 + 16)
                for j in range(2):
                    nt = nb * 2 + j
                    self.w(v, f"lx{nt % 4}", lxb[nt % 4] + 16 * (nt // 4 + 1))
                    tt = v.tensor_tensor(out=self.X[:, nt, :],
                                         in0=self.PS[:, 2048 + j * 1024:2048 + (j + 1) * 1024],
                                         in1=self.xstage[:, nt % 4, :], op=ALU.add)
                    self.inc("vres", tt)

        @blk.gpsimd
        def _(g_):
            for nb in range(NB):
                for m in range(16):
                    i = nb * 16 + m
                    pe = self.pe_buf[:, i % 2, :]
                    self.w(g_, "vsm", vs0 + i + 1)
                    for g in range(6, G):
                        tt = g_.tensor_tensor(out=pe[:, g * 256:(g + 1) * 256],
                                              in0=pe[:, g * 256:(g + 1) * 256],
                                              in1=self.r_buf[:], op=ALU.mult)
                    self.inc("ggen", tt)

    def b1_router(self):
        blk = self.block
        tg0 = self.n("tgp")
        sg0 = self.n("sgen")

        @blk.tensor
        def _(t):
            self.w(t, "ldc")
            for nt in range(NT):
                if nt >= 2:
                    self.w(t, "sgen", sg0 + nt - 1)
                r = (nt % 2) * 512
                for dc in range(8):
                    mm = t.matmul(self.PS[:, r:r + 80],
                                  self.lnfm[:, dc, nt * 128:(nt + 1) * 128],
                                  self.wsd[:, dc, :],
                                  start=(dc == 0), stop=(dc == 7))
                self.inc("tgp", mm)

        @blk.scalar
        def _(s_):
            for nt in range(NT):
                self.w(s_, "tgp", tg0 + nt + 1)
                r = (nt % 2) * 512
                a = s_.activation(out=self.ssd[:, nt, :], in_=self.PS[:, r:r + 80],
                                  func=AF.Sigmoid)
                self.inc("sgen", a)

        @blk.vector
        def _(v):
            self.w(v, "sgen", sg0 + NT)
            for gi in range(2):
                gsl = self.ssd[:, :, gi * 40:(gi + 1) * 40].rearrange(
                    "p t (h e) -> p t h e", e=E)
                for ei in range(E):
                    jj = 0
                    for ej in range(E):
                        if ej == ei:
                            continue
                        op = ALU.is_ge if ej < ei else ALU.is_gt
                        v.tensor_tensor(out=self.cmp[:, jj, :, :].unsqueeze(3),
                                        in0=gsl[:, :, :, ej:ej + 1],
                                        in1=gsl[:, :, :, ei:ei + 1], op=op)
                        jj += 1
                    v.drain()
                    v.tensor_tensor(out=self.cnt[:], in0=self.cmp[:, 0, :, :],
                                    in1=self.cmp[:, 1, :, :], op=ALU.add)
                    v.drain()
                    v.tensor_tensor(out=self.cnt[:], in0=self.cnt[:],
                                    in1=self.cmp[:, 2, :, :], op=ALU.add)
                    v.drain()
                    v.tensor_tensor(out=self.cnt[:], in0=self.cnt[:],
                                    in1=self.cmp[:, 3, :, :], op=ALU.add)
                    v.drain()
                    mview = self.msk[:, gi, :, :].rearrange(
                        "p t (h e) -> p t h e", e=E)[:, :, :, ei:ei + 1]
                    v.tensor_scalar(out=mview, in0=self.cnt[:].unsqueeze(3),
                                    scalar1=2.5, scalar2=None, op0=ALU.is_le)
                    v.drain()
            v.tensor_tensor(out=self.ssg[:], in0=self.ssd[:, :, 0:40],
                            in1=self.msk[:, 0, :, :], op=ALU.mult)
            v.tensor_tensor(out=self.sdg[:], in0=self.ssd[:, :, 40:80],
                            in1=self.msk[:, 1, :, :], op=ALU.mult)
            vv = self.vtmf.rearrange("p t (h c) -> p t h c", c=66)
            tt = v.memset(vv[:, :, :, 64:65], 1.0)
            self.inc("vgen", tt)

    def b2_projgate(self):
        blk = self.block
        lwsb = [self.n(f"lws{k}") for k in range(2)]
        tg0 = self.n("tgp")
        vg0 = self.n("vgen")

        blocks = []
        for h in range(H):
            blocks.append((self.wq_f, h * 320, self.sdg, self.qh, h * 64, 64))
        for h in range(H):
            blocks.append((self.wkv_f, h * 320, self.ssg, self.kh, h * 64, 64))
        for h in range(H):
            blocks.append((self.wkv_f, 2560 + h * 320, self.ssg, self.vtmf, h * 66, 66))

        @blk.gpsimd
        def _(g_):
            for bi_, (wsrc, coff, _, _, _, _) in enumerate(blocks):
                if bi_ >= 2:
                    self.w(g_, "vgen", vg0 + (bi_ - 2) * 4 + 4)
                for dc in range(8):
                    self.inc(f"lws{bi_ % 2}", g_.dma_start(
                        out=self.wslc[:, bi_ % 2, dc, :],
                        in_=wsrc[dc * 128:(dc + 1) * 128, coff:coff + 320]), 16)

        @blk.tensor
        def _(t):
            for bi_ in range(24):
                self.w(t, f"lws{bi_ % 2}", lwsb[bi_ % 2] + 16 * 8 * (bi_ // 2 + 1))
                for tg in range(4):
                    idx = bi_ * 4 + tg
                    if idx >= 2:
                        self.w(t, "vgen", vg0 + idx - 1)
                    r = (tg % 2) * 2048
                    for t_ in range(4):
                        nt = tg * 4 + t_
                        for dc in range(8):
                            mm = t.matmul(self.PS[:, r + t_ * 512: r + t_ * 512 + 320],
                                          self.lnfm[:, dc, nt * 128:(nt + 1) * 128],
                                          self.wslc[:, bi_ % 2, dc, :],
                                          start=(dc == 0), stop=(dc == 7))
                    self.inc("tgp", mm)

        @blk.vector
        def _(v):
            for bi_, (wsrc, coff, gates, dst, dcol, dstride) in enumerate(blocks):
                for tg in range(4):
                    idx = bi_ * 4 + tg
                    self.w(v, "tgp", tg0 + idx + 1)
                    r = (tg % 2) * 2048
                    psin = self.PS[:, r:r + 2048].rearrange(
                        "p (t c) -> p t c", c=512)[:, :, 0:320].rearrange(
                        "p t (e d) -> p t e d", d=64)
                    gin = gates[:, tg * 4:(tg + 1) * 4,
                                (coff % 2560) // 320 * 5:(coff % 2560) // 320 * 5 + 5]
                    gin = gin.unsqueeze(3).to_broadcast([128, 4, 5, 64])
                    gb = self.gated[:, idx % 2, :, :, :]
                    tt = v.tensor_tensor(out=gb, in0=psin, in1=gin, op=ALU.mult)
                    self.inc("vgen", tt)
                    v.drain()
                    dv = dst[:, tg * 4:(tg + 1) * 4, dcol:dcol + 64]
                    with self.nc.allow_low_precision(reason="bf16 gated sum"):
                        v.reduce_sum(dv.unsqueeze(3),
                                     gb.rearrange("p t e c -> p t c e"), axis=AX.X)

    def b3_transpose(self):
        blk = self.block
        tg0 = self.n("tgp")
        vg0 = self.n("vgen")

        @blk.tensor
        def _(t):
            for ti, (src, dst) in enumerate(((self.qh, self.qfmb), (self.kh, self.kfmb))):
                for ntg in range(4):
                    for p in range(4):
                        idx = (ti * 4 + ntg) * 4 + p
                        if idx >= 2:
                            self.w(t, "vgen", vg0 + idx - 1)
                        base = (p % 2) * 2048
                        for tt_ in range(4):
                            nt = ntg * 4 + tt_
                            tp = t.transpose(self.PSB[:, base + tt_ * 128: base + (tt_ + 1) * 128],
                                             src[:, nt, p * 128:(p + 1) * 128],
                                             self.ident[:])
                        self.inc("tgp", tp)

        @blk.vector
        def _(v):
            for ti, (src, dst) in enumerate(((self.qh, self.qfmb), (self.kh, self.kfmb))):
                for ntg in range(4):
                    for p in range(4):
                        idx = (ti * 4 + ntg) * 4 + p
                        self.w(v, "tgp", tg0 + idx + 1)
                        base = (p % 2) * 2048
                        cp = v.tensor_copy(out=dst[:, p, ntg * 512:(ntg + 1) * 512],
                                           in_=self.PSB[:, base:base + 512])
                        self.inc("vgen", cp)

    def b4_attn(self):
        blk, s = self.block, self.seq
        ldw0 = self.n("ldw")

        @blk.gpsimd
        def _(g_):
            for e_ in range(2):
                self.inc("ldw", g_.dma_start(out=self.woc[0][e_ * 64:(e_ + 1) * 64, :],
                         in_=self.wo_f[:, e_ * D:(e_ + 1) * D]), 16)
                self.inc("ldw", g_.dma_start(out=self.woc[1][e_ * 64:(e_ + 1) * 64, :],
                         in_=self.wo_f[:, (2 + e_) * D:(3 + e_) * D]), 16)
            self.inc("ldw", g_.dma_start(out=self.woc[2][0:64, :], in_=self.wo_f[:, 4 * D:5 * D]), 16)

        tsc0 = self.n("tsc")
        tcx0 = self.n("tcx")
        tg0 = self.n("tgp")
        sx0 = self.n("sexp")
        se0 = self.n("seva")
        vs0 = self.n("vsm")
        vd0 = self.n("vdr")
        vy0 = self.n("vyh")
        sob = [self.n(f"so{k}") for k in range(2)]
        ztb = [self.n("zt0"), self.n("zt1")]

        OB, YB = 2048, 3584
        ocol = lambda q: OB + (q // 7) * 512 + (q % 7) * 65

        def o_group(t, i):
            nb_, jc_ = i // 16, i % 16
            if jc_ == 0 and nb_ > 0:
                self.w(t, "seva", se0 + nb_ * 2)
                self.w(t, "vdr", vd0 + nb_ * 2)
            for j in range(2):
                for h in range(H):
                    q = j * 8 + h
                    mm = t.matmul(self.PS[:, ocol(q):ocol(q) + 65],
                                  self.pe_buf[:, i % 2, hpos(h) + j * 128:
                                              hpos(h) + (j + 1) * 128],
                                  self.vtmf[:, jc_, h * 66:h * 66 + 65],
                                  start=(jc_ == 0 and q % 7 == 0),
                                  stop=(jc_ == 15 and (q % 7 == 6 or q == 15)))
            self.inc("tcx", mm)

        def y_piece(t, nb_, pidx):
            j, half = pidx // 2, pidx % 2
            nt = nb_ * 2 + j
            self.w(t, f"zt{nt % 2}", ztb[nt % 2] + 48 * (nt // 2 + 1))
            self.w(t, "vyh", vy0 + 2 * nt + half)
            if nb_ == 0 and pidx == 0:
                self.w(t, "ldw", ldw0 + 16 * 5)
            for cch in range(3):
                if cch < 2:
                    lhs = self.zt[:, nt % 2, cch * 128:(cch + 1) * 128]
                    rhsw = self.woc[cch][:, half * 512:(half + 1) * 512]
                else:
                    lhs = self.zt[0:64, nt % 2, 256:384]
                    rhsw = self.woc[2][0:64, half * 512:(half + 1) * 512]
                mm = t.matmul(self.PS[:, YB:YB + 512], lhs, rhsw,
                              start=(cch == 0), stop=(cch == 2))
            self.inc("tgp", mm)

        @blk.tensor
        def _(t):
            for nb in range(NB):
                for jc in range(16):
                    i = nb * 16 + jc
                    self.w(t, "sexp", sx0 + i)
                    for p in range(4):
                        h0, h1 = 2 * p, 2 * p + 1
                        t.matmul(self.PS[:, hpos(h0):hpos(h0) + 256],
                                 self.kfmb[0:64, p, jc * 128:(jc + 1) * 128],
                                 self.qfmb[0:64, p, nb * 256:(nb + 1) * 256],
                                 start=True, stop=True)
                        mm = t.matmul(self.PS[:, hpos(h1):hpos(h1) + 256],
                                      self.kfmb[64:128, p, jc * 128:(jc + 1) * 128],
                                      self.qfmb[64:128, p, nb * 256:(nb + 1) * 256],
                                      start=True, stop=True, tile_position=(64, 0))
                    self.inc("tsc", mm)
                    if nb > 0 and jc in (5, 7, 9, 11):
                        y_piece(t, nb - 1, (jc - 5) // 2)
                    if jc > 0:
                        o_group(t, i - 1)
                self.w(t, "sexp", sx0 + nb * 16 + 16)
                o_group(t, nb * 16 + 15)
            for pidx in range(4):
                y_piece(t, NB - 1, pidx)

        @blk.scalar
        def _(s_):
            for nb in range(NB):
                for jc in range(16):
                    i = nb * 16 + jc
                    self.w(s_, "tsc", tsc0 + i + 1)
                    if i >= 2:
                        self.w(s_, "tcx", tcx0 + i - 1)
                    a = s_.activation(out=self.pe_buf[:, i % 2, :],
                                      in_=self.PS[:, 0:2048], func=AF.Exp,
                                      scale=0.125)
                    self.inc("sexp", a)
                self.w(s_, "tcx", tcx0 + nb * 16 + 16)
                for j in range(2):
                    nt = nb * 2 + j
                    if nt >= 2:
                        self.w(s_, "vsm", vs0 + nt - 1)
                    if j == 0:
                        segs = (((2048, 7), 0), ((2560, 1), 448))
                    else:
                        segs = (((2625, 6), 0), ((3072, 2), 384))
                    for (pc, nq), oc in segs:
                        pin = self.PS[:, pc:pc + nq * 65].rearrange(
                            "p (q c) -> p q c", c=65)[:, :, 0:64]
                        a = s_.activation(
                            out=self.o_sb[:, nt % 2, oc:oc + nq * 64].rearrange(
                                "p (q c) -> p q c", c=64),
                            in_=pin, func=AF.Copy)
                    self.inc("seva", a)

        @blk.vector
        def _(v):
            v.memset(self.z_b[:, :, 320:384], 0.0)
            for nb in range(NB + 1):
                if nb > 0:
                    for j in range(2):
                        nt = (nb - 1) * 2 + j
                        for half in range(2):
                            self.w(v, "tgp", tg0 + 2 * nt + half + 1)
                            if half == 0 and nt >= 2:
                                self.w(v, f"so{nt % 2}",
                                       sob[nt % 2] + 16 * (nt // 2))
                            tt = v.tensor_tensor(
                                out=self.out_stg[:, nt % 2, half * 512:(half + 1) * 512],
                                in0=self.PS[:, YB:YB + 512],
                                in1=self.X[:, nt, half * 512:(half + 1) * 512],
                                op=ALU.add)
                            self.inc("vyh", tt)
                if nb == NB:
                    break
                self.w(v, "tcx", tcx0 + nb * 16 + 16)
                for j in range(2):
                    nt = nb * 2 + j
                    if j == 0:
                        segs = (((2048 + 64, 7), 0), ((2624, 1), 7))
                    else:
                        segs = (((2625 + 64, 6), 0), ((3136, 2), 6))
                    for (pc, nq), rc in segs:
                        if nq > 1:
                            pv = self.PS[:, pc:pc + nq * 65].rearrange(
                                "p (q c) -> p q c", c=65)[:, :, 0:1]
                            rr = v.reciprocal(
                                out=self.r8[:, nt % 2, rc:rc + nq].unsqueeze(2),
                                in_=pv)
                        else:
                            rr = v.reciprocal(out=self.r8[:, nt % 2, rc:rc + 1],
                                              in_=self.PS[:, pc:pc + 1])
                    self.inc("vdr", rr)
                v.drain()
                for j in range(2):
                    nt = nb * 2 + j
                    for h in range(H):
                        v.tensor_scalar(out=self.mdr[:, nt % 2, h * 5:(h + 1) * 5],
                                        in0=self.msk[:, 1, nt, h * 5:(h + 1) * 5],
                                        scalar1=self.r8[:, nt % 2, h:h + 1],
                                        scalar2=None, op0=ALU.mult)
                    v.drain()
                    self.w(v, "seva", se0 + nt + 1)
                    ob = self.o_sb[:, nt % 2, :].rearrange(
                        "p (h c) -> p h c", c=64).unsqueeze(2).to_broadcast(
                        [128, H, E, 64])
                    mb = self.mdr[:, nt % 2, :].rearrange(
                        "p (h e) -> p h e", e=E).unsqueeze(3).to_broadcast(
                        [128, H, E, 64])
                    v.tensor_tensor(out=self.z_acc.rearrange(
                        "p h (e c) -> p h e c", c=64), in0=ob, in1=mb, op=ALU.mult)
                    v.drain()
                    if nt >= 2:
                        self.w(v, f"zt{nt % 2}", ztb[nt % 2] + 48 * (nt // 2))
                    with self.nc.allow_low_precision(reason="fp32 accum, bf16 out"):
                        cp = v.reduce_sum(self.z_b[:, nt % 2, 0:320].unsqueeze(2),
                                          self.z_acc.rearrange("p h c -> p c h"),
                                          axis=AX.X)
                    self.inc("vsm", cp)

        @blk.sync
        def _(sy):
            for nb in range(NB):
                for j in range(2):
                    nt = nb * 2 + j
                    self.w(sy, "vsm", vs0 + nt + 1)
                    if nt >= 2:
                        self.w(sy, "tgp", tg0 + 2 * nt - 2)
                    for cch in range(3):
                        self.inc(f"zt{nt % 2}", sy.dma_start(
                            out=self.zt[:, nt % 2, cch * 128:(cch + 1) * 128],
                            in_=self.z_b[:, nt % 2, cch * 128:(cch + 1) * 128],
                            transpose=True), 16)
                for j in range(2):
                    nt = nb * 2 + j
                    self.w(sy, "vyh", vy0 + 2 * nt + 2)
                    self.inc(f"so{nt % 2}", sy.dma_start(
                             out=self.yout[s, nt * 128:(nt + 1) * 128, :],
                             in_=self.out_stg[:, nt % 2, :]), 16)


def build_nc(dbg=(), stop_after=None, nseq=BPC):
    nc = bass.Bass("TRN2", target_bir_lowering=False)
    b = Builder(nc, dbg)
    b.stop_after = stop_after
    b.nseq = nseq
    b.build()
    return nc, b.dbg_t


_NC_CACHE = {}


def kernel(**inputs):
    key = "main"
    if key not in _NC_CACHE:
        _NC_CACHE[key] = build_nc()
    nc, _ = _NC_CACHE[key]
    x = np.ascontiguousarray(inputs["L0_attn_hidden_states"])
    wkeys = ["L0_attn_Wq", "L0_attn_bq", "L0_attn_Wk", "L0_attn_bk", "L0_attn_Wv",
             "L0_attn_bv", "L0_ffn_Wq", "L0_ffn_Wkv", "L0_ffn_Ws", "L0_ffn_Wd",
             "L0_ffn_Wo"]
    in_maps = []
    for c in range(NCORE):
        m = {k: np.ascontiguousarray(inputs[k]) for k in wkeys}
        m["xc"] = x[c * BPC:(c + 1) * BPC]
        in_maps.append(m)
    res = run_bass_kernel_spmd(nc, in_maps, core_ids=list(range(NCORE)))
    return np.concatenate([r["yout"] for r in res.results], axis=0)


# version 16 (final)
# speedup vs baseline: 1.1190x; 1.0123x over previous
import numpy as np
from contextlib import ExitStack

import concourse.bass as bass
import concourse.mybir as mybir
from concourse.bass_utils import run_bass_kernel_spmd

F32 = mybir.dt.float32
BF16 = mybir.dt.bfloat16
AF = mybir.ActivationFunctionType
ALU = mybir.AluOpType
AX = mybir.AxisListType

B, S, D = 16, 2048, 1024
NCORE = 8
BPC = B // NCORE
NT = S // 128
G = 8
H, E, DH = 8, 5, 64
NB = S // 256
EPS = 1e-5


def hpos(h):
    return (h % 2) * 1024 + (h // 2) * 256


class Carve:
    def __init__(self, t):
        self.t = t
        self.off = 0

    def take(self, shape, dt=BF16):
        n = 1
        for d_ in shape:
            n *= d_
        w = n * (2 if dt is F32 else 1)
        self.off = (self.off + 15) // 16 * 16
        ap = self.t[:, self.off:self.off + w]
        self.off += w
        if dt is F32:
            ap = ap.bitcast(F32)
        if len(shape) == 1:
            return ap
        spec = "p (" + " ".join(f"a{i}" for i in range(len(shape))) + ") -> p " + \
               " ".join(f"a{i}" for i in range(len(shape)))
        kw = {f"a{i}": shape[i] for i in range(1, len(shape))}
        return ap.rearrange(spec, **kw)


class Builder:
    def __init__(self, nc, dbg=()):
        self.nc = nc
        self.dbgnames = set(dbg)
        self.dbg_t = {}
        self.es = ExitStack()
        self.stop_after = None
        self.nseq = BPC

    def declare_io(self):
        nc = self.nc
        self.xc = nc.dram_tensor("xc", [BPC, S, D], F32, kind="ExternalInput").ap()
        self.wq_a = nc.dram_tensor("L0_attn_Wq", [D, D], F32, kind="ExternalInput").ap()
        self.bq_a = nc.dram_tensor("L0_attn_bq", [D], F32, kind="ExternalInput").ap()
        self.wk_a = nc.dram_tensor("L0_attn_Wk", [D, D], F32, kind="ExternalInput").ap()
        self.bk_a = nc.dram_tensor("L0_attn_bk", [D], F32, kind="ExternalInput").ap()
        self.wv_a = nc.dram_tensor("L0_attn_Wv", [D, D], F32, kind="ExternalInput").ap()
        self.bv_a = nc.dram_tensor("L0_attn_bv", [D], F32, kind="ExternalInput").ap()
        self.wq_f = nc.dram_tensor("L0_ffn_Wq", [D, H * E * DH], F32, kind="ExternalInput").ap()
        self.wkv_f = nc.dram_tensor("L0_ffn_Wkv", [D, 2 * H * E * DH], F32, kind="ExternalInput").ap()
        self.ws_f = nc.dram_tensor("L0_ffn_Ws", [D, H * E], F32, kind="ExternalInput").ap()
        self.wd_f = nc.dram_tensor("L0_ffn_Wd", [D, H * E], F32, kind="ExternalInput").ap()
        self.wo_f = nc.dram_tensor("L0_ffn_Wo", [DH, E * D], F32, kind="ExternalInput").ap()
        self.yout = nc.dram_tensor("yout", [BPC, S, D], F32, kind="ExternalOutput").ap()
        if getattr(self, "dbg_ln_reload", False):
            self.dbg_ln_in = nc.dram_tensor("dbg_ln_in", [128, 8, S], F32, kind="ExternalInput").ap()
        if getattr(self, "dbg_qkv_reload", False):
            self.dbg_kfm_in = nc.dram_tensor("dbg_kfm_in", [128, G, S], F32, kind="ExternalInput").ap()
            self.dbg_qfm_in = nc.dram_tensor("dbg_qfm_in", [128, G, S], F32, kind="ExternalInput").ap()
            self.dbg_vtm_in = nc.dram_tensor("dbg_vtm_in", [128, NT, D], F32, kind="ExternalInput").ap()

    def declare_dbg(self, name, free_shape):
        if name in self.dbgnames:
            self.dbg_t[name] = self.nc.dram_tensor(
                "dbg_" + name, [128] + list(free_shape), F32, kind="ExternalOutput").ap()

    def alloc(self):
        nc, es = self.nc, self.es
        sb = lambda name, shape, dt=BF16: es.enter_context(nc.sbuf_tensor(name, shape, dt))

        self.ident = sb("ident", [128, 128])
        if "statd" in self.dbgnames:
            self.statd = sb("statd", [128, NT, 8], F32)
        self.ones1 = sb("ones1", [1, 128])
        self.bqk = sb("bqk", [128, 16], F32)
        self.bvv = sb("bvv", [1, D])
        self.wsd = sb("wsd", [128, 8, 2 * H * E])

        self.stat = sb("stat", [128, 8], F32)
        self.X = sb("X", [128, NT, D])
        self.xstage = sb("xstage", [128, 4, D], F32)
        self.wslc = sb("wslc", [128, 2, 8, 320])
        self.msk = sb("msk", [128, 2, NT, H * E])
        self.ssg = sb("ssg", [128, NT, H * E])
        self.sdg = sb("sdg", [128, NT, H * E])

        tpe = sb("tpe", [128, 12 * 512])
        c = Carve(tpe)
        self.ln_tm = c.take((4, D))
        self.sq = c.take((D,), F32)
        c = Carve(tpe)
        self.wk_ring = c.take((4, 8, 128))
        self.wv_ring = c.take((2, D))
        c = Carve(tpe)
        self.pe_buf = c.take((2, 2048))
        self.pe_scr = c.take((1024,))
        self.d32 = c.take((256,), F32)
        self.r_buf = c.take((256,))
        c = Carve(tpe)
        self.iota_scr = c.take((256,), F32)

        lnr = sb("lnr", [128, 32 * 512])
        c = Carve(lnr)
        self.lnfm = c.take((8, S))
        c = Carve(lnr)
        self.out_stg = c.take((2, D), F32)
        self.o_sb = c.take((2, 512))
        self.z_acc = c.take((8, 320), F32)
        self.z_b = c.take((2, 384))
        self.zt = c.take((2, 384))
        self.r8 = c.take((2, 8), F32)
        self.mdr = c.take((2, H * E), F32)
        w0 = c.take((D,))
        w1 = c.take((D,))
        w2 = c.take((D,))
        self.woc = [w0, w1, w2.t if False else w2]
        self.woc2_64 = None

        shr = sb("shr", [128, 97 * 512])
        c = Carve(shr)
        self.kfm = c.take((G, S))
        self.vtm = c.take((NT, D))
        self.qfm = c.take((G, S))
        c = Carve(shr)
        self.qh = c.take((NT, 512))
        self.kh = c.take((NT, 512))
        self.vtmf = c.take((NT, 528))
        self.qfmb = c.take((4, S))
        self.kfmb = c.take((4, S))
        self.ssd = c.take((NT, 2 * H * E), F32)
        if "ped" in self.dbgnames:
            self.ped = self.lnfm[:, 4, :]
            self.scd = self.lnfm[:, 0:2, :].rearrange("p a b -> p (a b)").bitcast(F32)
            self.scd1 = self.lnfm[:, 2:4, :].rearrange("p a b -> p (a b)").bitcast(F32)
        self.gated = c.take((2, 4, 5, 64))
        self.cmp = c.take((4, NT, H), F32)
        self.cnt = c.take((NT, H), F32)

        self.PS = es.enter_context(nc.psum_tensor("PS", [128, 4096], F32))
        self.PSB = self.PS[:].bitcast(BF16)

        names = ("ldw", "ldc", "st", "tsc", "tcx", "tgp", "sexp", "seva", "sgen",
                 "vsm", "vres", "vgen", "ggen", "bar",
                 "lx0", "lx1", "lx2", "lx3", "so0", "so1", "zt0", "zt1",
                 "vdr", "vyh",
                 "lwk0", "lwk1", "lwk2", "lwk3", "lwv0", "lwv1", "lws0", "lws1")
        self.sm = {n_: [es.enter_context(nc.semaphore(n_)), 0] for n_ in names}

    def inc(self, name, bi, k=1):
        s = self.sm[name]
        bi.then_inc(s[0], k)
        s[1] += k
        return bi

    def w(self, eng, name, val=None):
        s = self.sm[name]
        v = s[1] if val is None else val
        if v > 0:
            eng.wait_ge(s[0], v)

    def n(self, name):
        return self.sm[name][1]

    def barrier(self, with_sync=True):
        blk = self.block
        with_sync = True
        target = self.n("bar") + 5

        @blk.tensor
        def _(t):
            t.drain()
            self.inc("bar", t.nop())
            t.wait_ge(self.sm["bar"][0], target)

        @blk.vector
        def _(v):
            v.drain()
            self.inc("bar", v.engine_nop())
            v.wait_ge(self.sm["bar"][0], target)

        @blk.scalar
        def _(s_):
            s_.drain()
            self.inc("bar", s_.nop())
            s_.wait_ge(self.sm["bar"][0], target)

        @blk.gpsimd
        def _(g_):
            g_.drain()
            self.inc("bar", g_.nop())
            g_.wait_ge(self.sm["bar"][0], target)

        if with_sync:
            @blk.sync
            def _(sy):
                self.w(sy, "st")
                self.w(sy, "so0")
                self.w(sy, "so1")
                self.inc("bar", sy.nop())
                sy.wait_ge(self.sm["bar"][0], target)
        assert self.n("bar") == target

    def tap(self, name, src_ap, free_shape):
        if name not in self.dbg_t:
            return
        dst = self.dbg_t[name]

        src_ap = src_ap[:]

        @self.block.gpsimd
        def _(g_):
            self.inc("st", g_.dma_start(out=dst, in_=src_ap), 16)
            g_.wait_ge(self.sm["st"][0], self.n("st"))

    def build(self):
        self.declare_io()
        for nm, shp in (("lntm", [4, D]), ("lnfm2", [8, S]), ("kfm3", [G, S]), ("kfm2", [G, S]), ("vtm2", [NT, D]), ("qfm2", [G, S]), ("ped", [2048]), ("scd", [2048]), ("scd1", [2048]), ("statd", [NT, 8]), ("lnfm", [8, S]), ("kfm", [G, S]), ("vtm", [NT, D]),
                        ("qfm", [G, S]), ("x1", [NT, D]), ("ssd", [NT, 2 * H * E]),
                        ("msk", [2, NT, H * E]), ("qh", [NT, 512]), ("kh", [NT, 512]),
                        ("vtmf", [NT, 528]), ("pe", [2, 2048])):
            self.declare_dbg(nm, shp)
        self.alloc()
        with self.es:
            self.block = self.es.enter_context(self.nc.Block())
            self.p0_consts()
            stop = False
            for s in range(self.nseq):
                self.seq = s
                phases = [
                    ("ln1", lambda: (self.ln_phase(src_hbm=True), self.barrier())),
                    ("tap1", lambda: ((self.tap("lnfm", self.lnfm, None), self.tap("statd", getattr(self, "statd", None), None) if "statd" in self.dbg_t else None) if s == 0 else None)),
                    ("a2", lambda: (self.a2_proj(), self.barrier())),
                    ("tap2", lambda: ([self.tap(n_, getattr(self, n_), None) for n_ in ("kfm", "vtm", "qfm")] + [self.tap("lnfm2", self.lnfm, None)] if s == 0 else None)),
                    ("a3", lambda: (self.a3_attn(), self.barrier(with_sync=True))),
                    ("tap3", lambda: ((self.tap("kfm2", self.kfm, None), self.tap("vtm2", self.vtm, None), self.tap("qfm2", self.qfm, None), self.tap("x1", self.X, None), self.tap("ped", getattr(self, "ped", None), None) if "ped" in self.dbg_t else None, self.tap("scd", getattr(self, "scd", None), None) if "scd" in self.dbg_t else None, self.tap("scd1", getattr(self, "scd1", None), None) if "scd1" in self.dbg_t else None) if s == 0 else None)),
                    ("ln2", lambda: (self.ln_phase(src_hbm=False), self.barrier())),
                    ("b1", lambda: (self.b1_router(), self.barrier())),
                    ("tap4", lambda: ([self.tap(n_, getattr(self, n_), None) for n_ in ("ssd", "msk")] if s == 0 else None)),
                    ("b2", lambda: (self.b2_projgate(), self.barrier())),
                    ("tap5", lambda: ([self.tap(n_, getattr(self, n_), None) for n_ in ("qh", "kh", "vtmf")] if s == 0 else None)),
                    ("b3", lambda: (self.b3_transpose(), self.barrier())),
                    ("b4", lambda: (self.b4_attn(), self.barrier(with_sync=True))),
                ]
                for pname, pf in phases:
                    pf()
                    if self.stop_after is not None and pname == self.stop_after:
                        stop = True
                        break
                if stop:
                    break

            @self.block.sync
            def _(sy):
                self.w(sy, "st")
                self.w(sy, "so0")
                self.w(sy, "so1")

    def p0_consts(self):
        blk = self.block

        @blk.gpsimd
        def _(g_):
            g_.iota(self.iota_scr[:, 0:128], pattern=[[1, 128]], base=0,
                    channel_multiplier=0, allow_small_or_imprecise_dtypes=True)
            g_.iota(self.iota_scr[:, 128:256], pattern=[[0, 128]], base=0,
                    channel_multiplier=1, allow_small_or_imprecise_dtypes=True)
            g_.memset(self.ones1[:], 1.0)
            self.inc("ldc", g_.dma_start(out=self.bqk[:, 0:8],
                     in_=self.bq_a.rearrange("(g p) -> p g", p=128),
                     allow_slow_non_contiguous=True), 16)
            self.inc("ldc", g_.dma_start(out=self.bqk[:, 8:16],
                     in_=self.bk_a.rearrange("(g p) -> p g", p=128),
                     allow_slow_non_contiguous=True), 16)
            self.inc("ldc", g_.dma_start(out=self.bvv[:], in_=self.bv_a.unsqueeze(0)), 16)
            for dc in range(8):
                self.inc("ldc", g_.dma_start(out=self.wsd[:, dc, 0:40],
                         in_=self.ws_f[dc * 128:(dc + 1) * 128, :]), 16)
                self.inc("ldc", g_.dma_start(out=self.wsd[:, dc, 40:80],
                         in_=self.wd_f[dc * 128:(dc + 1) * 128, :]), 16)
            self.inc("ggen", g_.drain())

        @blk.vector
        def _(v):
            self.w(v, "ggen", 1)
            tt = v.tensor_tensor(out=self.ident[:], in0=self.iota_scr[:, 0:128],
                                 in1=self.iota_scr[:, 128:256], op=ALU.is_equal)
            self.inc("vgen", tt)

        @blk.tensor
        def _(t):
            self.w(t, "vgen", self.n("vgen"))

    def ln_phase(self, src_hbm):
        blk, s = self.block, self.seq
        lxb = [self.n(f"lx{k}") for k in range(4)]
        sg0 = self.n("sgen")
        vg0 = self.n("vgen")
        tg0 = self.n("tgp")
        vr0 = self.n("vres")

        if src_hbm and "lntm" in self.dbg_t and self.seq == 0:
            @blk.gpsimd
            def _(g_):
                self.w(g_, "sgen", sg0 + 48)
                self.inc("st", g_.dma_start(out=self.dbg_t["lntm"], in_=self.ln_tm), 16)
                g_.wait_ge(self.sm["st"][0], self.n("st"))

        if src_hbm:
            @blk.sync
            def _(sy):
                for nt in range(NT):
                    if nt >= 4:
                        self.w(sy, "sgen", sg0 + 3 * (nt - 4) + 3)
                    self.inc(f"lx{nt % 4}", sy.dma_start(out=self.xstage[:, nt % 4, :],
                             in_=self.xc[s, nt * 128:(nt + 1) * 128, :]), 16)

        @blk.scalar
        def _(s_):
            for nt in range(NT):
                xin = self.xstage[:, nt % 4, :] if src_hbm else self.X[:, nt, :]
                if src_hbm:
                    self.w(s_, f"lx{nt % 4}", lxb[nt % 4] + 16 * (nt // 4 + 1))
                a = s_.activation(out=self.sq[:], in_=xin, func=AF.Square,
                                  accum_out=self.stat[:, 1:2])
                self.inc("sgen", a)
                self.w(s_, "vgen", vg0 + 2 * nt + 1)
                a = s_.activation(out=self.stat[:, 5:6], in_=self.stat[:, 3:4],
                                  func=AF.Sqrt)
                self.inc("sgen", a)
                self.w(s_, "vgen", vg0 + 2 * nt + 2)
                if nt >= 4:
                    self.w(s_, "tgp", tg0 + ((nt - 4) // 4 + 1) * 8)
                s_.activation(out=self.ln_tm[:, nt % 4, :], in_=xin, func=AF.Identity,
                              bias=self.stat[:, 7:8], scale=self.stat[:, 6:7])
                dr = s_.drain()
                self.inc("sgen", dr)

        @blk.vector
        def _(v):
            for nt in range(NT):
                xin = self.xstage[:, nt % 4, :] if src_hbm else self.X[:, nt, :]
                if src_hbm:
                    self.w(v, f"lx{nt % 4}", lxb[nt % 4] + 16 * (nt // 4 + 1))
                v.reduce_sum(self.stat[:, 0:1], xin, axis=AX.X)
                self.w(v, "sgen", sg0 + 3 * nt + 1)
                v.drain()
                v.tensor_scalar(out=self.stat[:, 2:3], in0=self.stat[:, 0:1],
                                scalar1=1.0 / D, scalar2=None, op0=ALU.mult)
                v.tensor_scalar(out=self.stat[:, 3:4], in0=self.stat[:, 1:2],
                                scalar1=1.0 / D, scalar2=None, op0=ALU.mult)
                v.drain()
                v.tensor_tensor(out=self.stat[:, 4:5], in0=self.stat[:, 2:3],
                                in1=self.stat[:, 2:3], op=ALU.mult)
                v.drain()
                v.tensor_tensor(out=self.stat[:, 3:4], in0=self.stat[:, 3:4],
                                 in1=self.stat[:, 4:5], op=ALU.subtract)
                v.drain()
                tt = v.tensor_scalar(out=self.stat[:, 3:4], in0=self.stat[:, 3:4],
                                     scalar1=float(EPS), scalar2=None, op0=ALU.add)
                v.drain()
                self.inc("vgen", tt)
                self.w(v, "sgen", sg0 + 3 * nt + 2)
                v.reciprocal(out=self.stat[:, 6:7], in_=self.stat[:, 5:6])
                v.drain()
                tt = v.scalar_tensor_tensor(out=self.stat[:, 7:8], in0=self.stat[:, 2:3],
                                            scalar=-1.0, in1=self.stat[:, 6:7],
                                            op0=ALU.mult, op1=ALU.mult)
                v.drain()
                self.inc("vgen", tt)
                if "statd" in self.dbg_t and self.seq == 0 and src_hbm:
                    v.drain()
                    v.tensor_copy(out=self.statd[:, nt, :], in_=self.stat[:])

                if nt % 4 == 3:
                    tg = nt // 4
                    for dc in range(8):
                        self.w(v, "tgp", tg0 + tg * 8 + dc + 1)
                        base = (dc % 4) * 2048 + (dc // 4) * 512
                        cp = v.tensor_copy(
                            out=self.lnfm[:, dc, (4 * tg) * 128:(4 * tg + 4) * 128],
                            in_=self.PSB[:, base:base + 512])
                        self.inc("vres", cp)

        @blk.tensor
        def _(t):
            for tg in range(NT // 4):
                self.w(t, "sgen", sg0 + 3 * (4 * tg + 4))
                for dc in range(8):
                    gidx = tg * 8 + dc
                    if gidx >= 4:
                        self.w(t, "vres", vr0 + gidx - 3)
                    base = (dc % 4) * 2048 + (dc // 4) * 512
                    for j in range(4):
                        t.transpose(self.PSB[:, base + j * 128: base + (j + 1) * 128],
                                    self.ln_tm[:, j, dc * 128:(dc + 1) * 128],
                                    self.ident[:])
                    self.inc("tgp", t.drain())

    def a2_proj(self):
        blk = self.block
        if getattr(self, "dbg_ln_reload", False) and self.seq == 0:
            @blk.gpsimd
            def _(g_):
                for dc in range(8):
                    self.inc("ldw", g_.dma_start(out=self.lnfm[:, dc, :],
                             in_=self.dbg_ln_in[:, dc, :]), 16)
                self.inc("ggen", g_.nop())

            @blk.tensor
            def _(t):
                self.w(t, "ldw")
                self.w(t, "ggen")
        lwkb = [self.n(f"lwk{k}") for k in range(4)]
        lwvb = [self.n(f"lwv{k}") for k in range(2)]
        tg0 = self.n("tgp")
        se0 = self.n("seva")

        @blk.gpsimd
        def _(g_):
            for it, wsrc in enumerate((self.wk_a, self.wq_a)):
                for g in range(G):
                    gi = it * G + g
                    if gi >= 4:
                        self.w(g_, "tgp", tg0 + (gi - 3) * 2)
                    for dc in range(8):
                        self.inc(f"lwk{gi % 4}", g_.dma_start(
                            out=self.wk_ring[:, gi % 4, dc, :],
                            in_=wsrc[dc * 128:(dc + 1) * 128, g * 128:(g + 1) * 128]), 16)
            for ntp in range(8):
                for dc in range(8):
                    idx = ntp * 8 + dc
                    if idx >= 2:
                        i2 = idx - 2
                        self.w(g_, "tgp", tg0 + 32 + (i2 // 8) * 9 + (i2 % 8) + 1)
                    self.inc(f"lwv{idx % 2}", g_.dma_start(
                        out=self.wv_ring[:, idx % 2, :],
                        in_=self.wv_a[dc * 128:(dc + 1) * 128, :]), 16)

        @blk.tensor
        def _(t):
            self.w(t, "ldc")
            for it, dst_bias in enumerate(((self.kfm, 8), (self.qfm, 0))):
                for g in range(G):
                    gi = it * G + g
                    self.w(t, f"lwk{gi % 4}", lwkb[gi % 4] + 16 * 8 * (gi // 4 + 1))
                    for np_ in range(2):
                        m = gi * 2 + np_
                        if m >= 2:
                            self.w(t, "seva", se0 + (m - 1) * 2)
                        for dc in range(8):
                            for nbl in range(2):
                                nb = np_ * 2 + nbl
                                bank = (np_ * 2 + nbl) * 512
                                mm = t.matmul(self.PS[:, bank:bank + 512],
                                              self.wk_ring[:, gi % 4, dc, :],
                                              self.lnfm[:, dc, nb * 512:(nb + 1) * 512],
                                              start=(dc == 0), stop=(dc == 7))
                        self.inc("tgp", mm)
            for ntp in range(8):
                for dc in range(8):
                    idx = ntp * 8 + dc
                    self.w(t, f"lwv{idx % 2}", lwvb[idx % 2] + 16 * (idx // 2 + 1))
                    if dc == 0:
                        if ntp >= 2:
                            self.w(t, "seva", se0 + 64 + (ntp - 1) * 2)
                        else:
                            self.w(t, "seva", se0 + 64)
                    for t01 in range(2):
                        nt = ntp * 2 + t01
                        rbase = (ntp % 2) * 2048 + t01 * 1024
                        for half in range(2):
                            mm = t.matmul(self.PS[:, rbase + half * 512: rbase + (half + 1) * 512],
                                          self.lnfm[:, dc, nt * 128:(nt + 1) * 128],
                                          self.wv_ring[:, idx % 2, half * 512:(half + 1) * 512],
                                          start=(dc == 0), stop=False)
                    self.inc("tgp", mm)
                for t01 in range(2):
                    rbase = (ntp % 2) * 2048 + t01 * 1024
                    for half in range(2):
                        mm = t.matmul(self.PS[:, rbase + half * 512: rbase + (half + 1) * 512],
                                      self.ones1[:],
                                      self.bvv[0:1, half * 512:(half + 1) * 512],
                                      start=False, stop=True)
                self.inc("tgp", mm)

        @blk.vector
        def _(v):
            self.w(v, "ldc")
            for it, (dst, boff) in enumerate(((self.kfm, 8), (self.qfm, 0))):
                for g in range(G):
                    gi = it * G + g
                    for np_ in range(2):
                        m = gi * 2 + np_
                        self.w(v, "tgp", tg0 + m + 1)
                        for nbl in range(2):
                            nb = np_ * 2 + nbl
                            bank = (np_ * 2 + nbl) * 512
                            a = v.tensor_scalar(out=dst[:, g, nb * 512:(nb + 1) * 512],
                                                in0=self.PS[:, bank:bank + 512],
                                                scalar1=self.bqk[:, boff + g:boff + g + 1],
                                                scalar2=None, op0=ALU.add)
                            self.inc("seva", a)
            for ntp in range(8):
                self.w(v, "tgp", tg0 + 32 + ntp * 9 + 9)
                for t01 in range(2):
                    nt = ntp * 2 + t01
                    rbase = (ntp % 2) * 2048 + t01 * 1024
                    a = v.tensor_copy(out=self.vtm[:, nt, :],
                                      in_=self.PS[:, rbase:rbase + 1024])
                    self.inc("seva", a)

    def a3_attn(self):
        blk, s = self.block, self.seq
        if getattr(self, "dbg_qkv_reload", False) and s == 0:
            import os as _os
            _which = _os.environ.get("RELOAD", "kqv")

            @blk.gpsimd
            def _(g_):
                for dc in range(8):
                    if "k" in _which:
                        self.inc("ldw", g_.dma_start(out=self.kfm[:, dc, :],
                                 in_=self.dbg_kfm_in[:, dc, :]), 16)
                    if "q" in _which:
                        self.inc("ldw", g_.dma_start(out=self.qfm[:, dc, :],
                                 in_=self.dbg_qfm_in[:, dc, :]), 16)
                if "v" in _which:
                    for ti in range(NT):
                        self.inc("ldw", g_.dma_start(out=self.vtm[:, ti, :],
                                 in_=self.dbg_vtm_in[:, ti, :]), 16)
                self.inc("ggen", g_.nop())

            @blk.tensor
            def _(t):
                self.w(t, "ldw")
                self.w(t, "ggen")
        lxb = [self.n(f"lx{k}") for k in range(4)]
        tsc0 = self.n("tsc")
        tcx0 = self.n("tcx")
        sx0 = self.n("sexp")
        vs0 = self.n("vsm")
        vr0 = self.n("vres")

        @blk.sync
        def _(sy):
            for nt in range(NT):
                if nt >= 4:
                    self.w(sy, "vres", vr0 + nt - 3)
                self.inc(f"lx{nt % 4}", sy.dma_start(out=self.xstage[:, nt % 4, :],
                         in_=self.xc[s, nt * 128:(nt + 1) * 128, :]), 16)

        if "kfm3" in self.dbg_t and self.seq == 0:
            @blk.gpsimd
            def _(g_):
                self.w(g_, "vres", vr0 + 2)
                self.inc("st", g_.dma_start(out=self.dbg_t["kfm3"], in_=self.kfm), 16)

        morder = list(range(16))
        if getattr(self, "rev_m", False):
            morder = morder[::-1]

        gg0 = self.n("ggen")

        def ctx_mm(t, i, mi_, m, g, j):
            return t.matmul(self.PS[:, 2048 + j * 1024 + g * 128:
                                    2048 + j * 1024 + (g + 1) * 128],
                            self.pe_buf[:, i % 2, g * 256 + j * 128:
                                        g * 256 + (j + 1) * 128],
                            self.vtm[:, m, g * 128:(g + 1) * 128],
                            start=(mi_ == 0 and g % 4 == 0),
                            stop=(mi_ == 15 and g % 4 == 3))

        def emit_ctx(t, ip):
            nbp, mip = ip // 16, ip % 16
            mp = morder[mip]
            if mip == 0 and nbp > 0:
                self.w(t, "vres", vr0 + nbp * 2)
            self.w(t, "vsm", vs0 + ip + 1)
            for j in range(2):
                for g in range(6):
                    mm = ctx_mm(t, ip, mip, mp, g, j)
            self.w(t, "ggen", gg0 + ip + 1)
            for j in range(2):
                for g in range(6, G):
                    mm = ctx_mm(t, ip, mip, mp, g, j)
            self.inc("tcx", mm)

        @blk.tensor
        def _(t):
            for nb in range(NB):
                for mi_, m in enumerate(morder):
                    i = nb * 16 + mi_
                    if i > 0:
                        self.w(t, "sexp", sx0 + 2 * i - 1)
                    for g in range(4):
                        mm = t.matmul(self.PS[:, g * 256:(g + 1) * 256],
                                      self.kfm[:, g, m * 128:(m + 1) * 128],
                                      self.qfm[:, g, nb * 256:(nb + 1) * 256],
                                      start=True, stop=True)
                    self.inc("tsc", mm)
                    if i > 0:
                        self.w(t, "sexp", sx0 + 2 * i)
                    for g in range(4, G):
                        mm = t.matmul(self.PS[:, g * 256:(g + 1) * 256],
                                      self.kfm[:, g, m * 128:(m + 1) * 128],
                                      self.qfm[:, g, nb * 256:(nb + 1) * 256],
                                      start=True, stop=True)
                    self.inc("tsc", mm)
                    if i > 0:
                        emit_ctx(t, i - 1)
            emit_ctx(t, NB * 16 - 1)

        @blk.scalar
        def _(s_):
            for nb in range(NB):
                for m in range(16):
                    i = nb * 16 + m
                    if i >= 2:
                        self.w(s_, "tcx", tcx0 + i - 1)
                    self.w(s_, "tsc", tsc0 + 2 * i + 1)
                    a = s_.activation(out=self.pe_buf[:, i % 2, 0:1024],
                                      in_=self.PS[:, 0:1024], func=AF.Exp,
                                      scale=1.0 / 32.0)
                    self.inc("sexp", a)
                    self.w(s_, "tsc", tsc0 + 2 * i + 2)
                    a = s_.activation(out=self.pe_buf[:, i % 2, 1024:2048],
                                      in_=self.PS[:, 1024:2048], func=AF.Exp,
                                      scale=1.0 / 32.0)
                    self.inc("sexp", a)

        @blk.vector
        def _(v):
            for nb in range(NB):
                for m in range(16):
                    i = nb * 16 + m
                    pe = self.pe_buf[:, i % 2, :]
                    t0, t1, t2, t3 = (self.pe_scr[:, k * 256:(k + 1) * 256] for k in range(4))
                    self.w(v, "sexp", sx0 + 2 * i + 1)
                    v.tensor_tensor(out=t0, in0=pe[:, 0:256], in1=pe[:, 256:512], op=ALU.add)
                    v.tensor_tensor(out=t1, in0=pe[:, 512:768], in1=pe[:, 768:1024], op=ALU.add)
                    self.w(v, "sexp", sx0 + 2 * i + 2)
                    v.tensor_tensor(out=t2, in0=pe[:, 1024:1280], in1=pe[:, 1280:1536], op=ALU.add)
                    v.tensor_tensor(out=t3, in0=pe[:, 1536:1792], in1=pe[:, 1792:2048], op=ALU.add)
                    v.drain()
                    v.tensor_tensor(out=t0, in0=t0, in1=t1, op=ALU.add)
                    v.tensor_tensor(out=t2, in0=t2, in1=t3, op=ALU.add)
                    v.drain()
                    with self.nc.allow_low_precision(reason="bf16 softmax scale"):
                        v.tensor_tensor(out=t1, in0=t0, in1=t2, op=ALU.add)
                        v.drain()
                        if i > 0:
                            self.w(v, "ggen", gg0 + i)
                        v.reciprocal(out=self.r_buf[:], in_=t1)
                    v.drain()
                    for g in range(6):
                        tt = v.tensor_tensor(out=pe[:, g * 256:(g + 1) * 256],
                                             in0=pe[:, g * 256:(g + 1) * 256],
                                             in1=self.r_buf[:], op=ALU.mult)
                    self.inc("vsm", tt)
                self.w(v, "tcx", tcx0 + nb * 16 + 16)
                for j in range(2):
                    nt = nb * 2 + j
                    self.w(v, f"lx{nt % 4}", lxb[nt % 4] + 16 * (nt // 4 + 1))
                    tt = v.tensor_tensor(out=self.X[:, nt, :],
                                         in0=self.PS[:, 2048 + j * 1024:2048 + (j + 1) * 1024],
                                         in1=self.xstage[:, nt % 4, :], op=ALU.add)
                    self.inc("vres", tt)

        @blk.gpsimd
        def _(g_):
            for nb in range(NB):
                for m in range(16):
                    i = nb * 16 + m
                    pe = self.pe_buf[:, i % 2, :]
                    self.w(g_, "vsm", vs0 + i + 1)
                    for g in range(6, G):
                        tt = g_.tensor_tensor(out=pe[:, g * 256:(g + 1) * 256],
                                              in0=pe[:, g * 256:(g + 1) * 256],
                                              in1=self.r_buf[:], op=ALU.mult)
                    self.inc("ggen", tt)

    def b1_router(self):
        blk = self.block
        tg0 = self.n("tgp")
        sg0 = self.n("sgen")

        @blk.tensor
        def _(t):
            self.w(t, "ldc")
            for nt in range(NT):
                if nt >= 2:
                    self.w(t, "sgen", sg0 + nt - 1)
                r = (nt % 2) * 512
                for dc in range(8):
                    mm = t.matmul(self.PS[:, r:r + 80],
                                  self.lnfm[:, dc, nt * 128:(nt + 1) * 128],
                                  self.wsd[:, dc, :],
                                  start=(dc == 0), stop=(dc == 7))
                self.inc("tgp", mm)

        @blk.scalar
        def _(s_):
            for nt in range(NT):
                self.w(s_, "tgp", tg0 + nt + 1)
                r = (nt % 2) * 512
                a = s_.activation(out=self.ssd[:, nt, :], in_=self.PS[:, r:r + 80],
                                  func=AF.Sigmoid)
                self.inc("sgen", a)

        @blk.vector
        def _(v):
            self.w(v, "sgen", sg0 + NT)
            for gi in range(2):
                gsl = self.ssd[:, :, gi * 40:(gi + 1) * 40].rearrange(
                    "p t (h e) -> p t h e", e=E)
                for ei in range(E):
                    jj = 0
                    for ej in range(E):
                        if ej == ei:
                            continue
                        op = ALU.is_ge if ej < ei else ALU.is_gt
                        v.tensor_tensor(out=self.cmp[:, jj, :, :].unsqueeze(3),
                                        in0=gsl[:, :, :, ej:ej + 1],
                                        in1=gsl[:, :, :, ei:ei + 1], op=op)
                        jj += 1
                    v.drain()
                    v.tensor_tensor(out=self.cnt[:], in0=self.cmp[:, 0, :, :],
                                    in1=self.cmp[:, 1, :, :], op=ALU.add)
                    v.drain()
                    v.tensor_tensor(out=self.cnt[:], in0=self.cnt[:],
                                    in1=self.cmp[:, 2, :, :], op=ALU.add)
                    v.drain()
                    v.tensor_tensor(out=self.cnt[:], in0=self.cnt[:],
                                    in1=self.cmp[:, 3, :, :], op=ALU.add)
                    v.drain()
                    mview = self.msk[:, gi, :, :].rearrange(
                        "p t (h e) -> p t h e", e=E)[:, :, :, ei:ei + 1]
                    v.tensor_scalar(out=mview, in0=self.cnt[:].unsqueeze(3),
                                    scalar1=2.5, scalar2=None, op0=ALU.is_le)
                    v.drain()
            v.tensor_tensor(out=self.ssg[:], in0=self.ssd[:, :, 0:40],
                            in1=self.msk[:, 0, :, :], op=ALU.mult)
            v.tensor_tensor(out=self.sdg[:], in0=self.ssd[:, :, 40:80],
                            in1=self.msk[:, 1, :, :], op=ALU.mult)
            vv = self.vtmf.rearrange("p t (h c) -> p t h c", c=66)
            tt = v.memset(vv[:, :, :, 64:65], 1.0)
            self.inc("vgen", tt)

    def b2_projgate(self):
        blk = self.block
        lwsb = [self.n(f"lws{k}") for k in range(2)]
        tg0 = self.n("tgp")
        vg0 = self.n("vgen")

        blocks = []
        for h in range(H):
            blocks.append((self.wq_f, h * 320, self.sdg, self.qh, h * 64, 64))
        for h in range(H):
            blocks.append((self.wkv_f, h * 320, self.ssg, self.kh, h * 64, 64))
        for h in range(H):
            blocks.append((self.wkv_f, 2560 + h * 320, self.ssg, self.vtmf, h * 66, 66))

        @blk.gpsimd
        def _(g_):
            for bi_, (wsrc, coff, _, _, _, _) in enumerate(blocks):
                if bi_ >= 2:
                    self.w(g_, "vgen", vg0 + (bi_ - 2) * 4 + 4)
                for dc in range(8):
                    self.inc(f"lws{bi_ % 2}", g_.dma_start(
                        out=self.wslc[:, bi_ % 2, dc, :],
                        in_=wsrc[dc * 128:(dc + 1) * 128, coff:coff + 320]), 16)

        @blk.tensor
        def _(t):
            for bi_ in range(24):
                self.w(t, f"lws{bi_ % 2}", lwsb[bi_ % 2] + 16 * 8 * (bi_ // 2 + 1))
                for tg in range(4):
                    idx = bi_ * 4 + tg
                    if idx >= 2:
                        self.w(t, "vgen", vg0 + idx - 1)
                    r = (tg % 2) * 2048
                    for t_ in range(4):
                        nt = tg * 4 + t_
                        for dc in range(8):
                            mm = t.matmul(self.PS[:, r + t_ * 512: r + t_ * 512 + 320],
                                          self.lnfm[:, dc, nt * 128:(nt + 1) * 128],
                                          self.wslc[:, bi_ % 2, dc, :],
                                          start=(dc == 0), stop=(dc == 7))
                    self.inc("tgp", mm)

        @blk.vector
        def _(v):
            for bi_, (wsrc, coff, gates, dst, dcol, dstride) in enumerate(blocks):
                for tg in range(4):
                    idx = bi_ * 4 + tg
                    self.w(v, "tgp", tg0 + idx + 1)
                    r = (tg % 2) * 2048
                    psin = self.PS[:, r:r + 2048].rearrange(
                        "p (t c) -> p t c", c=512)[:, :, 0:320].rearrange(
                        "p t (e d) -> p t e d", d=64)
                    gin = gates[:, tg * 4:(tg + 1) * 4,
                                (coff % 2560) // 320 * 5:(coff % 2560) // 320 * 5 + 5]
                    gin = gin.unsqueeze(3).to_broadcast([128, 4, 5, 64])
                    gb = self.gated[:, idx % 2, :, :, :]
                    tt = v.tensor_tensor(out=gb, in0=psin, in1=gin, op=ALU.mult)
                    self.inc("vgen", tt)
                    v.drain()
                    dv = dst[:, tg * 4:(tg + 1) * 4, dcol:dcol + 64]
                    with self.nc.allow_low_precision(reason="bf16 gated sum"):
                        v.reduce_sum(dv.unsqueeze(3),
                                     gb.rearrange("p t e c -> p t c e"), axis=AX.X)

    def b3_transpose(self):
        blk = self.block
        tg0 = self.n("tgp")
        vg0 = self.n("vgen")

        @blk.tensor
        def _(t):
            for ti, (src, dst) in enumerate(((self.qh, self.qfmb), (self.kh, self.kfmb))):
                for ntg in range(4):
                    for p in range(4):
                        idx = (ti * 4 + ntg) * 4 + p
                        if idx >= 2:
                            self.w(t, "vgen", vg0 + idx - 1)
                        base = (p % 2) * 2048
                        for tt_ in range(4):
                            nt = ntg * 4 + tt_
                            tp = t.transpose(self.PSB[:, base + tt_ * 128: base + (tt_ + 1) * 128],
                                             src[:, nt, p * 128:(p + 1) * 128],
                                             self.ident[:])
                        self.inc("tgp", tp)

        @blk.vector
        def _(v):
            for ti, (src, dst) in enumerate(((self.qh, self.qfmb), (self.kh, self.kfmb))):
                for ntg in range(4):
                    for p in range(4):
                        idx = (ti * 4 + ntg) * 4 + p
                        self.w(v, "tgp", tg0 + idx + 1)
                        base = (p % 2) * 2048
                        cp = v.tensor_copy(out=dst[:, p, ntg * 512:(ntg + 1) * 512],
                                           in_=self.PSB[:, base:base + 512])
                        self.inc("vgen", cp)

    def b4_attn(self):
        blk, s = self.block, self.seq
        ldw0 = self.n("ldw")

        @blk.gpsimd
        def _(g_):
            for e_ in range(2):
                self.inc("ldw", g_.dma_start(out=self.woc[0][e_ * 64:(e_ + 1) * 64, :],
                         in_=self.wo_f[:, e_ * D:(e_ + 1) * D]), 16)
                self.inc("ldw", g_.dma_start(out=self.woc[1][e_ * 64:(e_ + 1) * 64, :],
                         in_=self.wo_f[:, (2 + e_) * D:(3 + e_) * D]), 16)
            self.inc("ldw", g_.dma_start(out=self.woc[2][0:64, :], in_=self.wo_f[:, 4 * D:5 * D]), 16)

        tsc0 = self.n("tsc")
        tcx0 = self.n("tcx")
        tg0 = self.n("tgp")
        sx0 = self.n("sexp")
        se0 = self.n("seva")
        vs0 = self.n("vsm")
        vd0 = self.n("vdr")
        vy0 = self.n("vyh")
        sob = [self.n(f"so{k}") for k in range(2)]
        ztb = [self.n("zt0"), self.n("zt1")]

        OB, YB = 2048, 3584
        ocol = lambda q: OB + (q // 7) * 512 + (q % 7) * 65

        def o_group(t, i):
            nb_, jc_ = i // 16, i % 16
            if jc_ == 0 and nb_ > 0:
                self.w(t, "seva", se0 + nb_ * 2)
                self.w(t, "vdr", vd0 + nb_ * 2)
            for j in range(2):
                for h in range(H):
                    q = j * 8 + h
                    mm = t.matmul(self.PS[:, ocol(q):ocol(q) + 65],
                                  self.pe_buf[:, i % 2, hpos(h) + j * 128:
                                              hpos(h) + (j + 1) * 128],
                                  self.vtmf[:, jc_, h * 66:h * 66 + 65],
                                  start=(jc_ == 0 and q % 7 == 0),
                                  stop=(jc_ == 15 and (q % 7 == 6 or q == 15)))
            self.inc("tcx", mm)

        def y_piece(t, nb_, pidx):
            j, half = pidx // 2, pidx % 2
            nt = nb_ * 2 + j
            self.w(t, f"zt{nt % 2}", ztb[nt % 2] + 48 * (nt // 2 + 1))
            self.w(t, "vyh", vy0 + 2 * nt + half)
            if nb_ == 0 and pidx == 0:
                self.w(t, "ldw", ldw0 + 16 * 5)
            for cch in range(3):
                if cch < 2:
                    lhs = self.zt[:, nt % 2, cch * 128:(cch + 1) * 128]
                    rhsw = self.woc[cch][:, half * 512:(half + 1) * 512]
                else:
                    lhs = self.zt[0:64, nt % 2, 256:384]
                    rhsw = self.woc[2][0:64, half * 512:(half + 1) * 512]
                mm = t.matmul(self.PS[:, YB:YB + 512], lhs, rhsw,
                              start=(cch == 0), stop=(cch == 2))
            self.inc("tgp", mm)

        @blk.tensor
        def _(t):
            for nb in range(NB):
                for jc in range(16):
                    i = nb * 16 + jc
                    self.w(t, "sexp", sx0 + i)
                    for p in range(4):
                        h0, h1 = 2 * p, 2 * p + 1
                        t.matmul(self.PS[:, hpos(h0):hpos(h0) + 256],
                                 self.kfmb[0:64, p, jc * 128:(jc + 1) * 128],
                                 self.qfmb[0:64, p, nb * 256:(nb + 1) * 256],
                                 start=True, stop=True)
                        mm = t.matmul(self.PS[:, hpos(h1):hpos(h1) + 256],
                                      self.kfmb[64:128, p, jc * 128:(jc + 1) * 128],
                                      self.qfmb[64:128, p, nb * 256:(nb + 1) * 256],
                                      start=True, stop=True, tile_position=(64, 0))
                    self.inc("tsc", mm)
                    if nb > 0 and jc in (5, 7, 9, 11):
                        y_piece(t, nb - 1, (jc - 5) // 2)
                    if jc > 0:
                        o_group(t, i - 1)
                self.w(t, "sexp", sx0 + nb * 16 + 16)
                o_group(t, nb * 16 + 15)
            for pidx in range(4):
                y_piece(t, NB - 1, pidx)

        @blk.scalar
        def _(s_):
            for nb in range(NB):
                for jc in range(16):
                    i = nb * 16 + jc
                    self.w(s_, "tsc", tsc0 + i + 1)
                    if i >= 2:
                        self.w(s_, "tcx", tcx0 + i - 1)
                    a = s_.activation(out=self.pe_buf[:, i % 2, :],
                                      in_=self.PS[:, 0:2048], func=AF.Exp,
                                      scale=0.125)
                    self.inc("sexp", a)
                self.w(s_, "tcx", tcx0 + nb * 16 + 16)
                for j in range(2):
                    nt = nb * 2 + j
                    if nt >= 2:
                        self.w(s_, "vsm", vs0 + nt - 1)
                    if j == 0:
                        segs = (((2048, 7), 0), ((2560, 1), 448))
                    else:
                        segs = (((2625, 6), 0), ((3072, 2), 384))
                    for (pc, nq), oc in segs:
                        pin = self.PS[:, pc:pc + nq * 65].rearrange(
                            "p (q c) -> p q c", c=65)[:, :, 0:64]
                        a = s_.activation(
                            out=self.o_sb[:, nt % 2, oc:oc + nq * 64].rearrange(
                                "p (q c) -> p q c", c=64),
                            in_=pin, func=AF.Copy)
                    self.inc("seva", a)

        @blk.vector
        def _(v):
            v.memset(self.z_b[:, :, 320:384], 0.0)
            for nb in range(NB + 1):
                if nb > 0:
                    for j in range(2):
                        nt = (nb - 1) * 2 + j
                        for half in range(2):
                            self.w(v, "tgp", tg0 + 2 * nt + half + 1)
                            if half == 0 and nt >= 2:
                                self.w(v, f"so{nt % 2}",
                                       sob[nt % 2] + 16 * (nt // 2))
                            tt = v.tensor_tensor(
                                out=self.out_stg[:, nt % 2, half * 512:(half + 1) * 512],
                                in0=self.PS[:, YB:YB + 512],
                                in1=self.X[:, nt, half * 512:(half + 1) * 512],
                                op=ALU.add)
                            self.inc("vyh", tt)
                if nb == NB:
                    break
                self.w(v, "tcx", tcx0 + nb * 16 + 16)
                for j in range(2):
                    nt = nb * 2 + j
                    if j == 0:
                        segs = (((2048 + 64, 7), 0), ((2624, 1), 7))
                    else:
                        segs = (((2625 + 64, 6), 0), ((3136, 2), 6))
                    for (pc, nq), rc in segs:
                        if nq > 1:
                            pv = self.PS[:, pc:pc + nq * 65].rearrange(
                                "p (q c) -> p q c", c=65)[:, :, 0:1]
                            rr = v.reciprocal(
                                out=self.r8[:, nt % 2, rc:rc + nq].unsqueeze(2),
                                in_=pv)
                        else:
                            rr = v.reciprocal(out=self.r8[:, nt % 2, rc:rc + 1],
                                              in_=self.PS[:, pc:pc + 1])
                    self.inc("vdr", rr)
                v.drain()
                for j in range(2):
                    nt = nb * 2 + j
                    for h in range(H):
                        v.tensor_scalar(out=self.mdr[:, nt % 2, h * 5:(h + 1) * 5],
                                        in0=self.msk[:, 1, nt, h * 5:(h + 1) * 5],
                                        scalar1=self.r8[:, nt % 2, h:h + 1],
                                        scalar2=None, op0=ALU.mult)
                    v.drain()
                    self.w(v, "seva", se0 + nt + 1)
                    ob = self.o_sb[:, nt % 2, :].rearrange(
                        "p (h c) -> p h c", c=64).unsqueeze(2).to_broadcast(
                        [128, H, E, 64])
                    mb = self.mdr[:, nt % 2, :].rearrange(
                        "p (h e) -> p h e", e=E).unsqueeze(3).to_broadcast(
                        [128, H, E, 64])
                    v.tensor_tensor(out=self.z_acc.rearrange(
                        "p h (e c) -> p h e c", c=64), in0=ob, in1=mb, op=ALU.mult)
                    v.drain()
                    if nt >= 2:
                        self.w(v, f"zt{nt % 2}", ztb[nt % 2] + 48 * (nt // 2))
                    with self.nc.allow_low_precision(reason="fp32 accum, bf16 out"):
                        cp = v.reduce_sum(self.z_b[:, nt % 2, 0:320].unsqueeze(2),
                                          self.z_acc.rearrange("p h c -> p c h"),
                                          axis=AX.X)
                    self.inc("vsm", cp)

        @blk.sync
        def _(sy):
            for nb in range(NB):
                for j in range(2):
                    nt = nb * 2 + j
                    self.w(sy, "vsm", vs0 + nt + 1)
                    if nt >= 2:
                        self.w(sy, "tgp", tg0 + 2 * nt - 2)
                    for cch in range(3):
                        self.inc(f"zt{nt % 2}", sy.dma_start(
                            out=self.zt[:, nt % 2, cch * 128:(cch + 1) * 128],
                            in_=self.z_b[:, nt % 2, cch * 128:(cch + 1) * 128],
                            transpose=True), 16)
                for j in range(2):
                    nt = nb * 2 + j
                    self.w(sy, "vyh", vy0 + 2 * nt + 2)
                    self.inc(f"so{nt % 2}", sy.dma_start(
                             out=self.yout[s, nt * 128:(nt + 1) * 128, :],
                             in_=self.out_stg[:, nt % 2, :]), 16)


def build_nc(dbg=(), stop_after=None, nseq=BPC):
    nc = bass.Bass("TRN2", target_bir_lowering=False)
    b = Builder(nc, dbg)
    b.stop_after = stop_after
    b.nseq = nseq
    b.build()
    return nc, b.dbg_t


_NC_CACHE = {}


def kernel(**inputs):
    key = "main"
    if key not in _NC_CACHE:
        _NC_CACHE[key] = build_nc()
    nc, _ = _NC_CACHE[key]
    x = np.ascontiguousarray(inputs["L0_attn_hidden_states"])
    wkeys = ["L0_attn_Wq", "L0_attn_bq", "L0_attn_Wk", "L0_attn_bk", "L0_attn_Wv",
             "L0_attn_bv", "L0_ffn_Wq", "L0_ffn_Wkv", "L0_ffn_Ws", "L0_ffn_Wd",
             "L0_ffn_Wo"]
    in_maps = []
    for c in range(NCORE):
        m = {k: np.ascontiguousarray(inputs[k]) for k in wkeys}
        m["xc"] = x[c * BPC:(c + 1) * BPC]
        in_maps.append(m)
    res = run_bass_kernel_spmd(nc, in_maps, core_ids=list(range(NCORE)))
    return np.concatenate([r["yout"] for r in res.results], axis=0)
